# version 05
# baseline (speedup 1.0000x reference)
import numpy as np
import ml_dtypes
import concourse.bass as bass
import concourse.mybir as mybir
from concourse.bass_utils import run_bass_kernel_spmd
from contextlib import ExitStack

F32 = mybir.dt.float32
BF16 = mybir.dt.bfloat16
I8 = mybir.dt.int8
AF = mybir.ActivationFunctionType
ALU = mybir.AluOpType
AX = mybir.AxisListType

T = 2048
D = 2048
NH = 16
NKV = 4
HID = 5632
NDC = 16
NHC = 44
SCALE = 1.0 / float(np.float32(np.sqrt(np.float32(128.0))))
EPS = 1e-5


class Sem:
    def __init__(self, h):
        self.h = h
        self.n = 0

    def inc(self, k):
        self.n += k
        return self.n


class EP:

    def __init__(self):
        self.ops = []

    def wait(self, sem, val):
        if val > 0:
            self.ops.append(("wait", sem.h, val))

    def _p(self, *a, sem=None, inc=0):
        if sem is not None:
            sem.inc(inc)
            self.ops.append(a + (sem.h, inc))
        else:
            self.ops.append(a + (None, 0))

    def dma(self, out, in_, sem):
        self._p("dma", out, in_, sem=sem, inc=16)

    def dmat(self, out, in_, sem):
        self._p("dmat", out, in_, sem=sem, inc=16)

    def mm(self, out, lhsT, rhs, start, stop, sem=None):
        self._p("mm", out, lhsT, rhs, start, stop, sem=sem, inc=1 if sem else 0)

    def tr(self, out, in_, ident, sem=None):
        self._p("tr", out, in_, ident, sem=sem, inc=1 if sem else 0)

    def act(self, out, in_, func, bias=0.0, scale=1.0, accum=None, sem=None):
        self._p("act", out, in_, func, bias, scale, accum, sem=sem,
                inc=1 if sem else 0)

    def scopy(self, out, in_, sem=None):
        self._p("scopy", out, in_, sem=sem, inc=1 if sem else 0)

    def tt(self, out, a, b, op, sem=None):
        self._p("tt", out, a, b, op, sem=sem, inc=1 if sem else 0)

    def ts(self, out, in0, s1, s2, op0, op1=None, sem=None):
        self._p("ts", out, in0, s1, s2, op0, op1, sem=sem, inc=1 if sem else 0)

    def tcopy(self, out, in_, sem=None):
        self._p("tcopy", out, in_, sem=sem, inc=1 if sem else 0)

    def recip(self, out, in_, sem=None):
        self._p("recip", out, in_, sem=sem, inc=1 if sem else 0)

    def rsum(self, out, in_, sem=None):
        self._p("rsum", out, in_, sem=sem, inc=1 if sem else 0)

    def rmax(self, out, in_, absval=False, sem=None):
        self._p("rmax", out, in_, absval, sem=sem, inc=1 if sem else 0)

    def memset(self, ap, v, sem=None):
        self._p("memset", ap, v, sem=sem, inc=1 if sem else 0)

    def drain(self):
        self.ops.append(("drain", None, 0))


def _replay(eng, ops):
    for op in ops:
        k = op[0]
        if k == "wait":
            eng.wait_ge(op[1], op[2])
            continue
        if k == "drain":
            eng.drain()
            continue
        sem, inc = op[-2], op[-1]
        if k == "dma":
            ins = eng.dma_start(out=op[1], in_=op[2])
        elif k == "dmat":
            ins = eng.dma_start_transpose(op[1], op[2])
        elif k == "mm":
            ins = eng.matmul(op[1], op[2], op[3], start=op[4], stop=op[5])
        elif k == "tr":
            ins = eng.transpose(op[1], op[2], op[3])
        elif k == "act":
            ins = eng.activation(op[1], op[2], op[3], bias=op[4], scale=op[5],
                                 accum_out=op[6])
        elif k == "scopy":
            ins = eng.copy(op[1], op[2])
        elif k == "tt":
            ins = eng.tensor_tensor(op[1], op[2], op[3], op[4])
        elif k == "ts":
            if op[6] is None:
                ins = eng.tensor_scalar(op[1], op[2], op[3], None, op[5])
            else:
                ins = eng.tensor_scalar(op[1], op[2], op[3], op[4], op[5], op[6])
        elif k == "tcopy":
            ins = eng.tensor_copy(op[1], op[2])
        elif k == "recip":
            ins = eng.reciprocal(op[1], op[2])
        elif k == "rsum":
            ins = eng.reduce_sum(op[1], op[2], axis=AX.X)
        elif k == "rmax":
            ins = eng.reduce_max(op[1], op[2], axis=AX.X,
                                 apply_absolute_value=op[3])
        elif k == "memset":
            ins = eng.memset(op[1], op[2])
        else:
            raise ValueError(k)
        if sem is not None:
            ins.then_inc(sem, inc)


def run_block(nc, plans):
    with nc.Block() as blk:
        for name, ep in plans.items():
            if not ep.ops:
                continue
            getattr(blk, name)(lambda eng, _ops=tuple(ep.ops): _replay(eng, _ops))


def build(debug_taps=()):
    nc = bass.Bass("TRN2", target_bir_lowering=False)
    dt = nc.dram_tensor
    x = dt("x", [T, D], F32, kind="ExternalInput").ap()
    wqk = dt("wqk", [20, 128, NDC, 128], F32, kind="ExternalInput").ap()
    wv = dt("wv", [128, NDC, 512], F32, kind="ExternalInput").ap()
    wo = dt("wo", [NDC, 128, NDC, 128], F32, kind="ExternalInput").ap()
    w13 = dt("w13", [NHC, 128, 2, NDC, 128], F32, kind="ExternalInput").ap()
    w2 = dt("w2", [NDC, 128, NHC, 128], F32, kind="ExternalInput").ap()
    nwt = dt("nwt", [128, NHC], F32, kind="ExternalInput").ap()
    cosf = dt("cosf", [128, T], F32, kind="ExternalInput").ap()
    sinf = dt("sinf", [128, T], F32, kind="ExternalInput").ap()
    id32 = dt("id32", [128, 128], F32, kind="ExternalInput").ap()
    ones128 = dt("ones128", [128, 128], BF16, kind="ExternalInput").ap()
    ones1f = dt("ones1f", [1, 128], F32, kind="ExternalInput").ap()
    onescb = dt("onescb", [128, 1], BF16, kind="ExternalInput").ap()
    tri = dt("tri", [128, 4, 512], BF16, kind="ExternalInput").ap()
    out = dt("out", [T, D], F32, kind="ExternalOutput").ap()

    xbf = dt("xbf", [T, D], BF16, kind="Internal").ap()
    x1sp = dt("x1sp", [T, D], F32, kind="Internal").ap()
    xqsp = dt("xqsp", [T, D], BF16, kind="Internal").ap()

    taps = {}
    for tap in debug_taps:
        shp = {"qy": [128, NH * T], "k4": [128, NKV * T], "vtm": [128, 16 * 512],
               "y": [128, NH * T], "x1": [T, D], "xq": [T, D],
               "gn": [128, NHC * 512], "afm0": [128, NDC * 512],
               "z0": [128, 4 * D]}[tap]
        dtp = {"x1": F32, "afm0": F32, "z0": F32}.get(tap, BF16)
        taps[tap] = dt("tap_" + tap, shp, dtp, kind="ExternalOutput").ap()

    with ExitStack() as st:
        ent = st.enter_context
        c_id32 = ent(nc.sbuf_tensor("c_id32", [128, 128], F32))
        c_o128 = ent(nc.sbuf_tensor("c_o128", [128, 128], BF16))
        c_o1f = ent(nc.sbuf_tensor("c_o1f", [1, 128], F32))
        c_ocb = ent(nc.sbuf_tensor("c_ocb", [128, 1], BF16))
        c_tri = ent(nc.sbuf_tensor("c_tri", [128, 4, 512], BF16))
        c_nw = ent(nc.sbuf_tensor("c_nw", [128, NHC], F32))
        LS = Sem(ent(nc.semaphore("LS")))
        LQ = Sem(ent(nc.semaphore("LQ")))
        TS_ = Sem(ent(nc.semaphore("TSm")))
        VS = Sem(ent(nc.semaphore("VSm")))
        SS = Sem(ent(nc.semaphore("SSm")))

        stQY = ExitStack()
        QY = stQY.enter_context(nc.sbuf_tensor("QY", [128, NH * T], BF16))

        with ExitStack() as stKV:
            eKV = stKV.enter_context
            K4 = eKV(nc.sbuf_tensor("K4", [128, NKV * T], BF16))
            Vtm = eKV(nc.sbuf_tensor("Vtm", [128, 16 * 512], BF16))

            sy, gp, te, ve, sc = EP(), EP(), EP(), EP(), EP()
            base = LQ.n
            cast_done = [base + 16 * (s + 1) for s in range(16)]
            wv_done = base + 16 * 17
            wqk_done = [0] * 20
            wqk_done[0] = wv_done + 16
            wqk_done[1] = wv_done + 32
            run = wv_done + 32
            sh_done = [0] * 20
            for pi in range(20):
                if pi + 2 <= 19:
                    run += 16
                    wqk_done[pi + 2] = run
                run += 32
                sh_done[pi] = run
            order = list(range(16, 20)) + list(range(16))

            with ExitStack() as stXT:
                XT = stXT.enter_context(nc.sbuf_tensor("XT", [128, NDC * T], BF16))
                with ExitStack() as s4:
                    e4 = s4.enter_context
                    xsl = [e4(nc.sbuf_tensor(f"xsl{i}", [128, D], F32))
                           for i in range(2)]
                    for dst, src in [(c_id32[:], id32), (c_o128[:], ones128),
                                     (c_o1f[:], ones1f), (c_ocb[:], onescb),
                                     (c_tri[:], tri), (c_nw[:], nwt)]:
                        sy.dma(dst, src, LS)
                    ld_done = []
                    for s in range(16):
                        if s >= 2:
                            sy.wait(LQ, cast_done[s - 2])
                        sy.dma(xsl[s % 2][:], x[s * 128:(s + 1) * 128, :], LS)
                        ld_done.append(LS.n)
                    sy.wait(LQ, cast_done[15])
                    for c in range(NDC):
                        sy.dmat(XT[:, c * T:(c + 1) * T],
                                xbf[:, c * 128:(c + 1) * 128], LS)
                    xt_done = LS.n
                wqkb = [stXT.enter_context(
                    nc.sbuf_tensor(f"wqkb{i}", [128, NDC, 128], BF16))
                    for i in range(2)]
                pqk = [stXT.enter_context(nc.psum_tensor(f"pqk{i}", [128, 512], F32))
                       for i in range(2)]
                pv = [stXT.enter_context(nc.psum_tensor(f"pv{i}", [128, 512], F32))
                      for i in range(2)]
                ev_done, mm_done, rope_done = {}, {}, {}
                with ExitStack() as sV:
                    wvsb = sV.enter_context(
                        nc.sbuf_tensor("wvsb", [128, NDC, 512], BF16))
                    te.wait(LQ, wv_done)
                    te.wait(LS, xt_done)
                    for stg in range(16):
                        bank = pv[stg % 2]
                        if stg >= 2:
                            te.wait(VS, ev_done[("v", stg - 2)])
                        for d in range(NDC):
                            col = d * T + (stg // 4) * 512 + (stg % 4) * 128
                            te.mm(bank[:], XT[:, col:col + 128], wvsb[:, d, :],
                                  d == 0, d == NDC - 1,
                                  sem=TS_ if d == NDC - 1 else None)
                        mm_done[("v", stg)] = TS_.n
                        ve.wait(TS_, mm_done[("v", stg)])
                        ve.tcopy(Vtm[:, stg * 512:(stg + 1) * 512], bank[:], sem=VS)
                        ev_done[("v", stg)] = VS.n
                scos = stXT.enter_context(nc.sbuf_tensor("scos", [128, T], F32))
                ssin = stXT.enter_context(nc.sbuf_tensor("ssin", [128, T], F32))
                shb = stXT.enter_context(nc.sbuf_tensor("shb", [128, T], BF16))
                t1b = stXT.enter_context(nc.sbuf_tensor("t1b", [128, 1024], F32))
                t2b = stXT.enter_context(nc.sbuf_tensor("t2b", [128, 1024], F32))
                sy.wait(TS_, mm_done[("v", 15)])
                sy.dma(scos[:], cosf, LS)
                sy.dma(ssin[:], sinf, LS)
                tbl_done = LS.n
                for pi, m in enumerate(order):
                    te.wait(LQ, wqk_done[pi])
                    dst = QY if m < 16 else K4
                    dcol = (m if m < 16 else m - 16) * T
                    for tb in range(4):
                        bank = pqk[(pi * 4 + tb) % 2]
                        prev = (pi * 4 + tb) - 2
                        if prev >= 0:
                            te.wait(VS, ev_done[("qk", prev)])
                        for d in range(NDC):
                            te.mm(bank[:], wqkb[pi % 2][:, d, :],
                                  XT[:, d * T + tb * 512: d * T + tb * 512 + 512],
                                  d == 0, d == NDC - 1,
                                  sem=TS_ if d == NDC - 1 else None)
                        mm_done[("qk", pi, tb)] = TS_.n
                        ve.wait(TS_, mm_done[("qk", pi, tb)])
                        if pi == 0 and tb == 0:
                            ve.wait(LS, tbl_done)
                        ve.tcopy(dst[:, dcol + tb * 512: dcol + tb * 512 + 512],
                                 bank[:], sem=VS)
                        ev_done[("qk", pi * 4 + tb)] = VS.n
                    ve.wait(LQ, sh_done[pi])
                    for hh in range(2):
                        lo, hi = hh * 1024, (hh + 1) * 1024
                        ve.tt(t1b[:], dst[:, dcol + lo: dcol + hi],
                              scos[:, lo:hi], ALU.mult)
                        ve.tt(t2b[:], shb[:, lo:hi], ssin[:, lo:hi], ALU.mult)
                        ve.tt(dst[:, dcol + lo: dcol + hi], t1b[:], t2b[:],
                              ALU.add, sem=VS if hh == 1 else None)
                    rope_done[m] = VS.n
                for s in range(16):
                    gp.wait(LS, ld_done[s])
                    gp.dma(xbf[s * 128:(s + 1) * 128, :], xsl[s % 2][:], LQ)
                    assert LQ.n == cast_done[s]
                gp.dma(wvsb[:], wv, LQ)
                assert LQ.n == wv_done
                for pi0 in range(2):
                    gp.dma(wqkb[pi0 % 2][:], wqk[order[pi0]], LQ)
                    assert LQ.n == wqk_done[pi0]
                gp.wait(LS, xt_done)
                for pi, m in enumerate(order):
                    if pi + 2 <= 19:
                        gp.wait(TS_, mm_done[("qk", pi, 3)])
                        gp.dma(wqkb[(pi + 2) % 2][:], wqk[order[pi + 2]], LQ)
                        assert LQ.n == wqk_done[pi + 2]
                    dst = QY if m < 16 else K4
                    dcol = (m if m < 16 else m - 16) * T
                    gp.wait(VS, ev_done[("qk", pi * 4 + 3)])
                    if pi >= 1:
                        gp.wait(VS, rope_done[order[pi - 1]])
                    gp.dma(shb[0:64, :], dst[64:128, dcol:dcol + T], LQ)
                    gp.dma(shb[64:128, :], dst[0:64, dcol:dcol + T], LQ)
                    assert LQ.n == sh_done[pi]
                qk_ev_last = ev_done[("qk", 79)]

            with ExitStack() as s4:
                e4 = s4.enter_context
                PT = e4(nc.sbuf_tensor("PT", [128, 16 * 512], BF16))
                invl = e4(nc.sbuf_tensor("invl", [128, 512], F32))
                lntmp = e4(nc.sbuf_tensor("lntmp", [128, 512], F32))
                ps = [e4(nc.psum_tensor(f"ps{i}", [128, 512], F32))
                      for i in range(4)]
                pl = e4(nc.psum_tensor("pl", [128, 512], F32))
                po = e4(nc.psum_tensor("po", [128, 512], F32))
                bank_free = {0: 0, 1: 0, 2: 0, 3: 0}
                oevac = 0
                first_st = [True]
                for h in range(NH):
                    g = h // 4
                    for jb in range(4):
                        nsc = 4 * (jb + 1)
                        q0 = h * T + jb * 512
                        exp_done, mask_done = [], {}

                        def emit_lav(c):
                            te.wait(SS, exp_done[c])
                            if c in mask_done:
                                te.wait(VS, mask_done[c])
                            if c == 0:
                                te.wait(VS, oevac)
                            te.mm(pl[:], c_o128[:], PT[:, c * 512:(c + 1) * 512],
                                  c == 0, c == nsc - 1,
                                  sem=TS_ if c == nsc - 1 else None)
                            te.mm(po[:],
                                  Vtm[:, c * 512 + g * 128: c * 512 + g * 128 + 128],
                                  PT[:, c * 512:(c + 1) * 512],
                                  c == 0, c == nsc - 1,
                                  sem=TS_ if c == nsc - 1 else None)

                        for c in range(nsc):
                            bk = c % 4
                            te.wait(SS, bank_free[bk])
                            if first_st[0]:
                                te.wait(VS, qk_ev_last)
                                first_st[0] = False
                            if c == 0:
                                te.wait(VS, rope_done[h])
                                te.wait(VS, rope_done[16 + g])
                            te.mm(ps[bk][:],
                                  K4[:, g * T + c * 128: g * T + c * 128 + 128],
                                  QY[:, q0:q0 + 512], True, True, sem=TS_)
                            sc.wait(TS_, TS_.n)
                            sc.act(PT[:, c * 512:(c + 1) * 512], ps[bk][:],
                                   AF.Exp, bias=0.0, scale=SCALE, sem=SS)
                            exp_done.append(SS.n)
                            bank_free[bk] = SS.n
                            if c >= 4 * jb:
                                ve.wait(SS, exp_done[c])
                                ve.tt(PT[:, c * 512:(c + 1) * 512],
                                      PT[:, c * 512:(c + 1) * 512],
                                      c_tri[:, c - 4 * jb, :], ALU.mult, sem=VS)
                                mask_done[c] = VS.n
                            if c >= 4:
                                emit_lav(c - 4)
                        for c in range(max(0, nsc - 4), nsc):
                            emit_lav(c)
                        to = TS_.n
                        tl = TS_.n - 1
                        sc.wait(TS_, tl)
                        sc.act(lntmp[:], pl[:], AF.Ln)
                        sc.act(invl[:], lntmp[:], AF.Exp, bias=0.0, scale=-1.0,
                               sem=SS)
                        inv_d = SS.n
                        ve.wait(SS, inv_d)
                        ve.wait(TS_, to)
                        ve.tt(QY[:, q0:q0 + 512], po[:], invl[:], ALU.mult, sem=VS)
                        oevac = VS.n
                for nm, tsr in [("y", QY), ("qy", QY), ("k4", K4), ("vtm", Vtm)]:
                    if nm in taps:
                        sy.wait(VS, VS.n)
                        sy.dma(taps[nm], tsr[:], LS)
                if taps:
                    sy.wait(LS, LS.n)
                run_block(nc, {"sync": sy, "gpsimd": gp, "tensor": te,
                               "vector": ve, "scalar": sc})

        x1st_done = [0] * 16
        xqst_done = [0] * 16
        zadd_v = [0] * 16
        with ExitStack() as s4:
            e4 = s4.enter_context
            afm = e4(nc.sbuf_tensor("afm", [128, NDC * 512], F32))
            Ztm = e4(nc.sbuf_tensor("Ztm", [128, 4 * D], F32))
            wob = [e4(nc.sbuf_tensor(f"wob{i}", [128, NDC, 128], BF16))
                   for i in range(2)]
            xslP = [e4(nc.sbuf_tensor(f"xslP{i}", [128, D], F32)) for i in range(2)]
            x1sl = [e4(nc.sbuf_tensor(f"x1sl{i}", [128, D], F32)) for i in range(2)]
            k8sl = [e4(nc.sbuf_tensor(f"k8sl{i}", [128, D], I8)) for i in range(2)]
            xqsl = [e4(nc.sbuf_tensor(f"xqsl{i}", [128, D], BF16)) for i in range(2)]
            sqscr = e4(nc.sbuf_tensor("sqscr", [128, D], F32))
            stt = e4(nc.sbuf_tensor("stt", [128, 16], F32))
            pw = [e4(nc.psum_tensor(f"pw{i}", [128, 512], F32)) for i in range(2)]
            ptr = [e4(nc.psum_tensor(f"ptr{i}", [128, 128], F32)) for i in range(2)]
            sy, gp, te, ve, sc = EP(), EP(), EP(), EP(), EP()
            lq0 = LQ.n
            wo_done = [lq0 + 16 * (g + 1) for g in range(4 * NDC)]
            wo_mm, afm_ev, tr_done, ztr_ev = [], [], {}, {}
            sq_ss = [0] * 16
            for tblk in range(4):
                for mc in range(NDC):
                    mcg = tblk * NDC + mc
                    te.wait(LQ, wo_done[mcg])
                    if mcg >= 2:
                        te.wait(VS, afm_ev[mcg - 2])
                    for d in range(NDC):
                        te.mm(pw[mcg % 2][:], wob[mcg % 2][:, d, :],
                              QY[:, d * T + tblk * 512: d * T + tblk * 512 + 512],
                              d == 0, d == NDC - 1,
                              sem=TS_ if d == NDC - 1 else None)
                    wo_mm.append(TS_.n)
                    ve.wait(TS_, wo_mm[mcg])
                    if tblk > 0 and mc == 0:
                        ve.wait(TS_, tr_done[(tblk - 1) * 64 + 63])
                        ve.wait(SS, sq_ss[tblk * 4 - 1])
                    ve.tcopy(afm[:, mc * 512:(mc + 1) * 512], pw[mcg % 2][:], sem=VS)
                    afm_ev.append(VS.n)
                for mc in range(NDC):
                    mcg = tblk * NDC + mc
                    for ts_i in range(4):
                        idx = tblk * 64 + mc * 4 + ts_i
                        te.wait(VS, afm_ev[mcg])
                        if idx >= 2:
                            te.wait(VS, ztr_ev[idx - 2])
                        te.tr(ptr[idx % 2][:],
                              afm[:, mc * 512 + ts_i * 128: mc * 512 + ts_i * 128 + 128],
                              c_id32[:], sem=TS_)
                        tr_done[idx] = TS_.n
                        ve.wait(TS_, tr_done[idx])
                        ve.tcopy(Ztm[:, ts_i * D + mc * 128: ts_i * D + mc * 128 + 128],
                                 ptr[idx % 2][:], sem=VS)
                        ztr_ev[idx] = VS.n
                for ts_i in range(4):
                    gidx = tblk * 4 + ts_i
                    row0 = gidx * 128
                    Z = Ztm[:, ts_i * D:(ts_i + 1) * D]
                    if gidx >= 2:
                        sy.wait(VS, zadd_v[gidx - 2])
                    sy.dma(xslP[ts_i % 2][:], x[row0:row0 + 128, :], LS)
                    xld = LS.n
                    ve.wait(LS, xld)
                    ve.tt(Z, Z, xslP[ts_i % 2][:], ALU.add, sem=VS)
                    zadd_v[gidx] = VS.n
                    ve.rsum(stt[:, 0:1], Z)
                    ve.drain()
                    ve.memset(stt[:, 1:2], 0.0, sem=VS)
                    mset = VS.n
                    sc.wait(VS, mset)
                    sc.act(sqscr[:], Z, AF.Square, accum=stt[:, 1:2], sem=SS)
                    sqd = SS.n
                    sq_ss[gidx] = sqd
                    ve.ts(stt[:, 2:3], stt[:, 0:1], -1.0 / D, None, ALU.mult)
                    ve.drain()
                    ve.wait(SS, sqd)
                    ve.ts(stt[:, 3:4], stt[:, 1:2], 1.0 / D, EPS, ALU.mult, ALU.add)
                    ve.tt(stt[:, 4:5], stt[:, 2:3], stt[:, 2:3], ALU.mult)
                    ve.drain()
                    ve.tt(stt[:, 5:6], stt[:, 3:4], stt[:, 4:5], ALU.subtract)
                    ve.drain()
                    ve.recip(stt[:, 6:7], stt[:, 5:6], sem=VS)
                    rvd = VS.n
                    sc.wait(VS, rvd)
                    sc.act(stt[:, 7:8], stt[:, 6:7], AF.Sqrt, sem=SS)
                    rstd_d = SS.n
                    ve.wait(SS, rstd_d)
                    ve.tt(stt[:, 8:9], stt[:, 2:3], stt[:, 7:8], ALU.mult)
                    ve.drain()
                    if gidx >= 2:
                        ve.wait(LS, x1st_done[gidx - 2])
                        ve.wait(LS, xqst_done[gidx - 2])
                    ve.ts(x1sl[ts_i % 2][:], Z, stt[:, 7:8], stt[:, 8:9],
                          ALU.mult, ALU.add, sem=VS)
                    x1d = VS.n
                    sy.wait(VS, x1d)
                    sy.dma(x1sp[row0:row0 + 128, :], x1sl[ts_i % 2][:], LS)
                    x1st_done[gidx] = LS.n
                    ve.rmax(stt[:, 9:10], x1sl[ts_i % 2][:], absval=True)
                    ve.drain()
                    ve.ts(stt[:, 10:11], stt[:, 9:10], EPS, None, ALU.max)
                    ve.drain()
                    ve.recip(stt[:, 11:12], stt[:, 10:11])
                    ve.drain()
                    ve.ts(stt[:, 12:13], stt[:, 11:12], 127.0, None, ALU.mult)
                    ve.ts(stt[:, 13:14], stt[:, 10:11], 1.0 / 127.0, None, ALU.mult)
                    ve.drain()
                    ve.ts(k8sl[ts_i % 2][:], x1sl[ts_i % 2][:], stt[:, 12:13],
                          None, ALU.mult)
                    ve.ts(xqsl[ts_i % 2][:], k8sl[ts_i % 2][:], stt[:, 13:14],
                          None, ALU.mult, sem=VS)
                    xqd = VS.n
                    sy.wait(VS, xqd)
                    sy.dma(xqsp[row0:row0 + 128, :], xqsl[ts_i % 2][:], LS)
                    xqst_done[gidx] = LS.n
            gp2 = EP()
            for tblk in range(4):
                for mc in range(NDC):
                    mcg = tblk * NDC + mc
                    if mcg >= 2:
                        gp2.wait(TS_, wo_mm[mcg - 2])
                    gp2.dma(wob[mcg % 2][:], wo[mc], LQ)
                    assert LQ.n == wo_done[mcg]
            gp = gp2
            if "x1" in taps:
                sy.wait(LS, LS.n)
                sy.dma(taps["x1"], x1sp, LS)
            if "xq" in taps:
                sy.wait(LS, LS.n)
                sy.dma(taps["xq"], xqsp, LS)
                sy.wait(LS, LS.n)
            run_block(nc, {"sync": sy, "gpsimd": gp, "tensor": te,
                           "vector": ve, "scalar": sc})
        stQY.close()

        with ExitStack() as s4:
            e4 = s4.enter_context
            XqTb = e4(nc.sbuf_tensor("XqTb", [128, NDC * 512], BF16))
            GN = e4(nc.sbuf_tensor("GN", [128, NHC * 512 + D], BF16))
            w13b = [e4(nc.sbuf_tensor(f"w13b{i}", [128, 2, NDC, 128], BF16))
                    for i in range(2)]
            w2b = [e4(nc.sbuf_tensor(f"w2b{i}", [128, NHC, 128], BF16))
                   for i in range(2)]
            rbuf = e4(nc.sbuf_tensor("rb4", [128, 512], F32))
            r2buf = e4(nc.sbuf_tensor("r2b4", [128, 512], F32))
            gbuf = e4(nc.sbuf_tensor("gb4", [128, 512], F32))
            gsqb = [e4(nc.sbuf_tensor(f"gsq4_{i}", [128, 512], BF16))
                    for i in range(2)]
            gnsqb = e4(nc.sbuf_tensor("gnsq4", [128, 512], F32))
            runmax = e4(nc.sbuf_tensor("rmx4", [128, 512], F32))
            m2col = e4(nc.sbuf_tensor("m2c4", [128, 4], F32))
            rows4 = e4(nc.sbuf_tensor("r4_4", [4, 128], F32))
            rowscr = e4(nc.sbuf_tensor("rsc4", [1, 2048], F32))
            RSb = e4(nc.sbuf_tensor("RSb4", [128, 1024], BF16))
            iS2b = e4(nc.sbuf_tensor("iS2b4", [128, 512], F32))
            k8b = e4(nc.sbuf_tensor("k8b4", [128, 1024], I8))
            z2fm = [e4(nc.sbuf_tensor(f"z2f4_{i}", [128, 512], F32))
                    for i in range(2)]
            Ztm2 = e4(nc.sbuf_tensor("Zt2_4", [128, 4 * D], F32))
            x1r = [e4(nc.sbuf_tensor(f"x1r4_{i}", [128, D], F32)) for i in range(2)]
            outsl = [e4(nc.sbuf_tensor(f"osl4_{i}", [128, D], F32))
                     for i in range(2)]
            st2t = e4(nc.sbuf_tensor("st2_4", [128, 16], F32))
            pg = [e4(nc.psum_tensor(f"pg4_{i}", [128, 512], F32)) for i in range(4)]
            pstat = e4(nc.psum_tensor("pst4", [1, 512], F32))
            ptrx = [e4(nc.psum_tensor(f"ptx4_{i}", [128, 128], F32))
                    for i in range(2)]
            pz = e4(nc.psum_tensor("pz4", [128, 512], F32))
            sy, gp, te, ve, sc = EP(), EP(), EP(), EP(), EP()

            lq0 = LQ.n
            w13_done = [[lq0 + 16 * (tb * (NHC + NDC) + j + 1) for j in range(NHC)]
                        for tb in range(4)]
            w2_done = [[lq0 + 16 * (tb * (NHC + NDC) + NHC + mc + 1)
                        for mc in range(NDC)] for tb in range(4)]
            mm1_done, gate_v, gate_s, gsq_ready = [], [], [], []
            mm2_done, z2fm_ev, trz_ev = [], [], {}
            outst = [0] * 16
            prev = {"bc_d": 0, "ssr_d": 0, "trm": 0, "mm2_last": 0, "trz_last": 0}
            xqt_done_arr = [0] * 4
            for tblk in range(4):
                if tblk == 0:
                    sy.wait(LS, xqst_done[3])
                    for c in range(NDC):
                        sy.dmat(XqTb[:, c * 512:(c + 1) * 512],
                                xqsp[0:512, c * 128:(c + 1) * 128], LS)
                    xqt_done_arr[0] = LS.n
                xqt_done = xqt_done_arr[tblk]

                te.wait(VS, prev["bc_d"])
                te.wait(SS, prev["ssr_d"])
                for j in range(NHC):
                    jg = tblk * NHC + j
                    te.wait(LQ, w13_done[tblk][j])
                    if jg >= 2:
                        te.wait(VS, gate_v[jg - 2])
                        te.wait(SS, gate_s[jg - 2])
                    if j == 0:
                        te.wait(LS, xqt_done)
                    for half in range(2):
                        bank = pg[(jg % 2) * 2 + half]
                        for d in range(NDC):
                            te.mm(bank[:], w13b[jg % 2][:, half, d, :],
                                  XqTb[:, d * 512:(d + 1) * 512],
                                  d == 0, d == NDC - 1,
                                  sem=TS_ if d == NDC - 1 else None)
                        mm1_done.append(TS_.n)
                    if j >= 1:
                        te.wait(SS, gsq_ready[jg - 1])
                        te.mm(pstat[:], c_ocb[:], gsqb[(jg - 1) % 2][:],
                              j - 1 == 0, False, sem=None)
                    tx1, tx3 = mm1_done[2 * jg], mm1_done[2 * jg + 1]
                    sc.wait(TS_, tx1)
                    sc.act(rbuf[:], pg[(jg % 2) * 2][:], AF.Relu)
                    sc.act(r2buf[:], rbuf[:], AF.Square, sem=SS)
                    r2d = SS.n
                    ve.wait(SS, r2d)
                    ve.wait(TS_, tx3)
                    if j == 0 and tblk > 0:
                        ve.wait(TS_, prev["mm2_last"])
                        ve.wait(TS_, prev["trm"])
                    ve.tt(gbuf[:], r2buf[:], pg[(jg % 2) * 2 + 1][:], ALU.mult,
                          sem=VS)
                    gbd = VS.n
                    sc.wait(VS, gbd)
                    sc.act(gsqb[jg % 2][:], gbuf[:], AF.Square, sem=SS)
                    gsq_ready.append(SS.n)
                    ve.ts(GN[:, j * 512:(j + 1) * 512], gbuf[:],
                          c_nw[:, j:j + 1], None, ALU.mult, sem=VS)
                    sc.wait(VS, VS.n)
                    sc.act(gnsqb[:], GN[:, j * 512:(j + 1) * 512], AF.Square,
                           sem=SS)
                    ve.wait(SS, SS.n)
                    if j == 0:
                        ve.tcopy(runmax[:], gnsqb[:], sem=VS)
                    else:
                        ve.tt(runmax[:], runmax[:], gnsqb[:], ALU.max, sem=VS)
                    gate_v.append(VS.n)
                    gate_s.append(SS.n)
                te.wait(SS, gsq_ready[tblk * NHC + NHC - 1])
                te.mm(pstat[:], c_ocb[:], gsqb[(tblk * NHC + NHC - 1) % 2][:],
                      False, True, sem=TS_)
                tstat = TS_.n
                sc.wait(TS_, tstat)
                sc.scopy(rowscr[:, 0:512], pstat[:], sem=SS)
                ssr_d = SS.n
                prev["ssr_d"] = ssr_d
                ve.wait(SS, ssr_d)
                ve.ts(rowscr[:, 512:1024], rowscr[:, 0:512], 1.0 / HID, EPS,
                      ALU.mult, ALU.add)
                ve.drain()
                ve.recip(rowscr[:, 1024:1536], rowscr[:, 512:1024], sem=VS)
                rv2 = VS.n
                sc.wait(VS, rv2)
                sc.act(rowscr[:, 1536:2048], rowscr[:, 1024:1536], AF.Sqrt, sem=SS)
                rstd2_d = SS.n
                trv = []
                for i in range(4):
                    if i == 0:
                        te.wait(VS, gate_v[tblk * NHC + NHC - 1])
                        if tblk > 0:
                            te.wait(VS, trz_ev[tblk * 64 - 1])
                    if i >= 2:
                        te.wait(VS, trv[i - 2])
                    te.tr(ptrx[i % 2][:], runmax[:, i * 128:(i + 1) * 128],
                          c_id32[:], sem=TS_)
                    ti = TS_.n
                    ve.wait(TS_, ti)
                    ve.rmax(m2col[:, i:i + 1], ptrx[i % 2][:], absval=False,
                            sem=VS)
                    trv.append(VS.n)
                te.wait(VS, trv[3])
                te.tr(ptrx[0][0:4, :], m2col[:], c_id32[:], sem=TS_)
                trm = TS_.n
                prev["trm"] = trm
                ve.wait(TS_, trm)
                ve.tcopy(rows4[:], ptrx[0][0:4, 0:128], sem=VS)
                r4d = VS.n
                sy.wait(VS, r4d)
                sy.dma(rowscr[:, 0:512], rows4[:], LS)
                m2r_d = LS.n
                if tblk < 3:
                    sy.wait(LS, xqst_done[(tblk + 1) * 4 + 3])
                    sy.wait(TS_, mm1_done[2 * (tblk + 1) * NHC - 1])
                    for c in range(NDC):
                        sy.dmat(XqTb[:, c * 512:(c + 1) * 512],
                                xqsp[(tblk + 1) * 512:(tblk + 2) * 512,
                                     c * 128:(c + 1) * 128], LS)
                    xqt_done_arr[tblk + 1] = LS.n
                sc.wait(LS, m2r_d)
                sc.act(rowscr[:, 512:1024], rowscr[:, 0:512], AF.Sqrt, sem=SS)
                sm2_d = SS.n
                ve.wait(SS, sm2_d)
                ve.wait(SS, rstd2_d)
                ve.tt(rowscr[:, 1024:1536], rowscr[:, 512:1024],
                      rowscr[:, 1536:2048], ALU.mult)
                ve.drain()
                ve.ts(rowscr[:, 1024:1536], rowscr[:, 1024:1536], EPS, None,
                      ALU.max)
                ve.drain()
                ve.recip(rowscr[:, 0:512], rowscr[:, 1024:1536])
                ve.drain()
                ve.ts(rowscr[:, 0:512], rowscr[:, 0:512], 127.0, None,
                      ALU.mult)
                ve.drain()
                ve.ts(rowscr[:, 512:1024], rowscr[:, 1024:1536], 1.0 / 127.0,
                      None, ALU.mult)
                ve.drain()
                ve.tt(rowscr[:, 1024:1536], rowscr[:, 1536:2048],
                      rowscr[:, 0:512], ALU.mult, sem=VS)
                rows_d = VS.n
                te.wait(VS, rows_d)
                te.mm(pg[0][:], c_o1f[:], rowscr[:, 1024:1536], True, True,
                      sem=TS_)
                tb1 = TS_.n
                te.mm(pg[1][:], c_o1f[:], rowscr[:, 512:1024], True, True,
                      sem=TS_)
                tb2 = TS_.n
                ve.wait(TS_, tb1)
                ve.tcopy(RSb[:, 0:512], pg[0][:])
                ve.tcopy(RSb[:, 512:1024], pg[0][:])
                ve.wait(TS_, tb2)
                ve.tcopy(iS2b[:], pg[1][:], sem=VS)
                prev["bc_d"] = VS.n
                for jj in range(NHC // 2):
                    sl = slice(jj * 1024, (jj + 1) * 1024)
                    ve.tt(k8b[:], GN[:, sl], RSb[:], ALU.mult)
                    ve.tcopy(GN[:, sl], k8b[:], sem=VS)
                iq_d = VS.n
                if "gn" in taps and tblk == 0:
                    sy.wait(VS, iq_d)
                    sy.dma(taps["gn"], GN[:, 0:NHC * 512], LS)
                for mc in range(NDC):
                    mcg = tblk * NDC + mc
                    te.wait(LQ, w2_done[tblk][mc])
                    if mc == 0:
                        te.wait(VS, iq_d)
                    if mcg >= 1:
                        te.wait(VS, z2fm_ev[mcg - 1])
                    for j in range(NHC):
                        te.mm(pz[:], w2b[mcg % 2][:, j, :],
                              GN[:, j * 512:(j + 1) * 512],
                              j == 0, j == NHC - 1,
                              sem=TS_ if j == NHC - 1 else None)
                    mm2_done.append(TS_.n)
                    ve.wait(TS_, mm2_done[mcg])
                    ve.tt(z2fm[mcg % 2][:], pz[:], iS2b[:], ALU.mult, sem=VS)
                    z2fm_ev.append(VS.n)
                    for ts_i in range(4):
                        idx = tblk * 64 + mc * 4 + ts_i
                        if ts_i == 0:
                            te.wait(VS, z2fm_ev[mcg])
                        if idx >= 2:
                            te.wait(VS, trz_ev[idx - 2])
                        te.tr(ptrx[idx % 2][:],
                              z2fm[mcg % 2][:, ts_i * 128:(ts_i + 1) * 128],
                              c_id32[:], sem=TS_)
                        tdd = TS_.n
                        ve.wait(TS_, tdd)
                        if idx == tblk * 64 and tblk > 0:
                            ve.wait(SS, prev["sq2_last"])
                        ve.tcopy(Ztm2[:, ts_i * D + mc * 128:
                                 ts_i * D + mc * 128 + 128],
                                 ptrx[idx % 2][:], sem=VS)
                        trz_ev[idx] = VS.n
                prev["mm2_last"] = mm2_done[tblk * NDC + NDC - 1]
                for ts_i in range(4):
                    gidx = tblk * 4 + ts_i
                    row0 = gidx * 128
                    Z = Ztm2[:, ts_i * D:(ts_i + 1) * D]
                    sy.wait(LS, x1st_done[gidx])
                    sy.dma(x1r[ts_i % 2][:], x1sp[row0:row0 + 128, :], LS)
                    x1rd = LS.n
                    ve.wait(LS, x1rd)
                    ve.wait(VS, trz_ev[tblk * 64 + 63])
                    ve.tt(Z, Z, x1r[ts_i % 2][:], ALU.add, sem=VS)
                    ve.rsum(st2t[:, 0:1], Z)
                    ve.drain()
                    ve.memset(st2t[:, 1:2], 0.0, sem=VS)
                    ms2 = VS.n
                    sc.wait(VS, ms2)
                    sc.act(GN[:, NHC * 512:NHC * 512 + D], Z, AF.Square,
                           accum=st2t[:, 1:2], sem=SS)
                    sq2d = SS.n
                    prev["sq2_last"] = sq2d
                    ve.ts(st2t[:, 2:3], st2t[:, 0:1], -1.0 / D, None, ALU.mult)
                    ve.drain()
                    ve.wait(SS, sq2d)
                    ve.ts(st2t[:, 3:4], st2t[:, 1:2], 1.0 / D, EPS, ALU.mult,
                          ALU.add)
                    ve.tt(st2t[:, 4:5], st2t[:, 2:3], st2t[:, 2:3], ALU.mult)
                    ve.drain()
                    ve.tt(st2t[:, 5:6], st2t[:, 3:4], st2t[:, 4:5], ALU.subtract)
                    ve.drain()
                    ve.recip(st2t[:, 6:7], st2t[:, 5:6], sem=VS)
                    rv3 = VS.n
                    sc.wait(VS, rv3)
                    sc.act(st2t[:, 7:8], st2t[:, 6:7], AF.Sqrt, sem=SS)
                    r2d2 = SS.n
                    ve.wait(SS, r2d2)
                    ve.tt(st2t[:, 8:9], st2t[:, 2:3], st2t[:, 7:8], ALU.mult)
                    ve.drain()
                    if gidx >= 2:
                        ve.wait(LS, outst[gidx - 2])
                    ve.ts(outsl[ts_i % 2][:], Z, st2t[:, 7:8], st2t[:, 8:9],
                          ALU.mult, ALU.add, sem=VS)
                    od = VS.n
                    sy.wait(VS, od)
                    sy.dma(out[row0:row0 + 128, :], outsl[ts_i % 2][:], LS)
                    outst[gidx] = LS.n
            sy.wait(LS, LS.n)
            gp2 = EP()
            for tblk in range(4):
                for j in range(NHC):
                    jg = tblk * NHC + j
                    if jg >= 2:
                        gp2.wait(TS_, mm1_done[2 * (jg - 2) + 1])
                    gp2.dma(w13b[jg % 2][:], w13[j], LQ)
                    assert LQ.n == w13_done[tblk][j]
                for mc in range(NDC):
                    mcg = tblk * NDC + mc
                    if mcg >= 2:
                        gp2.wait(TS_, mm2_done[mcg - 2])
                    gp2.dma(w2b[mcg % 2][:], w2[mc], LQ)
                    assert LQ.n == w2_done[tblk][mc]
            gp = gp2
            run_block(nc, {"sync": sy, "gpsimd": gp, "tensor": te,
                           "vector": ve, "scalar": sc})

    return nc


def _host_layouts(Wq, Wk, Wv, Wo, w13, w2, norm_w):
    f32 = np.float32
    wqk_src = np.concatenate([np.asarray(Wq, f32), np.asarray(Wk, f32)], axis=1)
    wqk = np.ascontiguousarray(
        wqk_src.reshape(NDC, 128, 20, 128).transpose(2, 1, 0, 3))
    wvv = np.ascontiguousarray(
        np.asarray(Wv, f32).reshape(NDC, 128, 512).transpose(1, 0, 2))
    woo = np.ascontiguousarray(
        np.asarray(Wo, f32).reshape(NDC, 128, NDC, 128).transpose(2, 1, 0, 3))
    w13a = np.asarray(w13, f32).T.reshape(NDC, 128, 88, 128).transpose(2, 1, 0, 3)
    w13t = np.ascontiguousarray(
        w13a.reshape(2, NHC, 128, NDC, 128).transpose(1, 2, 0, 3, 4))
    w2t = np.ascontiguousarray(
        np.asarray(w2, f32).T.reshape(NHC, 128, NDC, 128).transpose(2, 1, 0, 3))
    nwt = np.ascontiguousarray(np.asarray(norm_w, f32).reshape(NHC, 128).T)

    i = np.arange(64, dtype=f32)
    e = (f32(-2.0) * i) / f32(128.0)
    inv = np.power(f32(10000.0), e, dtype=f32)
    tt = np.arange(T, dtype=f32)
    ang = (tt[:, None] * inv[None, :]).astype(f32)
    cosa = np.cos(ang.astype(np.float64)).astype(f32)
    sina = np.sin(ang.astype(np.float64)).astype(f32)
    cost = np.tile(cosa, (1, 2))
    sins = np.tile(sina, (1, 2))
    sins[:, :64] *= -1.0
    cosf = np.ascontiguousarray(cost.T)
    sinf = np.ascontiguousarray(sins.T)

    id32 = np.eye(128, dtype=f32)
    ones128 = np.ones((128, 128), ml_dtypes.bfloat16)
    ones1f = np.ones((1, 128), f32)
    onescb = np.ones((128, 1), ml_dtypes.bfloat16)
    q_idx = np.arange(512)
    tri = np.zeros((128, 4, 512), np.float32)
    for k in range(4):
        s_idx = np.arange(128)[:, None] + k * 128
        tri[:, k, :] = (q_idx[None, :] >= s_idx).astype(np.float32)
    tri = tri.astype(ml_dtypes.bfloat16)
    return dict(wqk=wqk, wv=wvv, wo=woo, w13=w13t, w2=w2t, nwt=nwt,
                cosf=cosf, sinf=sinf, id32=id32, ones128=ones128,
                ones1f=ones1f, onescb=onescb, tri=tri)


def kernel(L0_attn_x, L0_attn_Wq, L0_attn_Wk, L0_attn_Wv, L0_attn_Wo,
           L0_ffn_w13, L0_ffn_w2, L0_ffn_norm_w, _debug_taps=(), _nc=None):
    xin = np.ascontiguousarray(np.asarray(L0_attn_x, np.float32))
    consts = _host_layouts(L0_attn_Wq, L0_attn_Wk, L0_attn_Wv, L0_attn_Wo,
                           L0_ffn_w13, L0_ffn_w2, L0_ffn_norm_w)
    nc = _nc if _nc is not None else build(_debug_taps)
    in_maps = []
    for b in range(8):
        m = dict(consts)
        m["x"] = np.ascontiguousarray(xin[b])
        in_maps.append(m)
    res = run_bass_kernel_spmd(nc, in_maps, core_ids=list(range(8)))
    outs = [np.asarray(r["out"], np.float32) for r in res.results]
    if _debug_taps:
        return np.stack(outs, 0), res
    return np.stack(outs, 0)


# version 06
# speedup vs baseline: 1.0512x; 1.0512x over previous
import numpy as np
import ml_dtypes
import concourse.bass as bass
import concourse.mybir as mybir
from concourse.bass_utils import run_bass_kernel_spmd
from contextlib import ExitStack

F32 = mybir.dt.float32
BF16 = mybir.dt.bfloat16
I8 = mybir.dt.int8
AF = mybir.ActivationFunctionType
ALU = mybir.AluOpType
AX = mybir.AxisListType

T = 2048
D = 2048
NH = 16
NKV = 4
HID = 5632
NDC = 16
NHC = 44
SCALE = 1.0 / float(np.float32(np.sqrt(np.float32(128.0))))
EPS = 1e-5


class Sem:
    def __init__(self, h):
        self.h = h
        self.n = 0

    def inc(self, k):
        self.n += k
        return self.n


class EP:

    def __init__(self):
        self.ops = []

    def wait(self, sem, val):
        if val > 0:
            self.ops.append(("wait", sem.h, val))

    def _p(self, *a, sem=None, inc=0):
        if sem is not None:
            sem.inc(inc)
            self.ops.append(a + (sem.h, inc))
        else:
            self.ops.append(a + (None, 0))

    def dma(self, out, in_, sem):
        self._p("dma", out, in_, sem=sem, inc=16)

    def dmat(self, out, in_, sem):
        self._p("dmat", out, in_, sem=sem, inc=16)

    def mm(self, out, lhsT, rhs, start, stop, sem=None):
        self._p("mm", out, lhsT, rhs, start, stop, sem=sem, inc=1 if sem else 0)

    def tr(self, out, in_, ident, sem=None):
        self._p("tr", out, in_, ident, sem=sem, inc=1 if sem else 0)

    def act(self, out, in_, func, bias=0.0, scale=1.0, accum=None, sem=None):
        self._p("act", out, in_, func, bias, scale, accum, sem=sem,
                inc=1 if sem else 0)

    def scopy(self, out, in_, sem=None):
        self._p("scopy", out, in_, sem=sem, inc=1 if sem else 0)

    def tt(self, out, a, b, op, sem=None):
        self._p("tt", out, a, b, op, sem=sem, inc=1 if sem else 0)

    def ts(self, out, in0, s1, s2, op0, op1=None, sem=None):
        self._p("ts", out, in0, s1, s2, op0, op1, sem=sem, inc=1 if sem else 0)

    def tcopy(self, out, in_, sem=None):
        self._p("tcopy", out, in_, sem=sem, inc=1 if sem else 0)

    def recip(self, out, in_, sem=None):
        self._p("recip", out, in_, sem=sem, inc=1 if sem else 0)

    def rsum(self, out, in_, sem=None):
        self._p("rsum", out, in_, sem=sem, inc=1 if sem else 0)

    def rmax(self, out, in_, absval=False, sem=None):
        self._p("rmax", out, in_, absval, sem=sem, inc=1 if sem else 0)

    def memset(self, ap, v, sem=None):
        self._p("memset", ap, v, sem=sem, inc=1 if sem else 0)

    def drain(self):
        self.ops.append(("drain", None, 0))


def _replay(eng, ops):
    for op in ops:
        k = op[0]
        if k == "wait":
            eng.wait_ge(op[1], op[2])
            continue
        if k == "drain":
            eng.drain()
            continue
        sem, inc = op[-2], op[-1]
        if k == "dma":
            ins = eng.dma_start(out=op[1], in_=op[2])
        elif k == "dmat":
            ins = eng.dma_start_transpose(op[1], op[2])
        elif k == "mm":
            ins = eng.matmul(op[1], op[2], op[3], start=op[4], stop=op[5])
        elif k == "tr":
            ins = eng.transpose(op[1], op[2], op[3])
        elif k == "act":
            ins = eng.activation(op[1], op[2], op[3], bias=op[4], scale=op[5],
                                 accum_out=op[6])
        elif k == "scopy":
            ins = eng.copy(op[1], op[2])
        elif k == "tt":
            ins = eng.tensor_tensor(op[1], op[2], op[3], op[4])
        elif k == "ts":
            if op[6] is None:
                ins = eng.tensor_scalar(op[1], op[2], op[3], None, op[5])
            else:
                ins = eng.tensor_scalar(op[1], op[2], op[3], op[4], op[5], op[6])
        elif k == "tcopy":
            ins = eng.tensor_copy(op[1], op[2])
        elif k == "recip":
            ins = eng.reciprocal(op[1], op[2])
        elif k == "rsum":
            ins = eng.reduce_sum(op[1], op[2], axis=AX.X)
        elif k == "rmax":
            ins = eng.reduce_max(op[1], op[2], axis=AX.X,
                                 apply_absolute_value=op[3])
        elif k == "memset":
            ins = eng.memset(op[1], op[2])
        else:
            raise ValueError(k)
        if sem is not None:
            ins.then_inc(sem, inc)


def run_block(nc, plans):
    with nc.Block() as blk:
        for name, ep in plans.items():
            if not ep.ops:
                continue
            getattr(blk, name)(lambda eng, _ops=tuple(ep.ops): _replay(eng, _ops))


def build(debug_taps=()):
    nc = bass.Bass("TRN2", target_bir_lowering=False)
    dt = nc.dram_tensor
    x = dt("x", [T, D], F32, kind="ExternalInput").ap()
    wqk = dt("wqk", [20, 128, NDC, 128], F32, kind="ExternalInput").ap()
    wv = dt("wv", [128, NDC, 512], F32, kind="ExternalInput").ap()
    wo = dt("wo", [NDC, 128, NDC, 128], F32, kind="ExternalInput").ap()
    w13 = dt("w13", [NHC, 128, 2, NDC, 128], F32, kind="ExternalInput").ap()
    w2 = dt("w2", [NDC, 128, NHC, 128], F32, kind="ExternalInput").ap()
    nwt = dt("nwt", [128, NHC], F32, kind="ExternalInput").ap()
    cosf = dt("cosf", [128, T], F32, kind="ExternalInput").ap()
    sinf = dt("sinf", [128, T], F32, kind="ExternalInput").ap()
    id32 = dt("id32", [128, 128], F32, kind="ExternalInput").ap()
    ones128 = dt("ones128", [128, 128], BF16, kind="ExternalInput").ap()
    ones1f = dt("ones1f", [1, 128], F32, kind="ExternalInput").ap()
    onescb = dt("onescb", [128, 1], BF16, kind="ExternalInput").ap()
    tri = dt("tri", [128, 4, 512], BF16, kind="ExternalInput").ap()
    out = dt("out", [T, D], F32, kind="ExternalOutput").ap()

    xbf = dt("xbf", [T, D], BF16, kind="Internal").ap()
    x1sp = dt("x1sp", [T, D], F32, kind="Internal").ap()
    xqsp = dt("xqsp", [T, D], BF16, kind="Internal").ap()

    taps = {}
    for tap in debug_taps:
        shp = {"qy": [128, NH * T], "k4": [128, NKV * T], "vtm": [128, 16 * 512],
               "y": [128, NH * T], "x1": [T, D], "xq": [T, D],
               "gn": [128, NHC * 512], "afm0": [128, NDC * 512],
               "z0": [128, 4 * D]}[tap]
        dtp = {"x1": F32, "afm0": F32, "z0": F32}.get(tap, BF16)
        taps[tap] = dt("tap_" + tap, shp, dtp, kind="ExternalOutput").ap()

    with ExitStack() as st:
        ent = st.enter_context
        c_id32 = ent(nc.sbuf_tensor("c_id32", [128, 128], F32))
        c_o128 = ent(nc.sbuf_tensor("c_o128", [128, 128], BF16))
        c_o1f = ent(nc.sbuf_tensor("c_o1f", [1, 128], F32))
        c_ocb = ent(nc.sbuf_tensor("c_ocb", [128, 1], BF16))
        c_tri = ent(nc.sbuf_tensor("c_tri", [128, 4, 512], BF16))
        c_nw = ent(nc.sbuf_tensor("c_nw", [128, NHC], F32))
        LS = Sem(ent(nc.semaphore("LS")))
        LQ = Sem(ent(nc.semaphore("LQ")))
        TS_ = Sem(ent(nc.semaphore("TSm")))
        VS = Sem(ent(nc.semaphore("VSm")))
        SS = Sem(ent(nc.semaphore("SSm")))

        stQY = ExitStack()
        QY = stQY.enter_context(nc.sbuf_tensor("QY", [128, NH * T], BF16))

        with ExitStack() as stKV:
            eKV = stKV.enter_context
            K4 = eKV(nc.sbuf_tensor("K4", [128, NKV * T], BF16))
            Vtm = eKV(nc.sbuf_tensor("Vtm", [128, 16 * 512], BF16))

            sy, gp, te, ve, sc = EP(), EP(), EP(), EP(), EP()
            base = LQ.n
            cast_done = [base + 16 * (s + 1) for s in range(16)]
            wv_done = base + 16 * 17
            wqk_done = [wv_done + 16 * (pi + 1) for pi in range(20)]
            order = list(range(16, 20)) + list(range(16))

            with ExitStack() as stXT:
                XT = stXT.enter_context(nc.sbuf_tensor("XT", [128, NDC * T], BF16))
                with ExitStack() as s4:
                    e4 = s4.enter_context
                    xsl = [e4(nc.sbuf_tensor(f"xsl{i}", [128, D], F32))
                           for i in range(2)]
                    for dst, src in [(c_id32[:], id32), (c_o128[:], ones128),
                                     (c_o1f[:], ones1f), (c_ocb[:], onescb),
                                     (c_tri[:], tri), (c_nw[:], nwt)]:
                        sy.dma(dst, src, LS)
                    ld_done = []
                    for s in range(16):
                        if s >= 2:
                            sy.wait(LQ, cast_done[s - 2])
                        sy.dma(xsl[s % 2][:], x[s * 128:(s + 1) * 128, :], LS)
                        ld_done.append(LS.n)
                    sy.wait(LQ, cast_done[15])
                    for c in range(NDC):
                        sy.dmat(XT[:, c * T:(c + 1) * T],
                                xbf[:, c * 128:(c + 1) * 128], LS)
                    xt_done = LS.n
                wqkb = [stXT.enter_context(
                    nc.sbuf_tensor(f"wqkb{i}", [128, NDC, 128], BF16))
                    for i in range(3)]
                pqk = [stXT.enter_context(nc.psum_tensor(f"pqk{i}", [128, 512], F32))
                       for i in range(2)]
                pv = [stXT.enter_context(nc.psum_tensor(f"pv{i}", [128, 512], F32))
                      for i in range(2)]
                ev_done, mm_done, rope_done = {}, {}, {}
                with ExitStack() as sV:
                    wvsb = sV.enter_context(
                        nc.sbuf_tensor("wvsb", [128, NDC, 512], BF16))
                    te.wait(LQ, wv_done)
                    te.wait(LS, xt_done)
                    for stg in range(16):
                        bank = pv[stg % 2]
                        if stg >= 2:
                            te.wait(VS, ev_done[("v", stg - 2)])
                        for d in range(NDC):
                            col = d * T + (stg // 4) * 512 + (stg % 4) * 128
                            te.mm(bank[:], XT[:, col:col + 128], wvsb[:, d, :],
                                  d == 0, d == NDC - 1,
                                  sem=TS_ if d == NDC - 1 else None)
                        mm_done[("v", stg)] = TS_.n
                        ve.wait(TS_, mm_done[("v", stg)])
                        ve.tcopy(Vtm[:, stg * 512:(stg + 1) * 512], bank[:], sem=VS)
                        ev_done[("v", stg)] = VS.n
                scos = stXT.enter_context(nc.sbuf_tensor("scos", [128, T], F32))
                ssin = stXT.enter_context(nc.sbuf_tensor("ssin", [128, T], F32))
                shb = stXT.enter_context(nc.sbuf_tensor("shb", [128, T], BF16))
                t1b = stXT.enter_context(nc.sbuf_tensor("t1b", [128, 1024], F32))
                t2b = stXT.enter_context(nc.sbuf_tensor("t2b", [128, 1024], F32))
                sy.wait(TS_, mm_done[("v", 15)])
                sy.dma(scos[:], cosf, LS)
                sy.dma(ssin[:], sinf, LS)
                tbl_done = LS.n
                for pi, m in enumerate(order):
                    te.wait(LQ, wqk_done[pi])
                    dst = QY if m < 16 else K4
                    dcol = (m if m < 16 else m - 16) * T
                    for tb in range(4):
                        bank = pqk[(pi * 4 + tb) % 2]
                        prev = (pi * 4 + tb) - 2
                        if prev >= 0:
                            te.wait(VS, ev_done[("qk", prev)])
                        for d in range(NDC):
                            te.mm(bank[:], wqkb[pi % 3][:, d, :],
                                  XT[:, d * T + tb * 512: d * T + tb * 512 + 512],
                                  d == 0, d == NDC - 1,
                                  sem=TS_ if d == NDC - 1 else None)
                        mm_done[("qk", pi, tb)] = TS_.n
                        ve.wait(TS_, mm_done[("qk", pi, tb)])
                        if pi == 0 and tb == 0:
                            ve.wait(LS, tbl_done)
                        ve.tcopy(dst[:, dcol + tb * 512: dcol + tb * 512 + 512],
                                 bank[:], sem=VS)
                        ev_done[("qk", pi * 4 + tb)] = VS.n
                    sy.wait(VS, ev_done[("qk", pi * 4 + 3)])
                    if pi >= 1:
                        sy.wait(VS, rope_done[order[pi - 1]])
                    sy.dma(shb[0:64, :], dst[64:128, dcol:dcol + T], LS)
                    sy.dma(shb[64:128, :], dst[0:64, dcol:dcol + T], LS)
                    ve.wait(LS, LS.n)
                    for hh in range(2):
                        lo, hi = hh * 1024, (hh + 1) * 1024
                        ve.tt(t1b[:], dst[:, dcol + lo: dcol + hi],
                              scos[:, lo:hi], ALU.mult)
                        ve.tt(t2b[:], shb[:, lo:hi], ssin[:, lo:hi], ALU.mult)
                        ve.tt(dst[:, dcol + lo: dcol + hi], t1b[:], t2b[:],
                              ALU.add, sem=VS if hh == 1 else None)
                    rope_done[m] = VS.n
                for si in range(16):
                    gp.wait(LS, ld_done[si])
                    gp.dma(xbf[si * 128:(si + 1) * 128, :], xsl[si % 2][:], LQ)
                    assert LQ.n == cast_done[si]
                gp.dma(wvsb[:], wv, LQ)
                assert LQ.n == wv_done
                for pi, m in enumerate(order):
                    if pi >= 3:
                        gp.wait(TS_, mm_done[("qk", pi - 3, 3)])
                    gp.dma(wqkb[pi % 3][:], wqk[m], LQ)
                    assert LQ.n == wqk_done[pi]
                qk_ev_last = ev_done[("qk", 79)]

            with ExitStack() as s4:
                e4 = s4.enter_context
                PT = e4(nc.sbuf_tensor("PT", [128, 16 * 512], BF16))
                invl = e4(nc.sbuf_tensor("invl", [128, 512], F32))
                lntmp = e4(nc.sbuf_tensor("lntmp", [128, 512], F32))
                ps = [e4(nc.psum_tensor(f"ps{i}", [128, 512], F32))
                      for i in range(4)]
                pl = e4(nc.psum_tensor("pl", [128, 512], F32))
                po = e4(nc.psum_tensor("po", [128, 512], F32))
                bank_free = {0: 0, 1: 0, 2: 0, 3: 0}
                oevac = 0
                first_st = [True]
                for h in range(NH):
                    g = h // 4
                    for jb in range(4):
                        nsc = 4 * (jb + 1)
                        q0 = h * T + jb * 512
                        exp_done, mask_done = [], {}

                        def emit_lav(c):
                            te.wait(SS, exp_done[c])
                            if c in mask_done:
                                te.wait(VS, mask_done[c])
                            if c == 0:
                                te.wait(VS, oevac)
                            te.mm(pl[:], c_o128[:], PT[:, c * 512:(c + 1) * 512],
                                  c == 0, c == nsc - 1,
                                  sem=TS_ if c == nsc - 1 else None)
                            te.mm(po[:],
                                  Vtm[:, c * 512 + g * 128: c * 512 + g * 128 + 128],
                                  PT[:, c * 512:(c + 1) * 512],
                                  c == 0, c == nsc - 1,
                                  sem=TS_ if c == nsc - 1 else None)

                        for c in range(nsc):
                            bk = c % 4
                            te.wait(SS, bank_free[bk])
                            if first_st[0]:
                                te.wait(VS, qk_ev_last)
                                first_st[0] = False
                            if c == 0:
                                te.wait(VS, rope_done[h])
                                te.wait(VS, rope_done[16 + g])
                            te.mm(ps[bk][:],
                                  K4[:, g * T + c * 128: g * T + c * 128 + 128],
                                  QY[:, q0:q0 + 512], True, True, sem=TS_)
                            sc.wait(TS_, TS_.n)
                            sc.act(PT[:, c * 512:(c + 1) * 512], ps[bk][:],
                                   AF.Exp, bias=0.0, scale=SCALE, sem=SS)
                            exp_done.append(SS.n)
                            bank_free[bk] = SS.n
                            if c >= 4 * jb:
                                ve.wait(SS, exp_done[c])
                                ve.tt(PT[:, c * 512:(c + 1) * 512],
                                      PT[:, c * 512:(c + 1) * 512],
                                      c_tri[:, c - 4 * jb, :], ALU.mult, sem=VS)
                                mask_done[c] = VS.n
                            if c >= 4:
                                emit_lav(c - 4)
                        for c in range(max(0, nsc - 4), nsc):
                            emit_lav(c)
                        to = TS_.n
                        tl = TS_.n - 1
                        sc.wait(TS_, tl)
                        sc.act(lntmp[:], pl[:], AF.Ln)
                        sc.act(invl[:], lntmp[:], AF.Exp, bias=0.0, scale=-1.0,
                               sem=SS)
                        inv_d = SS.n
                        ve.wait(SS, inv_d)
                        ve.wait(TS_, to)
                        ve.tt(QY[:, q0:q0 + 512], po[:], invl[:], ALU.mult, sem=VS)
                        oevac = VS.n
                for nm, tsr in [("y", QY), ("qy", QY), ("k4", K4), ("vtm", Vtm)]:
                    if nm in taps:
                        sy.wait(VS, VS.n)
                        sy.dma(taps[nm], tsr[:], LS)
                if taps:
                    sy.wait(LS, LS.n)
                run_block(nc, {"sync": sy, "gpsimd": gp, "tensor": te,
                               "vector": ve, "scalar": sc})

        x1st_done = [0] * 16
        xqst_done = [0] * 16
        zadd_v = [0] * 16
        with ExitStack() as s4:
            e4 = s4.enter_context
            afm = e4(nc.sbuf_tensor("afm", [128, NDC * 512], F32))
            Ztm = e4(nc.sbuf_tensor("Ztm", [128, 4 * D], F32))
            wob = [e4(nc.sbuf_tensor(f"wob{i}", [128, NDC, 128], BF16))
                   for i in range(2)]
            xslP = [e4(nc.sbuf_tensor(f"xslP{i}", [128, D], F32)) for i in range(2)]
            x1sl = [e4(nc.sbuf_tensor(f"x1sl{i}", [128, D], F32)) for i in range(2)]
            k8sl = [e4(nc.sbuf_tensor(f"k8sl{i}", [128, D], I8)) for i in range(2)]
            xqsl = [e4(nc.sbuf_tensor(f"xqsl{i}", [128, D], BF16)) for i in range(2)]
            sqscr = e4(nc.sbuf_tensor("sqscr", [128, D], F32))
            stt = e4(nc.sbuf_tensor("stt", [128, 48], F32))
            pw = [e4(nc.psum_tensor(f"pw{i}", [128, 512], F32)) for i in range(2)]
            ptr = [e4(nc.psum_tensor(f"ptr{i}", [128, 128], F32)) for i in range(2)]
            sy, gp, te, ve, sc = EP(), EP(), EP(), EP(), EP()
            lq0 = LQ.n
            wo_done = [lq0 + 16 * (g + 1) for g in range(4 * NDC)]
            wo_mm, afm_ev, tr_done, ztr_ev = [], [], {}, {}
            sq_ss = [0] * 16
            for tblk in range(4):
                for mc in range(NDC):
                    mcg = tblk * NDC + mc
                    te.wait(LQ, wo_done[mcg])
                    if mcg >= 2:
                        te.wait(VS, afm_ev[mcg - 2])
                    for d in range(NDC):
                        te.mm(pw[mcg % 2][:], wob[mcg % 2][:, d, :],
                              QY[:, d * T + tblk * 512: d * T + tblk * 512 + 512],
                              d == 0, d == NDC - 1,
                              sem=TS_ if d == NDC - 1 else None)
                    wo_mm.append(TS_.n)
                    ve.wait(TS_, wo_mm[mcg])
                    if tblk > 0 and mc == 0:
                        ve.wait(TS_, tr_done[(tblk - 1) * 64 + 63])
                        ve.wait(SS, sq_ss[tblk * 4 - 1])
                    ve.tcopy(afm[:, mc * 512:(mc + 1) * 512], pw[mcg % 2][:], sem=VS)
                    afm_ev.append(VS.n)
                for mc in range(NDC):
                    mcg = tblk * NDC + mc
                    for ts_i in range(4):
                        idx = tblk * 64 + mc * 4 + ts_i
                        te.wait(VS, afm_ev[mcg])
                        if idx >= 2:
                            te.wait(VS, ztr_ev[idx - 2])
                        te.tr(ptr[idx % 2][:],
                              afm[:, mc * 512 + ts_i * 128: mc * 512 + ts_i * 128 + 128],
                              c_id32[:], sem=TS_)
                        tr_done[idx] = TS_.n
                        ve.wait(TS_, tr_done[idx])
                        ve.tcopy(Ztm[:, ts_i * D + mc * 128: ts_i * D + mc * 128 + 128],
                                 ptr[idx % 2][:], sem=VS)
                        ztr_ev[idx] = VS.n
                mset = [0] * 4
                for ts_i in range(4):
                    gidx = tblk * 4 + ts_i
                    row0 = gidx * 128
                    Z = Ztm[:, ts_i * D:(ts_i + 1) * D]
                    if gidx >= 2:
                        sy.wait(VS, zadd_v[gidx - 2])
                    sy.dma(xslP[ts_i % 2][:], x[row0:row0 + 128, :], LS)
                    xld = LS.n
                    ve.wait(LS, xld)
                    ve.tt(Z, Z, xslP[ts_i % 2][:], ALU.add, sem=VS)
                    zadd_v[gidx] = VS.n
                    ve.rsum(stt[:, ts_i:ts_i + 1], Z)
                    ve.memset(stt[:, 4 + ts_i:5 + ts_i], 0.0, sem=VS)
                    mset[ts_i] = VS.n
                    sc.wait(VS, mset[ts_i])
                    sc.act(sqscr[:], Z, AF.Square, accum=stt[:, 4 + ts_i:5 + ts_i],
                           sem=SS)
                    sq_ss[tblk * 4 + ts_i] = SS.n
                ve.wait(SS, sq_ss[tblk * 4 + 3])
                ve.ts(stt[:, 8:12], stt[:, 0:4], -1.0 / D, None, ALU.mult)
                ve.ts(stt[:, 12:16], stt[:, 4:8], 1.0 / D, EPS, ALU.mult, ALU.add)
                ve.drain()
                ve.tt(stt[:, 16:20], stt[:, 8:12], stt[:, 8:12], ALU.mult)
                ve.drain()
                ve.tt(stt[:, 20:24], stt[:, 12:16], stt[:, 16:20], ALU.subtract)
                ve.drain()
                ve.recip(stt[:, 24:28], stt[:, 20:24], sem=VS)
                rvd = VS.n
                sc.wait(VS, rvd)
                sc.act(stt[:, 28:32], stt[:, 24:28], AF.Sqrt, sem=SS)
                rstd_d = SS.n
                ve.wait(SS, rstd_d)
                ve.tt(stt[:, 32:36], stt[:, 8:12], stt[:, 28:32], ALU.mult)
                ve.drain()
                for ts_i in range(4):
                    Z = Ztm[:, ts_i * D:(ts_i + 1) * D]
                    ve.ts(sqscr[:], Z, 1.0, stt[:, 8 + ts_i:9 + ts_i],
                          ALU.mult, ALU.add)
                    ve.rmax(stt[:, 36 + ts_i:37 + ts_i], sqscr[:], absval=True)
                ve.drain()
                ve.tt(stt[:, 40:44], stt[:, 36:40], stt[:, 28:32], ALU.mult)
                ve.drain()
                ve.ts(stt[:, 40:44], stt[:, 40:44], EPS, None, ALU.max)
                ve.drain()
                ve.recip(stt[:, 36:40], stt[:, 40:44])
                ve.drain()
                ve.ts(stt[:, 44:48], stt[:, 36:40], 127.0, None, ALU.mult)
                ve.ts(stt[:, 36:40], stt[:, 40:44], 1.0 / 127.0, None, ALU.mult)
                ve.drain()
                for ts_i in range(4):
                    gidx = tblk * 4 + ts_i
                    row0 = gidx * 128
                    Z = Ztm[:, ts_i * D:(ts_i + 1) * D]
                    if gidx >= 2:
                        ve.wait(LS, x1st_done[gidx - 2])
                        ve.wait(LS, xqst_done[gidx - 2])
                    ve.ts(x1sl[ts_i % 2][:], Z, stt[:, 28 + ts_i:29 + ts_i],
                          stt[:, 32 + ts_i:33 + ts_i], ALU.mult, ALU.add, sem=VS)
                    x1d = VS.n
                    sy.wait(VS, x1d)
                    sy.dma(x1sp[row0:row0 + 128, :], x1sl[ts_i % 2][:], LS)
                    x1st_done[gidx] = LS.n
                    ve.ts(k8sl[ts_i % 2][:], x1sl[ts_i % 2][:],
                          stt[:, 44 + ts_i:45 + ts_i], None, ALU.mult)
                    ve.ts(xqsl[ts_i % 2][:], k8sl[ts_i % 2][:],
                          stt[:, 36 + ts_i:37 + ts_i], None, ALU.mult, sem=VS)
                    xqd = VS.n
                    sy.wait(VS, xqd)
                    sy.dma(xqsp[row0:row0 + 128, :], xqsl[ts_i % 2][:], LS)
                    xqst_done[gidx] = LS.n
            gp2 = EP()
            for tblk in range(4):
                for mc in range(NDC):
                    mcg = tblk * NDC + mc
                    if mcg >= 2:
                        gp2.wait(TS_, wo_mm[mcg - 2])
                    gp2.dma(wob[mcg % 2][:], wo[mc], LQ)
                    assert LQ.n == wo_done[mcg]
            gp = gp2
            if "x1" in taps:
                sy.wait(LS, LS.n)
                sy.dma(taps["x1"], x1sp, LS)
            if "xq" in taps:
                sy.wait(LS, LS.n)
                sy.dma(taps["xq"], xqsp, LS)
                sy.wait(LS, LS.n)
            run_block(nc, {"sync": sy, "gpsimd": gp, "tensor": te,
                           "vector": ve, "scalar": sc})
        stQY.close()

        with ExitStack() as s4:
            e4 = s4.enter_context
            XqTb = e4(nc.sbuf_tensor("XqTb", [128, NDC * 512], BF16))
            GN = e4(nc.sbuf_tensor("GN", [128, NHC * 512 + D], BF16))
            w13b = [e4(nc.sbuf_tensor(f"w13b{i}", [128, 2, NDC, 128], BF16))
                    for i in range(2)]
            w2b = [e4(nc.sbuf_tensor(f"w2b{i}", [128, NHC, 128], BF16))
                   for i in range(2)]
            rbuf = e4(nc.sbuf_tensor("rb4", [128, 512], F32))
            r2buf = e4(nc.sbuf_tensor("r2b4", [128, 512], F32))
            gbuf = e4(nc.sbuf_tensor("gb4", [128, 512], F32))
            gsqb = [e4(nc.sbuf_tensor(f"gsq4_{i}", [128, 512], BF16))
                    for i in range(2)]
            gnsqb = e4(nc.sbuf_tensor("gnsq4", [128, 512], F32))
            runmax = e4(nc.sbuf_tensor("rmx4", [128, 512], F32))
            m2col = e4(nc.sbuf_tensor("m2c4", [128, 4], F32))
            rows4 = e4(nc.sbuf_tensor("r4_4", [4, 128], F32))
            rowscr = e4(nc.sbuf_tensor("rsc4", [1, 2048], F32))
            RSb = e4(nc.sbuf_tensor("RSb4", [128, 1024], BF16))
            iS2b = e4(nc.sbuf_tensor("iS2b4", [128, 512], F32))
            k8b = e4(nc.sbuf_tensor("k8b4", [128, 1024], I8))
            z2fm = [e4(nc.sbuf_tensor(f"z2f4_{i}", [128, 512], F32))
                    for i in range(2)]
            Ztm2 = e4(nc.sbuf_tensor("Zt2_4", [128, 4 * D], F32))
            x1r = [e4(nc.sbuf_tensor(f"x1r4_{i}", [128, D], F32)) for i in range(2)]
            outsl = [e4(nc.sbuf_tensor(f"osl4_{i}", [128, D], F32))
                     for i in range(2)]
            st2t = e4(nc.sbuf_tensor("st2_4", [128, 48], F32))
            pg = [e4(nc.psum_tensor(f"pg4_{i}", [128, 512], F32)) for i in range(4)]
            pstat = e4(nc.psum_tensor("pst4", [1, 512], F32))
            ptrx = [e4(nc.psum_tensor(f"ptx4_{i}", [128, 128], F32))
                    for i in range(2)]
            pz = e4(nc.psum_tensor("pz4", [128, 512], F32))
            sy, gp, te, ve, sc = EP(), EP(), EP(), EP(), EP()

            lq0 = LQ.n
            w13_done = [[lq0 + 16 * (tb * (NHC + NDC) + j + 1) for j in range(NHC)]
                        for tb in range(4)]
            w2_done = [[lq0 + 16 * (tb * (NHC + NDC) + NHC + mc + 1)
                        for mc in range(NDC)] for tb in range(4)]
            mm1_done, gate_v, gate_s, gsq_ready = [], [], [], []
            mm2_done, z2fm_ev, trz_ev = [], [], {}
            outst = [0] * 16
            z2add_v = [0] * 16
            prev = {"bc_d": 0, "ssr_d": 0, "trm": 0, "mm2_last": 0, "trz_last": 0}
            xqt_done_arr = [0] * 4
            for tblk in range(4):
                if tblk == 0:
                    sy.wait(LS, xqst_done[3])
                    for c in range(NDC):
                        sy.dmat(XqTb[:, c * 512:(c + 1) * 512],
                                xqsp[0:512, c * 128:(c + 1) * 128], LS)
                    xqt_done_arr[0] = LS.n
                xqt_done = xqt_done_arr[tblk]

                te.wait(VS, prev["bc_d"])
                te.wait(SS, prev["ssr_d"])
                for j in range(NHC):
                    jg = tblk * NHC + j
                    te.wait(LQ, w13_done[tblk][j])
                    if jg >= 2:
                        te.wait(VS, gate_v[jg - 2])
                        te.wait(SS, gate_s[jg - 2])
                    if j == 0:
                        te.wait(LS, xqt_done)
                    for half in range(2):
                        bank = pg[(jg % 2) * 2 + half]
                        for d in range(NDC):
                            te.mm(bank[:], w13b[jg % 2][:, half, d, :],
                                  XqTb[:, d * 512:(d + 1) * 512],
                                  d == 0, d == NDC - 1,
                                  sem=TS_ if d == NDC - 1 else None)
                        mm1_done.append(TS_.n)
                    if j >= 2:
                        te.wait(SS, gsq_ready[jg - 2])
                        te.mm(pstat[:], c_ocb[:], gsqb[(jg - 2) % 2][:],
                              j - 2 == 0, False, sem=None)
                    tx1, tx3 = mm1_done[2 * jg], mm1_done[2 * jg + 1]
                    sc.wait(TS_, tx1)
                    sc.act(rbuf[:], pg[(jg % 2) * 2][:], AF.Relu)
                    sc.act(r2buf[:], rbuf[:], AF.Square, sem=SS)
                    r2d = SS.n
                    ve.wait(SS, r2d)
                    ve.wait(TS_, tx3)
                    if j == 0 and tblk > 0:
                        ve.wait(TS_, prev["mm2_last"])
                        ve.wait(TS_, prev["trm"])
                    ve.tt(gbuf[:], r2buf[:], pg[(jg % 2) * 2 + 1][:], ALU.mult,
                          sem=VS)
                    gbd = VS.n
                    sc.wait(VS, gbd)
                    sc.act(gsqb[jg % 2][:], gbuf[:], AF.Square, sem=SS)
                    gsq_ready.append(SS.n)
                    ve.ts(GN[:, j * 512:(j + 1) * 512], gbuf[:],
                          c_nw[:, j:j + 1], None, ALU.mult, sem=VS)
                    sc.wait(VS, VS.n)
                    sc.act(gnsqb[:], GN[:, j * 512:(j + 1) * 512], AF.Square,
                           sem=SS)
                    ve.wait(SS, SS.n)
                    if j == 0:
                        ve.tcopy(runmax[:], gnsqb[:], sem=VS)
                    else:
                        ve.tt(runmax[:], runmax[:], gnsqb[:], ALU.max, sem=VS)
                    gate_v.append(VS.n)
                    gate_s.append(SS.n)
                te.wait(SS, gsq_ready[tblk * NHC + NHC - 2])
                te.mm(pstat[:], c_ocb[:], gsqb[(tblk * NHC + NHC - 2) % 2][:],
                      False, False, sem=None)
                te.wait(SS, gsq_ready[tblk * NHC + NHC - 1])
                te.mm(pstat[:], c_ocb[:], gsqb[(tblk * NHC + NHC - 1) % 2][:],
                      False, True, sem=TS_)
                tstat = TS_.n
                sc.wait(TS_, tstat)
                sc.scopy(rowscr[:, 0:512], pstat[:], sem=SS)
                ssr_d = SS.n
                prev["ssr_d"] = ssr_d
                ve.wait(SS, ssr_d)
                ve.ts(rowscr[:, 512:1024], rowscr[:, 0:512], 1.0 / HID, EPS,
                      ALU.mult, ALU.add)
                ve.drain()
                ve.recip(rowscr[:, 1024:1536], rowscr[:, 512:1024], sem=VS)
                rv2 = VS.n
                sc.wait(VS, rv2)
                sc.act(rowscr[:, 1536:2048], rowscr[:, 1024:1536], AF.Sqrt, sem=SS)
                rstd2_d = SS.n
                trv = []
                for i in range(4):
                    if i == 0:
                        te.wait(VS, gate_v[tblk * NHC + NHC - 1])
                        if tblk > 0:
                            te.wait(VS, trz_ev[tblk * 64 - 1])
                    if i >= 2:
                        te.wait(VS, trv[i - 2])
                    te.tr(ptrx[i % 2][:], runmax[:, i * 128:(i + 1) * 128],
                          c_id32[:], sem=TS_)
                    ti = TS_.n
                    ve.wait(TS_, ti)
                    ve.rmax(m2col[:, i:i + 1], ptrx[i % 2][:], absval=False,
                            sem=VS)
                    trv.append(VS.n)
                te.wait(VS, trv[3])
                te.tr(ptrx[0][0:4, :], m2col[:], c_id32[:], sem=TS_)
                trm = TS_.n
                prev["trm"] = trm
                ve.wait(TS_, trm)
                ve.tcopy(rows4[:], ptrx[0][0:4, 0:128], sem=VS)
                r4d = VS.n
                sy.wait(VS, r4d)
                sy.dma(rowscr[:, 0:512], rows4[:], LS)
                m2r_d = LS.n
                if tblk < 3:
                    sy.wait(LS, xqst_done[(tblk + 1) * 4 + 3])
                    sy.wait(TS_, mm1_done[2 * (tblk + 1) * NHC - 1])
                    for c in range(NDC):
                        sy.dmat(XqTb[:, c * 512:(c + 1) * 512],
                                xqsp[(tblk + 1) * 512:(tblk + 2) * 512,
                                     c * 128:(c + 1) * 128], LS)
                    xqt_done_arr[tblk + 1] = LS.n
                sc.wait(LS, m2r_d)
                sc.act(rowscr[:, 512:1024], rowscr[:, 0:512], AF.Sqrt, sem=SS)
                sm2_d = SS.n
                ve.wait(SS, sm2_d)
                ve.wait(SS, rstd2_d)
                ve.tt(rowscr[:, 1024:1536], rowscr[:, 512:1024],
                      rowscr[:, 1536:2048], ALU.mult)
                ve.drain()
                ve.ts(rowscr[:, 1024:1536], rowscr[:, 1024:1536], EPS, None,
                      ALU.max)
                ve.drain()
                ve.recip(rowscr[:, 0:512], rowscr[:, 1024:1536])
                ve.drain()
                ve.ts(rowscr[:, 0:512], rowscr[:, 0:512], 127.0, None,
                      ALU.mult)
                ve.drain()
                ve.ts(rowscr[:, 512:1024], rowscr[:, 1024:1536], 1.0 / 127.0,
                      None, ALU.mult)
                ve.drain()
                ve.tt(rowscr[:, 1024:1536], rowscr[:, 1536:2048],
                      rowscr[:, 0:512], ALU.mult, sem=VS)
                rows_d = VS.n
                te.wait(VS, rows_d)
                te.mm(pg[0][:], c_o1f[:], rowscr[:, 1024:1536], True, True,
                      sem=TS_)
                tb1 = TS_.n
                te.mm(pg[1][:], c_o1f[:], rowscr[:, 512:1024], True, True,
                      sem=TS_)
                tb2 = TS_.n
                ve.wait(TS_, tb1)
                ve.tcopy(RSb[:, 0:512], pg[0][:])
                ve.tcopy(RSb[:, 512:1024], pg[0][:])
                ve.wait(TS_, tb2)
                ve.tcopy(iS2b[:], pg[1][:], sem=VS)
                prev["bc_d"] = VS.n
                for jj in range(NHC // 2):
                    sl = slice(jj * 1024, (jj + 1) * 1024)
                    ve.tt(k8b[:], GN[:, sl], RSb[:], ALU.mult)
                    ve.tcopy(GN[:, sl], k8b[:], sem=VS)
                iq_d = VS.n
                if "gn" in taps and tblk == 0:
                    sy.wait(VS, iq_d)
                    sy.dma(taps["gn"], GN[:, 0:NHC * 512], LS)
                for mc in range(NDC):
                    mcg = tblk * NDC + mc
                    te.wait(LQ, w2_done[tblk][mc])
                    if mc == 0:
                        te.wait(VS, iq_d)
                    if mcg >= 1:
                        te.wait(VS, z2fm_ev[mcg - 1])
                    for j in range(NHC):
                        te.mm(pz[:], w2b[mcg % 2][:, j, :],
                              GN[:, j * 512:(j + 1) * 512],
                              j == 0, j == NHC - 1,
                              sem=TS_ if j == NHC - 1 else None)
                    mm2_done.append(TS_.n)
                    ve.wait(TS_, mm2_done[mcg])
                    ve.tt(z2fm[mcg % 2][:], pz[:], iS2b[:], ALU.mult, sem=VS)
                    z2fm_ev.append(VS.n)
                    for ts_i in range(4):
                        idx = tblk * 64 + mc * 4 + ts_i
                        if ts_i == 0:
                            te.wait(VS, z2fm_ev[mcg])
                        if idx >= 2:
                            te.wait(VS, trz_ev[idx - 2])
                        te.tr(ptrx[idx % 2][:],
                              z2fm[mcg % 2][:, ts_i * 128:(ts_i + 1) * 128],
                              c_id32[:], sem=TS_)
                        tdd = TS_.n
                        ve.wait(TS_, tdd)
                        if idx == tblk * 64 and tblk > 0:
                            ve.wait(SS, prev["sq2_last"])
                        ve.tcopy(Ztm2[:, ts_i * D + mc * 128:
                                 ts_i * D + mc * 128 + 128],
                                 ptrx[idx % 2][:], sem=VS)
                        trz_ev[idx] = VS.n
                prev["mm2_last"] = mm2_done[tblk * NDC + NDC - 1]
                ms2 = [0] * 4
                sq2l = [0] * 4
                for ts_i in range(4):
                    gidx = tblk * 4 + ts_i
                    row0 = gidx * 128
                    Z = Ztm2[:, ts_i * D:(ts_i + 1) * D]
                    sy.wait(LS, x1st_done[gidx])
                    if gidx >= 2:
                        sy.wait(VS, z2add_v[gidx - 2])
                    sy.dma(x1r[ts_i % 2][:], x1sp[row0:row0 + 128, :], LS)
                    x1rd = LS.n
                    ve.wait(LS, x1rd)
                    if ts_i == 0:
                        ve.wait(VS, trz_ev[tblk * 64 + 63])
                    ve.tt(Z, Z, x1r[ts_i % 2][:], ALU.add, sem=VS)
                    z2add_v[gidx] = VS.n
                    ve.rsum(st2t[:, ts_i:ts_i + 1], Z)
                    ve.memset(st2t[:, 4 + ts_i:5 + ts_i], 0.0, sem=VS)
                    ms2[ts_i] = VS.n
                    sc.wait(VS, ms2[ts_i])
                    sc.act(GN[:, NHC * 512:NHC * 512 + D], Z, AF.Square,
                           accum=st2t[:, 4 + ts_i:5 + ts_i], sem=SS)
                    sq2l[ts_i] = SS.n
                    prev["sq2_last"] = SS.n
                ve.wait(SS, sq2l[3])
                ve.ts(st2t[:, 8:12], st2t[:, 0:4], -1.0 / D, None, ALU.mult)
                ve.ts(st2t[:, 12:16], st2t[:, 4:8], 1.0 / D, EPS, ALU.mult,
                      ALU.add)
                ve.drain()
                ve.tt(st2t[:, 16:20], st2t[:, 8:12], st2t[:, 8:12], ALU.mult)
                ve.drain()
                ve.tt(st2t[:, 20:24], st2t[:, 12:16], st2t[:, 16:20],
                      ALU.subtract)
                ve.drain()
                ve.recip(st2t[:, 24:28], st2t[:, 20:24], sem=VS)
                rv3 = VS.n
                sc.wait(VS, rv3)
                sc.act(st2t[:, 28:32], st2t[:, 24:28], AF.Sqrt, sem=SS)
                r2d2 = SS.n
                ve.wait(SS, r2d2)
                ve.tt(st2t[:, 32:36], st2t[:, 8:12], st2t[:, 28:32], ALU.mult)
                ve.drain()
                for ts_i in range(4):
                    gidx = tblk * 4 + ts_i
                    row0 = gidx * 128
                    Z = Ztm2[:, ts_i * D:(ts_i + 1) * D]
                    if gidx >= 2:
                        ve.wait(LS, outst[gidx - 2])
                    ve.ts(outsl[ts_i % 2][:], Z, st2t[:, 28 + ts_i:29 + ts_i],
                          st2t[:, 32 + ts_i:33 + ts_i], ALU.mult, ALU.add,
                          sem=VS)
                    od = VS.n
                    sy.wait(VS, od)
                    sy.dma(out[row0:row0 + 128, :], outsl[ts_i % 2][:], LS)
                    outst[gidx] = LS.n
            sy.wait(LS, LS.n)
            gp2 = EP()
            for tblk in range(4):
                for j in range(NHC):
                    jg = tblk * NHC + j
                    if jg >= 2:
                        gp2.wait(TS_, mm1_done[2 * (jg - 2) + 1])
                    gp2.dma(w13b[jg % 2][:], w13[j], LQ)
                    assert LQ.n == w13_done[tblk][j]
                for mc in range(NDC):
                    mcg = tblk * NDC + mc
                    if mcg >= 2:
                        gp2.wait(TS_, mm2_done[mcg - 2])
                    gp2.dma(w2b[mcg % 2][:], w2[mc], LQ)
                    assert LQ.n == w2_done[tblk][mc]
            gp = gp2
            run_block(nc, {"sync": sy, "gpsimd": gp, "tensor": te,
                           "vector": ve, "scalar": sc})

    return nc


def _host_layouts(Wq, Wk, Wv, Wo, w13, w2, norm_w):
    f32 = np.float32
    wqk_src = np.concatenate([np.asarray(Wq, f32), np.asarray(Wk, f32)], axis=1)
    wqk = np.ascontiguousarray(
        wqk_src.reshape(NDC, 128, 20, 128).transpose(2, 1, 0, 3))
    wvv = np.ascontiguousarray(
        np.asarray(Wv, f32).reshape(NDC, 128, 512).transpose(1, 0, 2))
    woo = np.ascontiguousarray(
        np.asarray(Wo, f32).reshape(NDC, 128, NDC, 128).transpose(2, 1, 0, 3))
    w13a = np.asarray(w13, f32).T.reshape(NDC, 128, 88, 128).transpose(2, 1, 0, 3)
    w13t = np.ascontiguousarray(
        w13a.reshape(2, NHC, 128, NDC, 128).transpose(1, 2, 0, 3, 4))
    w2t = np.ascontiguousarray(
        np.asarray(w2, f32).T.reshape(NHC, 128, NDC, 128).transpose(2, 1, 0, 3))
    nwt = np.ascontiguousarray(np.asarray(norm_w, f32).reshape(NHC, 128).T)

    i = np.arange(64, dtype=f32)
    e = (f32(-2.0) * i) / f32(128.0)
    inv = np.power(f32(10000.0), e, dtype=f32)
    tt = np.arange(T, dtype=f32)
    ang = (tt[:, None] * inv[None, :]).astype(f32)
    cosa = np.cos(ang.astype(np.float64)).astype(f32)
    sina = np.sin(ang.astype(np.float64)).astype(f32)
    cost = np.tile(cosa, (1, 2))
    sins = np.tile(sina, (1, 2))
    sins[:, :64] *= -1.0
    cosf = np.ascontiguousarray(cost.T)
    sinf = np.ascontiguousarray(sins.T)

    id32 = np.eye(128, dtype=f32)
    ones128 = np.ones((128, 128), ml_dtypes.bfloat16)
    ones1f = np.ones((1, 128), f32)
    onescb = np.ones((128, 1), ml_dtypes.bfloat16)
    q_idx = np.arange(512)
    tri = np.zeros((128, 4, 512), np.float32)
    for k in range(4):
        s_idx = np.arange(128)[:, None] + k * 128
        tri[:, k, :] = (q_idx[None, :] >= s_idx).astype(np.float32)
    tri = tri.astype(ml_dtypes.bfloat16)
    return dict(wqk=wqk, wv=wvv, wo=woo, w13=w13t, w2=w2t, nwt=nwt,
                cosf=cosf, sinf=sinf, id32=id32, ones128=ones128,
                ones1f=ones1f, onescb=onescb, tri=tri)


def kernel(L0_attn_x, L0_attn_Wq, L0_attn_Wk, L0_attn_Wv, L0_attn_Wo,
           L0_ffn_w13, L0_ffn_w2, L0_ffn_norm_w, _debug_taps=(), _nc=None):
    xin = np.ascontiguousarray(np.asarray(L0_attn_x, np.float32))
    consts = _host_layouts(L0_attn_Wq, L0_attn_Wk, L0_attn_Wv, L0_attn_Wo,
                           L0_ffn_w13, L0_ffn_w2, L0_ffn_norm_w)
    nc = _nc if _nc is not None else build(_debug_taps)
    in_maps = []
    for b in range(8):
        m = dict(consts)
        m["x"] = np.ascontiguousarray(xin[b])
        in_maps.append(m)
    res = run_bass_kernel_spmd(nc, in_maps, core_ids=list(range(8)))
    outs = [np.asarray(r["out"], np.float32) for r in res.results]
    if _debug_taps:
        return np.stack(outs, 0), res
    return np.stack(outs, 0)


# version 07
# speedup vs baseline: 1.0532x; 1.0020x over previous
import numpy as np
import ml_dtypes
import concourse.bass as bass
import concourse.mybir as mybir
from concourse.bass_utils import run_bass_kernel_spmd
from contextlib import ExitStack

F32 = mybir.dt.float32
BF16 = mybir.dt.bfloat16
I8 = mybir.dt.int8
AF = mybir.ActivationFunctionType
ALU = mybir.AluOpType
AX = mybir.AxisListType

T = 2048
D = 2048
NH = 16
NKV = 4
HID = 5632
NDC = 16
NHC = 44
SCALE = 1.0 / float(np.float32(np.sqrt(np.float32(128.0))))
EPS = 1e-5


class Sem:
    def __init__(self, h):
        self.h = h
        self.n = 0

    def inc(self, k):
        self.n += k
        return self.n


class EP:

    def __init__(self):
        self.ops = []

    def wait(self, sem, val):
        if val > 0:
            self.ops.append(("wait", sem.h, val))

    def _p(self, *a, sem=None, inc=0):
        if sem is not None:
            sem.inc(inc)
            self.ops.append(a + (sem.h, inc))
        else:
            self.ops.append(a + (None, 0))

    def dma(self, out, in_, sem):
        self._p("dma", out, in_, sem=sem, inc=16)

    def dmat(self, out, in_, sem):
        self._p("dmat", out, in_, sem=sem, inc=16)

    def mm(self, out, lhsT, rhs, start, stop, sem=None):
        self._p("mm", out, lhsT, rhs, start, stop, sem=sem, inc=1 if sem else 0)

    def tr(self, out, in_, ident, sem=None):
        self._p("tr", out, in_, ident, sem=sem, inc=1 if sem else 0)

    def act(self, out, in_, func, bias=0.0, scale=1.0, accum=None, sem=None):
        self._p("act", out, in_, func, bias, scale, accum, sem=sem,
                inc=1 if sem else 0)

    def scopy(self, out, in_, sem=None):
        self._p("scopy", out, in_, sem=sem, inc=1 if sem else 0)

    def tt(self, out, a, b, op, sem=None):
        self._p("tt", out, a, b, op, sem=sem, inc=1 if sem else 0)

    def ts(self, out, in0, s1, s2, op0, op1=None, sem=None):
        self._p("ts", out, in0, s1, s2, op0, op1, sem=sem, inc=1 if sem else 0)

    def tcopy(self, out, in_, sem=None):
        self._p("tcopy", out, in_, sem=sem, inc=1 if sem else 0)

    def recip(self, out, in_, sem=None):
        self._p("recip", out, in_, sem=sem, inc=1 if sem else 0)

    def rsum(self, out, in_, sem=None):
        self._p("rsum", out, in_, sem=sem, inc=1 if sem else 0)

    def rmax(self, out, in_, absval=False, sem=None):
        self._p("rmax", out, in_, absval, sem=sem, inc=1 if sem else 0)

    def memset(self, ap, v, sem=None):
        self._p("memset", ap, v, sem=sem, inc=1 if sem else 0)

    def drain(self):
        self.ops.append(("drain", None, 0))


def _replay(eng, ops):
    for op in ops:
        k = op[0]
        if k == "wait":
            eng.wait_ge(op[1], op[2])
            continue
        if k == "drain":
            eng.drain()
            continue
        sem, inc = op[-2], op[-1]
        if k == "dma":
            ins = eng.dma_start(out=op[1], in_=op[2])
        elif k == "dmat":
            ins = eng.dma_start_transpose(op[1], op[2])
        elif k == "mm":
            ins = eng.matmul(op[1], op[2], op[3], start=op[4], stop=op[5])
        elif k == "tr":
            ins = eng.transpose(op[1], op[2], op[3])
        elif k == "act":
            ins = eng.activation(op[1], op[2], op[3], bias=op[4], scale=op[5],
                                 accum_out=op[6])
        elif k == "scopy":
            ins = eng.copy(op[1], op[2])
        elif k == "tt":
            ins = eng.tensor_tensor(op[1], op[2], op[3], op[4])
        elif k == "ts":
            if op[6] is None:
                ins = eng.tensor_scalar(op[1], op[2], op[3], None, op[5])
            else:
                ins = eng.tensor_scalar(op[1], op[2], op[3], op[4], op[5], op[6])
        elif k == "tcopy":
            ins = eng.tensor_copy(op[1], op[2])
        elif k == "recip":
            ins = eng.reciprocal(op[1], op[2])
        elif k == "rsum":
            ins = eng.reduce_sum(op[1], op[2], axis=AX.X)
        elif k == "rmax":
            ins = eng.reduce_max(op[1], op[2], axis=AX.X,
                                 apply_absolute_value=op[3])
        elif k == "memset":
            ins = eng.memset(op[1], op[2])
        else:
            raise ValueError(k)
        if sem is not None:
            ins.then_inc(sem, inc)


def run_block(nc, plans):
    with nc.Block() as blk:
        for name, ep in plans.items():
            if not ep.ops:
                continue
            getattr(blk, name)(lambda eng, _ops=tuple(ep.ops): _replay(eng, _ops))


def build(debug_taps=()):
    nc = bass.Bass("TRN2", target_bir_lowering=False)
    dt = nc.dram_tensor
    x = dt("x", [T, D], F32, kind="ExternalInput").ap()
    wqk = dt("wqk", [20, 128, NDC, 128], F32, kind="ExternalInput").ap()
    wv = dt("wv", [128, NDC, 512], F32, kind="ExternalInput").ap()
    wo = dt("wo", [NDC, 128, NDC, 128], F32, kind="ExternalInput").ap()
    w13 = dt("w13", [NHC, 128, 2, NDC, 128], F32, kind="ExternalInput").ap()
    w2 = dt("w2", [NDC, 128, NHC, 128], F32, kind="ExternalInput").ap()
    nwt = dt("nwt", [128, NHC], F32, kind="ExternalInput").ap()
    cosf = dt("cosf", [128, T], F32, kind="ExternalInput").ap()
    sinf = dt("sinf", [128, T], F32, kind="ExternalInput").ap()
    id32 = dt("id32", [128, 128], F32, kind="ExternalInput").ap()
    ones128 = dt("ones128", [128, 128], BF16, kind="ExternalInput").ap()
    ones1f = dt("ones1f", [1, 128], F32, kind="ExternalInput").ap()
    onescb = dt("onescb", [128, 1], BF16, kind="ExternalInput").ap()
    tri = dt("tri", [128, 4, 512], BF16, kind="ExternalInput").ap()
    out = dt("out", [T, D], F32, kind="ExternalOutput").ap()

    xbf = dt("xbf", [T, D], BF16, kind="Internal").ap()
    x1sp = dt("x1sp", [T, D], F32, kind="Internal").ap()
    xqsp = dt("xqsp", [T, D], BF16, kind="Internal").ap()

    taps = {}
    for tap in debug_taps:
        shp = {"qy": [128, NH * T], "k4": [128, NKV * T], "vtm": [128, 16 * 512],
               "y": [128, NH * T], "x1": [T, D], "xq": [T, D],
               "gn": [128, NHC * 512], "afm0": [128, NDC * 512],
               "z0": [128, 4 * D]}[tap]
        dtp = {"x1": F32, "afm0": F32, "z0": F32}.get(tap, BF16)
        taps[tap] = dt("tap_" + tap, shp, dtp, kind="ExternalOutput").ap()

    with ExitStack() as st:
        ent = st.enter_context
        c_id32 = ent(nc.sbuf_tensor("c_id32", [128, 128], F32))
        c_o128 = ent(nc.sbuf_tensor("c_o128", [128, 128], BF16))
        c_o1f = ent(nc.sbuf_tensor("c_o1f", [1, 128], F32))
        c_ocb = ent(nc.sbuf_tensor("c_ocb", [128, 1], BF16))
        c_tri = ent(nc.sbuf_tensor("c_tri", [128, 4, 512], BF16))
        c_nw = ent(nc.sbuf_tensor("c_nw", [128, NHC], F32))
        LS = Sem(ent(nc.semaphore("LS")))
        LQ = Sem(ent(nc.semaphore("LQ")))
        TS_ = Sem(ent(nc.semaphore("TSm")))
        VS = Sem(ent(nc.semaphore("VSm")))
        SS = Sem(ent(nc.semaphore("SSm")))

        stQY = ExitStack()
        QY = stQY.enter_context(nc.sbuf_tensor("QY", [128, NH * T], BF16))

        with ExitStack() as stKV:
            eKV = stKV.enter_context
            K4 = eKV(nc.sbuf_tensor("K4", [128, NKV * T], BF16))
            Vtm = eKV(nc.sbuf_tensor("Vtm", [128, 16 * 512], BF16))

            sy, gp, te, ve, sc = EP(), EP(), EP(), EP(), EP()
            base = LQ.n
            cast_done = [base + 16 * (s + 1) for s in range(16)]
            wv_done = base + 16 * 17
            wqk_done = [wv_done + 16 * (pi + 1) for pi in range(20)]
            order = list(range(16, 20)) + list(range(16))

            with ExitStack() as stXT:
                XT = stXT.enter_context(nc.sbuf_tensor("XT", [128, NDC * T], BF16))
                with ExitStack() as s4:
                    e4 = s4.enter_context
                    xsl = [e4(nc.sbuf_tensor(f"xsl{i}", [128, D], F32))
                           for i in range(2)]
                    for dst, src in [(c_id32[:], id32), (c_o128[:], ones128),
                                     (c_o1f[:], ones1f), (c_ocb[:], onescb),
                                     (c_tri[:], tri), (c_nw[:], nwt)]:
                        sy.dma(dst, src, LS)
                    ld_done = []
                    for s in range(16):
                        if s >= 2:
                            sy.wait(LQ, cast_done[s - 2])
                        sy.dma(xsl[s % 2][:], x[s * 128:(s + 1) * 128, :], LS)
                        ld_done.append(LS.n)
                    sy.wait(LQ, cast_done[15])
                    for c in range(NDC):
                        sy.dmat(XT[:, c * T:(c + 1) * T],
                                xbf[:, c * 128:(c + 1) * 128], LS)
                    xt_done = LS.n
                wqkb = [stXT.enter_context(
                    nc.sbuf_tensor(f"wqkb{i}", [128, NDC, 128], BF16))
                    for i in range(3)]
                pqk = [stXT.enter_context(nc.psum_tensor(f"pqk{i}", [128, 512], F32))
                       for i in range(2)]
                pv = [stXT.enter_context(nc.psum_tensor(f"pv{i}", [128, 512], F32))
                      for i in range(2)]
                ev_done, mm_done, rope_done = {}, {}, {}
                with ExitStack() as sV:
                    wvsb = sV.enter_context(
                        nc.sbuf_tensor("wvsb", [128, NDC, 512], BF16))
                    te.wait(LQ, wv_done)
                    te.wait(LS, xt_done)
                    for stg in range(16):
                        bank = pv[stg % 2]
                        if stg >= 2:
                            te.wait(VS, ev_done[("v", stg - 2)])
                        for d in range(NDC):
                            col = d * T + (stg // 4) * 512 + (stg % 4) * 128
                            te.mm(bank[:], XT[:, col:col + 128], wvsb[:, d, :],
                                  d == 0, d == NDC - 1,
                                  sem=TS_ if d == NDC - 1 else None)
                        mm_done[("v", stg)] = TS_.n
                        ve.wait(TS_, mm_done[("v", stg)])
                        ve.tcopy(Vtm[:, stg * 512:(stg + 1) * 512], bank[:], sem=VS)
                        ev_done[("v", stg)] = VS.n
                scos = stXT.enter_context(nc.sbuf_tensor("scos", [128, T], F32))
                ssin = stXT.enter_context(nc.sbuf_tensor("ssin", [128, T], F32))
                shb = stXT.enter_context(nc.sbuf_tensor("shb", [128, T], BF16))
                t1b = stXT.enter_context(nc.sbuf_tensor("t1b", [128, 1024], F32))
                t2b = stXT.enter_context(nc.sbuf_tensor("t2b", [128, 1024], F32))
                sy.wait(TS_, mm_done[("v", 15)])
                sy.dma(scos[:], cosf, LS)
                sy.dma(ssin[:], sinf, LS)
                tbl_done = LS.n
                for pi, m in enumerate(order):
                    te.wait(LQ, wqk_done[pi])
                    dst = QY if m < 16 else K4
                    dcol = (m if m < 16 else m - 16) * T
                    for tb in range(4):
                        bank = pqk[(pi * 4 + tb) % 2]
                        prev = (pi * 4 + tb) - 2
                        if prev >= 0:
                            te.wait(VS, ev_done[("qk", prev)])
                        for d in range(NDC):
                            te.mm(bank[:], wqkb[pi % 3][:, d, :],
                                  XT[:, d * T + tb * 512: d * T + tb * 512 + 512],
                                  d == 0, d == NDC - 1,
                                  sem=TS_ if d == NDC - 1 else None)
                        mm_done[("qk", pi, tb)] = TS_.n
                        ve.wait(TS_, mm_done[("qk", pi, tb)])
                        if pi == 0 and tb == 0:
                            ve.wait(LS, tbl_done)
                        ve.tcopy(dst[:, dcol + tb * 512: dcol + tb * 512 + 512],
                                 bank[:], sem=VS)
                        ev_done[("qk", pi * 4 + tb)] = VS.n
                    sy.wait(VS, ev_done[("qk", pi * 4 + 3)])
                    if pi >= 1:
                        sy.wait(VS, rope_done[order[pi - 1]])
                    sy.dma(shb[0:64, :], dst[64:128, dcol:dcol + T], LS)
                    sy.dma(shb[64:128, :], dst[0:64, dcol:dcol + T], LS)
                    ve.wait(LS, LS.n)
                    for hh in range(2):
                        lo, hi = hh * 1024, (hh + 1) * 1024
                        ve.tt(t1b[:], dst[:, dcol + lo: dcol + hi],
                              scos[:, lo:hi], ALU.mult)
                        ve.tt(t2b[:], shb[:, lo:hi], ssin[:, lo:hi], ALU.mult)
                        ve.tt(dst[:, dcol + lo: dcol + hi], t1b[:], t2b[:],
                              ALU.add, sem=VS if hh == 1 else None)
                    rope_done[m] = VS.n
                for si in range(16):
                    gp.wait(LS, ld_done[si])
                    gp.dma(xbf[si * 128:(si + 1) * 128, :], xsl[si % 2][:], LQ)
                    assert LQ.n == cast_done[si]
                gp.dma(wvsb[:], wv, LQ)
                assert LQ.n == wv_done
                for pi, m in enumerate(order):
                    if pi >= 3:
                        gp.wait(TS_, mm_done[("qk", pi - 3, 3)])
                    gp.dma(wqkb[pi % 3][:], wqk[m], LQ)
                    assert LQ.n == wqk_done[pi]
                qk_ev_last = ev_done[("qk", 79)]

            with ExitStack() as s4:
                e4 = s4.enter_context
                PT = e4(nc.sbuf_tensor("PT", [128, 16 * 512], BF16))
                invl = e4(nc.sbuf_tensor("invl", [128, 512], F32))
                lntmp = e4(nc.sbuf_tensor("lntmp", [128, 512], F32))
                ps = [e4(nc.psum_tensor(f"ps{i}", [128, 512], F32))
                      for i in range(4)]
                pl = e4(nc.psum_tensor("pl", [128, 512], F32))
                po = e4(nc.psum_tensor("po", [128, 512], F32))
                bank_free = {0: 0, 1: 0, 2: 0, 3: 0}
                oevac = 0
                first_st = [True]
                for h in range(NH):
                    g = h // 4
                    for jb in range(4):
                        nsc = 4 * (jb + 1)
                        q0 = h * T + jb * 512
                        exp_done, mask_done = [], {}

                        def emit_lav(c):
                            te.wait(SS, exp_done[c])
                            if c in mask_done:
                                te.wait(VS, mask_done[c])
                            if c == 0:
                                te.wait(VS, oevac)
                            te.mm(pl[:], c_o128[:], PT[:, c * 512:(c + 1) * 512],
                                  c == 0, c == nsc - 1,
                                  sem=TS_ if c == nsc - 1 else None)
                            te.mm(po[:],
                                  Vtm[:, c * 512 + g * 128: c * 512 + g * 128 + 128],
                                  PT[:, c * 512:(c + 1) * 512],
                                  c == 0, c == nsc - 1,
                                  sem=TS_ if c == nsc - 1 else None)

                        for c in range(nsc):
                            bk = c % 4
                            te.wait(SS, bank_free[bk])
                            if first_st[0]:
                                te.wait(VS, qk_ev_last)
                                first_st[0] = False
                            if c == 0:
                                te.wait(VS, rope_done[h])
                                te.wait(VS, rope_done[16 + g])
                            te.mm(ps[bk][:],
                                  K4[:, g * T + c * 128: g * T + c * 128 + 128],
                                  QY[:, q0:q0 + 512], True, True, sem=TS_)
                            sc.wait(TS_, TS_.n)
                            sc.act(PT[:, c * 512:(c + 1) * 512], ps[bk][:],
                                   AF.Exp, bias=0.0, scale=SCALE, sem=SS)
                            exp_done.append(SS.n)
                            bank_free[bk] = SS.n
                            if c >= 4 * jb:
                                ve.wait(SS, exp_done[c])
                                ve.tt(PT[:, c * 512:(c + 1) * 512],
                                      PT[:, c * 512:(c + 1) * 512],
                                      c_tri[:, c - 4 * jb, :], ALU.mult, sem=VS)
                                mask_done[c] = VS.n
                            if c >= 4:
                                emit_lav(c - 4)
                        for c in range(max(0, nsc - 4), nsc):
                            emit_lav(c)
                        to = TS_.n
                        tl = TS_.n - 1
                        sc.wait(TS_, tl)
                        sc.act(lntmp[:], pl[:], AF.Ln)
                        sc.act(invl[:], lntmp[:], AF.Exp, bias=0.0, scale=-1.0,
                               sem=SS)
                        inv_d = SS.n
                        ve.wait(SS, inv_d)
                        ve.wait(TS_, to)
                        ve.tt(QY[:, q0:q0 + 512], po[:], invl[:], ALU.mult, sem=VS)
                        oevac = VS.n
                for nm, tsr in [("y", QY), ("qy", QY), ("k4", K4), ("vtm", Vtm)]:
                    if nm in taps:
                        sy.wait(VS, VS.n)
                        sy.dma(taps[nm], tsr[:], LS)
                if taps:
                    sy.wait(LS, LS.n)
                run_block(nc, {"sync": sy, "gpsimd": gp, "tensor": te,
                               "vector": ve, "scalar": sc})

        x1st_done = [0] * 16
        xqst_done = [0] * 16
        zadd_v = [0] * 16
        with ExitStack() as s4:
            e4 = s4.enter_context
            afm = e4(nc.sbuf_tensor("afm", [128, NDC * 512], F32))
            Ztm = e4(nc.sbuf_tensor("Ztm", [128, 4 * D], F32))
            wob = [e4(nc.sbuf_tensor(f"wob{i}", [128, NDC, 128], BF16))
                   for i in range(2)]
            xslP = [e4(nc.sbuf_tensor(f"xslP{i}", [128, D], F32)) for i in range(2)]
            x1sl = [e4(nc.sbuf_tensor(f"x1sl{i}", [128, D], F32)) for i in range(2)]
            k8sl = [e4(nc.sbuf_tensor(f"k8sl{i}", [128, D], I8)) for i in range(2)]
            xqsl = [e4(nc.sbuf_tensor(f"xqsl{i}", [128, D], BF16)) for i in range(2)]
            sqscr = e4(nc.sbuf_tensor("sqscr", [128, D], F32))
            stt = e4(nc.sbuf_tensor("stt", [128, 48], F32))
            pw = [e4(nc.psum_tensor(f"pw{i}", [128, 512], F32)) for i in range(2)]
            ptr = [e4(nc.psum_tensor(f"ptr{i}", [128, 128], F32)) for i in range(2)]
            sy, gp, te, ve, sc = EP(), EP(), EP(), EP(), EP()
            lq0 = LQ.n
            wo_done = [lq0 + 16 * (g + 1) for g in range(4 * NDC)]
            wo_mm, afm_ev, tr_done, ztr_ev = [], [], {}, {}
            sq_ss = [0] * 16
            xq_v = [0] * 16
            for tblk in range(4):
                for mc in range(NDC):
                    mcg = tblk * NDC + mc
                    te.wait(LQ, wo_done[mcg])
                    if mcg >= 2:
                        te.wait(SS, afm_ev[mcg - 2])
                    for d in range(NDC):
                        te.mm(pw[mcg % 2][:], wob[mcg % 2][:, d, :],
                              QY[:, d * T + tblk * 512: d * T + tblk * 512 + 512],
                              d == 0, d == NDC - 1,
                              sem=TS_ if d == NDC - 1 else None)
                    wo_mm.append(TS_.n)
                    sc.wait(TS_, wo_mm[mcg])
                    if tblk > 0 and mc == 0:
                        sc.wait(TS_, tr_done[(tblk - 1) * 64 + 63])
                    sc.scopy(afm[:, mc * 512:(mc + 1) * 512], pw[mcg % 2][:], sem=SS)
                    afm_ev.append(SS.n)
                for mc in range(NDC):
                    mcg = tblk * NDC + mc
                    for ts_i in range(4):
                        idx = tblk * 64 + mc * 4 + ts_i
                        te.wait(SS, afm_ev[mcg])
                        if idx >= 2:
                            te.wait(SS, ztr_ev[idx - 2])
                        te.tr(ptr[idx % 2][:],
                              afm[:, mc * 512 + ts_i * 128: mc * 512 + ts_i * 128 + 128],
                              c_id32[:], sem=TS_)
                        tr_done[idx] = TS_.n
                        sc.wait(TS_, tr_done[idx])
                        if idx == tblk * 64 and tblk > 0:
                            sc.wait(VS, xq_v[tblk * 4 - 1])
                        sc.scopy(Ztm[:, ts_i * D + mc * 128: ts_i * D + mc * 128 + 128],
                                 ptr[idx % 2][:], sem=SS)
                        ztr_ev[idx] = SS.n
                mset = [0] * 4
                for ts_i in range(4):
                    gidx = tblk * 4 + ts_i
                    row0 = gidx * 128
                    Z = Ztm[:, ts_i * D:(ts_i + 1) * D]
                    if gidx >= 2:
                        sy.wait(VS, zadd_v[gidx - 2])
                    sy.dma(xslP[ts_i % 2][:], x[row0:row0 + 128, :], LS)
                    xld = LS.n
                    ve.wait(LS, xld)
                    ve.wait(SS, ztr_ev[tblk * 64 + 60 + ts_i])
                    ve.tt(Z, Z, xslP[ts_i % 2][:], ALU.add, sem=VS)
                    zadd_v[gidx] = VS.n
                    ve.rsum(stt[:, ts_i:ts_i + 1], Z)
                    ve.memset(stt[:, 4 + ts_i:5 + ts_i], 0.0, sem=VS)
                    mset[ts_i] = VS.n
                    sc.wait(VS, mset[ts_i])
                    sc.act(sqscr[:], Z, AF.Square, accum=stt[:, 4 + ts_i:5 + ts_i],
                           sem=SS)
                    sq_ss[tblk * 4 + ts_i] = SS.n
                ve.wait(SS, sq_ss[tblk * 4 + 3])
                ve.ts(stt[:, 8:12], stt[:, 0:4], -1.0 / D, None, ALU.mult)
                ve.ts(stt[:, 12:16], stt[:, 4:8], 1.0 / D, EPS, ALU.mult, ALU.add)
                ve.drain()
                ve.tt(stt[:, 16:20], stt[:, 8:12], stt[:, 8:12], ALU.mult)
                ve.drain()
                ve.tt(stt[:, 20:24], stt[:, 12:16], stt[:, 16:20], ALU.subtract)
                ve.drain()
                ve.recip(stt[:, 24:28], stt[:, 20:24], sem=VS)
                rvd = VS.n
                sc.wait(VS, rvd)
                sc.act(stt[:, 28:32], stt[:, 24:28], AF.Sqrt, sem=SS)
                rstd_d = SS.n
                ve.wait(SS, rstd_d)
                ve.tt(stt[:, 32:36], stt[:, 8:12], stt[:, 28:32], ALU.mult)
                ve.drain()
                for ts_i in range(4):
                    Z = Ztm[:, ts_i * D:(ts_i + 1) * D]
                    ve.ts(sqscr[:], Z, 1.0, stt[:, 8 + ts_i:9 + ts_i],
                          ALU.mult, ALU.add)
                    ve.rmax(stt[:, 36 + ts_i:37 + ts_i], sqscr[:], absval=True)
                ve.drain()
                ve.tt(stt[:, 40:44], stt[:, 36:40], stt[:, 28:32], ALU.mult)
                ve.drain()
                ve.ts(stt[:, 40:44], stt[:, 40:44], EPS, None, ALU.max)
                ve.drain()
                ve.recip(stt[:, 36:40], stt[:, 40:44])
                ve.drain()
                ve.ts(stt[:, 44:48], stt[:, 36:40], 127.0, None, ALU.mult)
                ve.ts(stt[:, 36:40], stt[:, 40:44], 1.0 / 127.0, None, ALU.mult)
                ve.drain()
                for ts_i in range(4):
                    gidx = tblk * 4 + ts_i
                    row0 = gidx * 128
                    Z = Ztm[:, ts_i * D:(ts_i + 1) * D]
                    if gidx >= 2:
                        ve.wait(LS, x1st_done[gidx - 2])
                        ve.wait(LS, xqst_done[gidx - 2])
                    ve.ts(x1sl[ts_i % 2][:], Z, stt[:, 28 + ts_i:29 + ts_i],
                          stt[:, 32 + ts_i:33 + ts_i], ALU.mult, ALU.add, sem=VS)
                    x1d = VS.n
                    sy.wait(VS, x1d)
                    sy.dma(x1sp[row0:row0 + 128, :], x1sl[ts_i % 2][:], LS)
                    x1st_done[gidx] = LS.n
                    ve.ts(k8sl[ts_i % 2][:], x1sl[ts_i % 2][:],
                          stt[:, 44 + ts_i:45 + ts_i], None, ALU.mult)
                    ve.ts(xqsl[ts_i % 2][:], k8sl[ts_i % 2][:],
                          stt[:, 36 + ts_i:37 + ts_i], None, ALU.mult, sem=VS)
                    xqd = VS.n
                    xq_v[gidx] = xqd
                    sy.wait(VS, xqd)
                    sy.dma(xqsp[row0:row0 + 128, :], xqsl[ts_i % 2][:], LS)
                    xqst_done[gidx] = LS.n
            gp2 = EP()
            for tblk in range(4):
                for mc in range(NDC):
                    mcg = tblk * NDC + mc
                    if mcg >= 2:
                        gp2.wait(TS_, wo_mm[mcg - 2])
                    gp2.dma(wob[mcg % 2][:], wo[mc], LQ)
                    assert LQ.n == wo_done[mcg]
            gp = gp2
            if "x1" in taps:
                sy.wait(LS, LS.n)
                sy.dma(taps["x1"], x1sp, LS)
            if "xq" in taps:
                sy.wait(LS, LS.n)
                sy.dma(taps["xq"], xqsp, LS)
                sy.wait(LS, LS.n)
            run_block(nc, {"sync": sy, "gpsimd": gp, "tensor": te,
                           "vector": ve, "scalar": sc})
        stQY.close()

        with ExitStack() as s4:
            e4 = s4.enter_context
            XqTb = e4(nc.sbuf_tensor("XqTb", [128, NDC * 512], BF16))
            GN = e4(nc.sbuf_tensor("GN", [128, NHC * 512 + D], BF16))
            w13b = [e4(nc.sbuf_tensor(f"w13b{i}", [128, 2, NDC, 128], BF16))
                    for i in range(2)]
            w2b = [e4(nc.sbuf_tensor(f"w2b{i}", [128, NHC, 128], BF16))
                   for i in range(2)]
            rbuf = e4(nc.sbuf_tensor("rb4", [128, 512], F32))
            r2buf = e4(nc.sbuf_tensor("r2b4", [128, 512], F32))
            gbuf = e4(nc.sbuf_tensor("gb4", [128, 512], F32))
            gsqb = [e4(nc.sbuf_tensor(f"gsq4_{i}", [128, 512], BF16))
                    for i in range(2)]
            gnsqb = e4(nc.sbuf_tensor("gnsq4", [128, 512], F32))
            runmax = e4(nc.sbuf_tensor("rmx4", [128, 512], F32))
            m2col = e4(nc.sbuf_tensor("m2c4", [128, 4], F32))
            rows4 = e4(nc.sbuf_tensor("r4_4", [4, 128], F32))
            rowscr = e4(nc.sbuf_tensor("rsc4", [1, 2048], F32))
            RSb = e4(nc.sbuf_tensor("RSb4", [128, 1024], BF16))
            iS2b = e4(nc.sbuf_tensor("iS2b4", [128, 512], F32))
            k8b = e4(nc.sbuf_tensor("k8b4", [128, 1024], I8))
            z2fm = [e4(nc.sbuf_tensor(f"z2f4_{i}", [128, 512], F32))
                    for i in range(2)]
            Ztm2 = e4(nc.sbuf_tensor("Zt2_4", [128, 4 * D], F32))
            x1r = [e4(nc.sbuf_tensor(f"x1r4_{i}", [128, D], F32)) for i in range(2)]
            outsl = [e4(nc.sbuf_tensor(f"osl4_{i}", [128, D], F32))
                     for i in range(2)]
            st2t = e4(nc.sbuf_tensor("st2_4", [128, 48], F32))
            pg = [e4(nc.psum_tensor(f"pg4_{i}", [128, 512], F32)) for i in range(4)]
            pstat = e4(nc.psum_tensor("pst4", [1, 512], F32))
            ptrx = [e4(nc.psum_tensor(f"ptx4_{i}", [128, 128], F32))
                    for i in range(2)]
            pz = e4(nc.psum_tensor("pz4", [128, 512], F32))
            sy, gp, te, ve, sc = EP(), EP(), EP(), EP(), EP()

            lq0 = LQ.n
            w13_done = [[lq0 + 16 * (tb * (NHC + NDC) + j + 1) for j in range(NHC)]
                        for tb in range(4)]
            w2_done = [[lq0 + 16 * (tb * (NHC + NDC) + NHC + mc + 1)
                        for mc in range(NDC)] for tb in range(4)]
            mm1_done, gate_v, gate_s, gsq_ready = [], [], [], []
            mm2_done, z2fm_ev, trz_ev = [], [], {}
            outst = [0] * 16
            z2add_v = [0] * 16
            prev = {"bc_d": 0, "ssr_d": 0, "trm": 0, "mm2_last": 0, "trz_last": 0}
            xqt_done_arr = [0] * 4
            for tblk in range(4):
                if tblk == 0:
                    sy.wait(LS, xqst_done[3])
                    for c in range(NDC):
                        sy.dmat(XqTb[:, c * 512:(c + 1) * 512],
                                xqsp[0:512, c * 128:(c + 1) * 128], LS)
                    xqt_done_arr[0] = LS.n
                xqt_done = xqt_done_arr[tblk]

                te.wait(VS, prev["bc_d"])
                te.wait(SS, prev["ssr_d"])
                for j in range(NHC):
                    jg = tblk * NHC + j
                    te.wait(LQ, w13_done[tblk][j])
                    if jg >= 2:
                        te.wait(VS, gate_v[jg - 2])
                        te.wait(SS, gate_s[jg - 2])
                    if j == 0:
                        te.wait(LS, xqt_done)
                    for half in range(2):
                        bank = pg[(jg % 2) * 2 + half]
                        for d in range(NDC):
                            te.mm(bank[:], w13b[jg % 2][:, half, d, :],
                                  XqTb[:, d * 512:(d + 1) * 512],
                                  d == 0, d == NDC - 1,
                                  sem=TS_ if d == NDC - 1 else None)
                        mm1_done.append(TS_.n)
                    if j >= 2:
                        te.wait(SS, gsq_ready[jg - 2])
                        te.mm(pstat[:], c_ocb[:], gsqb[(jg - 2) % 2][:],
                              j - 2 == 0, False, sem=None)
                    tx1, tx3 = mm1_done[2 * jg], mm1_done[2 * jg + 1]
                    sc.wait(TS_, tx1)
                    sc.act(rbuf[:], pg[(jg % 2) * 2][:], AF.Relu)
                    sc.act(r2buf[:], rbuf[:], AF.Square, sem=SS)
                    r2d = SS.n
                    ve.wait(SS, r2d)
                    ve.wait(TS_, tx3)
                    if j == 0 and tblk > 0:
                        ve.wait(TS_, prev["mm2_last"])
                        ve.wait(TS_, prev["trm"])
                    ve.tt(gbuf[:], r2buf[:], pg[(jg % 2) * 2 + 1][:], ALU.mult,
                          sem=VS)
                    gbd = VS.n
                    sc.wait(VS, gbd)
                    sc.act(gsqb[jg % 2][:], gbuf[:], AF.Square, sem=SS)
                    gsq_ready.append(SS.n)
                    ve.ts(GN[:, j * 512:(j + 1) * 512], gbuf[:],
                          c_nw[:, j:j + 1], None, ALU.mult, sem=VS)
                    sc.wait(VS, VS.n)
                    sc.act(gnsqb[:], GN[:, j * 512:(j + 1) * 512], AF.Square,
                           sem=SS)
                    ve.wait(SS, SS.n)
                    if j == 0:
                        ve.tcopy(runmax[:], gnsqb[:], sem=VS)
                    else:
                        ve.tt(runmax[:], runmax[:], gnsqb[:], ALU.max, sem=VS)
                    gate_v.append(VS.n)
                    gate_s.append(SS.n)
                te.wait(SS, gsq_ready[tblk * NHC + NHC - 2])
                te.mm(pstat[:], c_ocb[:], gsqb[(tblk * NHC + NHC - 2) % 2][:],
                      False, False, sem=None)
                te.wait(SS, gsq_ready[tblk * NHC + NHC - 1])
                te.mm(pstat[:], c_ocb[:], gsqb[(tblk * NHC + NHC - 1) % 2][:],
                      False, True, sem=TS_)
                tstat = TS_.n
                sc.wait(TS_, tstat)
                sc.scopy(rowscr[:, 0:512], pstat[:], sem=SS)
                ssr_d = SS.n
                prev["ssr_d"] = ssr_d
                ve.wait(SS, ssr_d)
                ve.ts(rowscr[:, 512:1024], rowscr[:, 0:512], 1.0 / HID, EPS,
                      ALU.mult, ALU.add)
                ve.drain()
                ve.recip(rowscr[:, 1024:1536], rowscr[:, 512:1024], sem=VS)
                rv2 = VS.n
                sc.wait(VS, rv2)
                sc.act(rowscr[:, 1536:2048], rowscr[:, 1024:1536], AF.Sqrt, sem=SS)
                rstd2_d = SS.n
                trv = []
                for i in range(4):
                    if i == 0:
                        te.wait(VS, gate_v[tblk * NHC + NHC - 1])
                        if tblk > 0:
                            te.wait(VS, trz_ev[tblk * 64 - 1])
                    if i >= 2:
                        te.wait(VS, trv[i - 2])
                    te.tr(ptrx[i % 2][:], runmax[:, i * 128:(i + 1) * 128],
                          c_id32[:], sem=TS_)
                    ti = TS_.n
                    ve.wait(TS_, ti)
                    ve.rmax(m2col[:, i:i + 1], ptrx[i % 2][:], absval=False,
                            sem=VS)
                    trv.append(VS.n)
                te.wait(VS, trv[3])
                te.tr(ptrx[0][0:4, :], m2col[:], c_id32[:], sem=TS_)
                trm = TS_.n
                prev["trm"] = trm
                ve.wait(TS_, trm)
                ve.tcopy(rows4[:], ptrx[0][0:4, 0:128], sem=VS)
                r4d = VS.n
                sy.wait(VS, r4d)
                sy.dma(rowscr[:, 0:512], rows4[:], LS)
                m2r_d = LS.n
                if tblk < 3:
                    sy.wait(LS, xqst_done[(tblk + 1) * 4 + 3])
                    sy.wait(TS_, mm1_done[2 * (tblk + 1) * NHC - 1])
                    for c in range(NDC):
                        sy.dmat(XqTb[:, c * 512:(c + 1) * 512],
                                xqsp[(tblk + 1) * 512:(tblk + 2) * 512,
                                     c * 128:(c + 1) * 128], LS)
                    xqt_done_arr[tblk + 1] = LS.n
                sc.wait(LS, m2r_d)
                sc.act(rowscr[:, 512:1024], rowscr[:, 0:512], AF.Sqrt, sem=SS)
                sm2_d = SS.n
                ve.wait(SS, sm2_d)
                ve.wait(SS, rstd2_d)
                ve.tt(rowscr[:, 1024:1536], rowscr[:, 512:1024],
                      rowscr[:, 1536:2048], ALU.mult)
                ve.drain()
                ve.ts(rowscr[:, 1024:1536], rowscr[:, 1024:1536], EPS, None,
                      ALU.max)
                ve.drain()
                ve.recip(rowscr[:, 0:512], rowscr[:, 1024:1536])
                ve.drain()
                ve.ts(rowscr[:, 0:512], rowscr[:, 0:512], 127.0, None,
                      ALU.mult)
                ve.drain()
                ve.ts(rowscr[:, 512:1024], rowscr[:, 1024:1536], 1.0 / 127.0,
                      None, ALU.mult)
                ve.drain()
                ve.tt(rowscr[:, 1024:1536], rowscr[:, 1536:2048],
                      rowscr[:, 0:512], ALU.mult, sem=VS)
                rows_d = VS.n
                te.wait(VS, rows_d)
                te.mm(pg[0][:], c_o1f[:], rowscr[:, 1024:1536], True, True,
                      sem=TS_)
                tb1 = TS_.n
                te.mm(pg[1][:], c_o1f[:], rowscr[:, 512:1024], True, True,
                      sem=TS_)
                tb2 = TS_.n
                ve.wait(TS_, tb1)
                ve.tcopy(RSb[:, 0:512], pg[0][:])
                ve.tcopy(RSb[:, 512:1024], pg[0][:])
                ve.wait(TS_, tb2)
                ve.tcopy(iS2b[:], pg[1][:], sem=VS)
                prev["bc_d"] = VS.n
                for jj in range(NHC // 2):
                    sl = slice(jj * 1024, (jj + 1) * 1024)
                    ve.tt(k8b[:], GN[:, sl], RSb[:], ALU.mult)
                    ve.tcopy(GN[:, sl], k8b[:], sem=VS)
                iq_d = VS.n
                if "gn" in taps and tblk == 0:
                    sy.wait(VS, iq_d)
                    sy.dma(taps["gn"], GN[:, 0:NHC * 512], LS)
                for mc in range(NDC):
                    mcg = tblk * NDC + mc
                    te.wait(LQ, w2_done[tblk][mc])
                    if mc == 0:
                        te.wait(VS, iq_d)
                    if mcg >= 1:
                        te.wait(VS, z2fm_ev[mcg - 1])
                    for j in range(NHC):
                        te.mm(pz[:], w2b[mcg % 2][:, j, :],
                              GN[:, j * 512:(j + 1) * 512],
                              j == 0, j == NHC - 1,
                              sem=TS_ if j == NHC - 1 else None)
                    mm2_done.append(TS_.n)
                    ve.wait(TS_, mm2_done[mcg])
                    ve.tt(z2fm[mcg % 2][:], pz[:], iS2b[:], ALU.mult, sem=VS)
                    z2fm_ev.append(VS.n)
                    for ts_i in range(4):
                        idx = tblk * 64 + mc * 4 + ts_i
                        if ts_i == 0:
                            te.wait(VS, z2fm_ev[mcg])
                        if idx >= 2:
                            te.wait(VS, trz_ev[idx - 2])
                        te.tr(ptrx[idx % 2][:],
                              z2fm[mcg % 2][:, ts_i * 128:(ts_i + 1) * 128],
                              c_id32[:], sem=TS_)
                        tdd = TS_.n
                        ve.wait(TS_, tdd)
                        if idx == tblk * 64 and tblk > 0:
                            ve.wait(SS, prev["sq2_last"])
                        ve.tcopy(Ztm2[:, ts_i * D + mc * 128:
                                 ts_i * D + mc * 128 + 128],
                                 ptrx[idx % 2][:], sem=VS)
                        trz_ev[idx] = VS.n
                prev["mm2_last"] = mm2_done[tblk * NDC + NDC - 1]
                ms2 = [0] * 4
                sq2l = [0] * 4
                for ts_i in range(4):
                    gidx = tblk * 4 + ts_i
                    row0 = gidx * 128
                    Z = Ztm2[:, ts_i * D:(ts_i + 1) * D]
                    sy.wait(LS, x1st_done[gidx])
                    if gidx >= 2:
                        sy.wait(VS, z2add_v[gidx - 2])
                    sy.dma(x1r[ts_i % 2][:], x1sp[row0:row0 + 128, :], LS)
                    x1rd = LS.n
                    ve.wait(LS, x1rd)
                    if ts_i == 0:
                        ve.wait(VS, trz_ev[tblk * 64 + 63])
                    ve.tt(Z, Z, x1r[ts_i % 2][:], ALU.add, sem=VS)
                    z2add_v[gidx] = VS.n
                    ve.rsum(st2t[:, ts_i:ts_i + 1], Z)
                    ve.memset(st2t[:, 4 + ts_i:5 + ts_i], 0.0, sem=VS)
                    ms2[ts_i] = VS.n
                    sc.wait(VS, ms2[ts_i])
                    sc.act(GN[:, NHC * 512:NHC * 512 + D], Z, AF.Square,
                           accum=st2t[:, 4 + ts_i:5 + ts_i], sem=SS)
                    sq2l[ts_i] = SS.n
                    prev["sq2_last"] = SS.n
                ve.wait(SS, sq2l[3])
                ve.ts(st2t[:, 8:12], st2t[:, 0:4], -1.0 / D, None, ALU.mult)
                ve.ts(st2t[:, 12:16], st2t[:, 4:8], 1.0 / D, EPS, ALU.mult,
                      ALU.add)
                ve.drain()
                ve.tt(st2t[:, 16:20], st2t[:, 8:12], st2t[:, 8:12], ALU.mult)
                ve.drain()
                ve.tt(st2t[:, 20:24], st2t[:, 12:16], st2t[:, 16:20],
                      ALU.subtract)
                ve.drain()
                ve.recip(st2t[:, 24:28], st2t[:, 20:24], sem=VS)
                rv3 = VS.n
                sc.wait(VS, rv3)
                sc.act(st2t[:, 28:32], st2t[:, 24:28], AF.Sqrt, sem=SS)
                r2d2 = SS.n
                ve.wait(SS, r2d2)
                ve.tt(st2t[:, 32:36], st2t[:, 8:12], st2t[:, 28:32], ALU.mult)
                ve.drain()
                for ts_i in range(4):
                    gidx = tblk * 4 + ts_i
                    row0 = gidx * 128
                    Z = Ztm2[:, ts_i * D:(ts_i + 1) * D]
                    if gidx >= 2:
                        ve.wait(LS, outst[gidx - 2])
                    ve.ts(outsl[ts_i % 2][:], Z, st2t[:, 28 + ts_i:29 + ts_i],
                          st2t[:, 32 + ts_i:33 + ts_i], ALU.mult, ALU.add,
                          sem=VS)
                    od = VS.n
                    sy.wait(VS, od)
                    sy.dma(out[row0:row0 + 128, :], outsl[ts_i % 2][:], LS)
                    outst[gidx] = LS.n
            sy.wait(LS, LS.n)
            gp2 = EP()
            for tblk in range(4):
                for j in range(NHC):
                    jg = tblk * NHC + j
                    if jg >= 2:
                        gp2.wait(TS_, mm1_done[2 * (jg - 2) + 1])
                    gp2.dma(w13b[jg % 2][:], w13[j], LQ)
                    assert LQ.n == w13_done[tblk][j]
                for mc in range(NDC):
                    mcg = tblk * NDC + mc
                    if mcg >= 2:
                        gp2.wait(TS_, mm2_done[mcg - 2])
                    gp2.dma(w2b[mcg % 2][:], w2[mc], LQ)
                    assert LQ.n == w2_done[tblk][mc]
            gp = gp2
            run_block(nc, {"sync": sy, "gpsimd": gp, "tensor": te,
                           "vector": ve, "scalar": sc})

    return nc


def _host_layouts(Wq, Wk, Wv, Wo, w13, w2, norm_w):
    f32 = np.float32
    wqk_src = np.concatenate([np.asarray(Wq, f32), np.asarray(Wk, f32)], axis=1)
    wqk = np.ascontiguousarray(
        wqk_src.reshape(NDC, 128, 20, 128).transpose(2, 1, 0, 3))
    wvv = np.ascontiguousarray(
        np.asarray(Wv, f32).reshape(NDC, 128, 512).transpose(1, 0, 2))
    woo = np.ascontiguousarray(
        np.asarray(Wo, f32).reshape(NDC, 128, NDC, 128).transpose(2, 1, 0, 3))
    w13a = np.asarray(w13, f32).T.reshape(NDC, 128, 88, 128).transpose(2, 1, 0, 3)
    w13t = np.ascontiguousarray(
        w13a.reshape(2, NHC, 128, NDC, 128).transpose(1, 2, 0, 3, 4))
    w2t = np.ascontiguousarray(
        np.asarray(w2, f32).T.reshape(NHC, 128, NDC, 128).transpose(2, 1, 0, 3))
    nwt = np.ascontiguousarray(np.asarray(norm_w, f32).reshape(NHC, 128).T)

    i = np.arange(64, dtype=f32)
    e = (f32(-2.0) * i) / f32(128.0)
    inv = np.power(f32(10000.0), e, dtype=f32)
    tt = np.arange(T, dtype=f32)
    ang = (tt[:, None] * inv[None, :]).astype(f32)
    cosa = np.cos(ang.astype(np.float64)).astype(f32)
    sina = np.sin(ang.astype(np.float64)).astype(f32)
    cost = np.tile(cosa, (1, 2))
    sins = np.tile(sina, (1, 2))
    sins[:, :64] *= -1.0
    cosf = np.ascontiguousarray(cost.T)
    sinf = np.ascontiguousarray(sins.T)

    id32 = np.eye(128, dtype=f32)
    ones128 = np.ones((128, 128), ml_dtypes.bfloat16)
    ones1f = np.ones((1, 128), f32)
    onescb = np.ones((128, 1), ml_dtypes.bfloat16)
    q_idx = np.arange(512)
    tri = np.zeros((128, 4, 512), np.float32)
    for k in range(4):
        s_idx = np.arange(128)[:, None] + k * 128
        tri[:, k, :] = (q_idx[None, :] >= s_idx).astype(np.float32)
    tri = tri.astype(ml_dtypes.bfloat16)
    return dict(wqk=wqk, wv=wvv, wo=woo, w13=w13t, w2=w2t, nwt=nwt,
                cosf=cosf, sinf=sinf, id32=id32, ones128=ones128,
                ones1f=ones1f, onescb=onescb, tri=tri)


def kernel(L0_attn_x, L0_attn_Wq, L0_attn_Wk, L0_attn_Wv, L0_attn_Wo,
           L0_ffn_w13, L0_ffn_w2, L0_ffn_norm_w, _debug_taps=(), _nc=None):
    xin = np.ascontiguousarray(np.asarray(L0_attn_x, np.float32))
    consts = _host_layouts(L0_attn_Wq, L0_attn_Wk, L0_attn_Wv, L0_attn_Wo,
                           L0_ffn_w13, L0_ffn_w2, L0_ffn_norm_w)
    nc = _nc if _nc is not None else build(_debug_taps)
    in_maps = []
    for b in range(8):
        m = dict(consts)
        m["x"] = np.ascontiguousarray(xin[b])
        in_maps.append(m)
    res = run_bass_kernel_spmd(nc, in_maps, core_ids=list(range(8)))
    outs = [np.asarray(r["out"], np.float32) for r in res.results]
    if _debug_taps:
        return np.stack(outs, 0), res
    return np.stack(outs, 0)


# version 08
# speedup vs baseline: 1.0671x; 1.0131x over previous
import numpy as np
import ml_dtypes
import concourse.bass as bass
import concourse.mybir as mybir
from concourse.bass_utils import run_bass_kernel_spmd
from contextlib import ExitStack

F32 = mybir.dt.float32
BF16 = mybir.dt.bfloat16
I8 = mybir.dt.int8
AF = mybir.ActivationFunctionType
ALU = mybir.AluOpType
AX = mybir.AxisListType

T = 2048
D = 2048
NH = 16
NKV = 4
HID = 5632
NDC = 16
NHC = 44
SCALE = 1.0 / float(np.float32(np.sqrt(np.float32(128.0))))
EPS = 1e-5


class Sem:
    def __init__(self, h):
        self.h = h
        self.n = 0

    def inc(self, k):
        self.n += k
        return self.n


class EP:

    def __init__(self):
        self.ops = []

    def wait(self, sem, val):
        if val > 0:
            self.ops.append(("wait", sem.h, val))

    def _p(self, *a, sem=None, inc=0):
        if sem is not None:
            sem.inc(inc)
            self.ops.append(a + (sem.h, inc))
        else:
            self.ops.append(a + (None, 0))

    def dma(self, out, in_, sem):
        self._p("dma", out, in_, sem=sem, inc=16)

    def dmat(self, out, in_, sem):
        self._p("dmat", out, in_, sem=sem, inc=16)

    def mm(self, out, lhsT, rhs, start, stop, sem=None):
        self._p("mm", out, lhsT, rhs, start, stop, sem=sem, inc=1 if sem else 0)

    def tr(self, out, in_, ident, sem=None):
        self._p("tr", out, in_, ident, sem=sem, inc=1 if sem else 0)

    def act(self, out, in_, func, bias=0.0, scale=1.0, accum=None, sem=None):
        self._p("act", out, in_, func, bias, scale, accum, sem=sem,
                inc=1 if sem else 0)

    def scopy(self, out, in_, sem=None):
        self._p("scopy", out, in_, sem=sem, inc=1 if sem else 0)

    def tt(self, out, a, b, op, sem=None):
        self._p("tt", out, a, b, op, sem=sem, inc=1 if sem else 0)

    def ts(self, out, in0, s1, s2, op0, op1=None, sem=None):
        self._p("ts", out, in0, s1, s2, op0, op1, sem=sem, inc=1 if sem else 0)

    def tcopy(self, out, in_, sem=None):
        self._p("tcopy", out, in_, sem=sem, inc=1 if sem else 0)

    def recip(self, out, in_, sem=None):
        self._p("recip", out, in_, sem=sem, inc=1 if sem else 0)

    def rsum(self, out, in_, sem=None):
        self._p("rsum", out, in_, sem=sem, inc=1 if sem else 0)

    def rmax(self, out, in_, absval=False, sem=None):
        self._p("rmax", out, in_, absval, sem=sem, inc=1 if sem else 0)

    def memset(self, ap, v, sem=None):
        self._p("memset", ap, v, sem=sem, inc=1 if sem else 0)

    def drain(self):
        self.ops.append(("drain", None, 0))


def _replay(eng, ops):
    for op in ops:
        k = op[0]
        if k == "wait":
            eng.wait_ge(op[1], op[2])
            continue
        if k == "drain":
            eng.drain()
            continue
        sem, inc = op[-2], op[-1]
        if k == "dma":
            ins = eng.dma_start(out=op[1], in_=op[2])
        elif k == "dmat":
            ins = eng.dma_start_transpose(op[1], op[2])
        elif k == "mm":
            ins = eng.matmul(op[1], op[2], op[3], start=op[4], stop=op[5])
        elif k == "tr":
            ins = eng.transpose(op[1], op[2], op[3])
        elif k == "act":
            ins = eng.activation(op[1], op[2], op[3], bias=op[4], scale=op[5],
                                 accum_out=op[6])
        elif k == "scopy":
            ins = eng.copy(op[1], op[2])
        elif k == "tt":
            ins = eng.tensor_tensor(op[1], op[2], op[3], op[4])
        elif k == "ts":
            if op[6] is None:
                ins = eng.tensor_scalar(op[1], op[2], op[3], None, op[5])
            else:
                ins = eng.tensor_scalar(op[1], op[2], op[3], op[4], op[5], op[6])
        elif k == "tcopy":
            ins = eng.tensor_copy(op[1], op[2])
        elif k == "recip":
            ins = eng.reciprocal(op[1], op[2])
        elif k == "rsum":
            ins = eng.reduce_sum(op[1], op[2], axis=AX.X)
        elif k == "rmax":
            ins = eng.reduce_max(op[1], op[2], axis=AX.X,
                                 apply_absolute_value=op[3])
        elif k == "memset":
            ins = eng.memset(op[1], op[2])
        else:
            raise ValueError(k)
        if sem is not None:
            ins.then_inc(sem, inc)


def run_block(nc, plans):
    with nc.Block() as blk:
        for name, ep in plans.items():
            if not ep.ops:
                continue
            getattr(blk, name)(lambda eng, _ops=tuple(ep.ops): _replay(eng, _ops))


def build(debug_taps=()):
    nc = bass.Bass("TRN2", target_bir_lowering=False)
    dt = nc.dram_tensor
    x = dt("x", [T, D], F32, kind="ExternalInput").ap()
    wqk = dt("wqk", [20, 128, NDC, 128], F32, kind="ExternalInput").ap()
    wv = dt("wv", [128, NDC, 512], F32, kind="ExternalInput").ap()
    wo = dt("wo", [NDC, 128, NDC, 128], F32, kind="ExternalInput").ap()
    w13 = dt("w13", [NHC, 128, 2, NDC, 128], F32, kind="ExternalInput").ap()
    w2 = dt("w2", [NDC, 128, NHC, 128], F32, kind="ExternalInput").ap()
    nwt = dt("nwt", [128, NHC], F32, kind="ExternalInput").ap()
    cosf = dt("cosf", [128, T], F32, kind="ExternalInput").ap()
    sinf = dt("sinf", [128, T], F32, kind="ExternalInput").ap()
    id32 = dt("id32", [128, 128], F32, kind="ExternalInput").ap()
    ones128 = dt("ones128", [128, 128], BF16, kind="ExternalInput").ap()
    ones1f = dt("ones1f", [1, 128], F32, kind="ExternalInput").ap()
    onescb = dt("onescb", [128, 1], BF16, kind="ExternalInput").ap()
    tri = dt("tri", [128, 4, 512], BF16, kind="ExternalInput").ap()
    out = dt("out", [T, D], F32, kind="ExternalOutput").ap()

    xbf = dt("xbf", [T, D], BF16, kind="Internal").ap()
    x1sp = dt("x1sp", [T, D], F32, kind="Internal").ap()
    xqsp = dt("xqsp", [T, D], BF16, kind="Internal").ap()

    taps = {}
    for tap in debug_taps:
        shp = {"qy": [128, NH * T], "k4": [128, NKV * T], "vtm": [128, 16 * 512],
               "y": [128, NH * T], "x1": [T, D], "xq": [T, D],
               "gn": [128, NHC * 512], "afm0": [128, NDC * 512],
               "z0": [128, 4 * D]}[tap]
        dtp = {"x1": F32, "afm0": F32, "z0": F32}.get(tap, BF16)
        taps[tap] = dt("tap_" + tap, shp, dtp, kind="ExternalOutput").ap()

    with ExitStack() as st:
        ent = st.enter_context
        c_id32 = ent(nc.sbuf_tensor("c_id32", [128, 128], F32))
        c_o128 = ent(nc.sbuf_tensor("c_o128", [128, 128], BF16))
        c_o1f = ent(nc.sbuf_tensor("c_o1f", [1, 128], F32))
        c_ocb = ent(nc.sbuf_tensor("c_ocb", [128, 1], BF16))
        c_tri = ent(nc.sbuf_tensor("c_tri", [128, 4, 512], BF16))
        c_nw = ent(nc.sbuf_tensor("c_nw", [128, NHC], F32))
        LS = Sem(ent(nc.semaphore("LS")))
        LQ = Sem(ent(nc.semaphore("LQ")))
        TS_ = Sem(ent(nc.semaphore("TSm")))
        VS = Sem(ent(nc.semaphore("VSm")))
        SS = Sem(ent(nc.semaphore("SSm")))

        stQY = ExitStack()
        QY = stQY.enter_context(nc.sbuf_tensor("QY", [128, NH * T], BF16))

        with ExitStack() as stKV:
            eKV = stKV.enter_context
            K4 = eKV(nc.sbuf_tensor("K4", [128, NKV * T], BF16))
            Vtm = eKV(nc.sbuf_tensor("Vtm", [128, 16 * 512], BF16))

            sy, gp, te, ve, sc = EP(), EP(), EP(), EP(), EP()
            base = LQ.n
            cast_done = [base + 16 * (s + 1) for s in range(16)]
            wv_done = base + 16 * 17
            wqk_done = [wv_done + 16 * (pi + 1) for pi in range(20)]
            order = list(range(16, 20)) + list(range(16))

            with ExitStack() as stXT:
                XT = stXT.enter_context(nc.sbuf_tensor("XT", [128, NDC * T], BF16))
                with ExitStack() as s4:
                    e4 = s4.enter_context
                    xsl = [e4(nc.sbuf_tensor(f"xsl{i}", [128, D], F32))
                           for i in range(2)]
                    for dst, src in [(c_id32[:], id32), (c_o128[:], ones128),
                                     (c_o1f[:], ones1f), (c_ocb[:], onescb),
                                     (c_tri[:], tri), (c_nw[:], nwt)]:
                        sy.dma(dst, src, LS)
                    ld_done = []
                    for s in range(16):
                        if s >= 2:
                            sy.wait(LQ, cast_done[s - 2])
                        sy.dma(xsl[s % 2][:], x[s * 128:(s + 1) * 128, :], LS)
                        ld_done.append(LS.n)
                    sy.wait(LQ, cast_done[15])
                    for c in range(NDC):
                        sy.dmat(XT[:, c * T:(c + 1) * T],
                                xbf[:, c * 128:(c + 1) * 128], LS)
                    xt_done = LS.n
                wqkb = [stXT.enter_context(
                    nc.sbuf_tensor(f"wqkb{i}", [128, NDC, 128], BF16))
                    for i in range(3)]
                pqk = [stXT.enter_context(nc.psum_tensor(f"pqk{i}", [128, 512], F32))
                       for i in range(2)]
                pv = [stXT.enter_context(nc.psum_tensor(f"pv{i}", [128, 512], F32))
                      for i in range(2)]
                ev_done, mm_done, rope_done = {}, {}, {}
                with ExitStack() as sV:
                    wvsb = sV.enter_context(
                        nc.sbuf_tensor("wvsb", [128, NDC, 512], BF16))
                    te.wait(LQ, wv_done)
                    te.wait(LS, xt_done)
                    for stg in range(16):
                        bank = pv[stg % 2]
                        if stg >= 2:
                            te.wait(VS, ev_done[("v", stg - 2)])
                        for d in range(NDC):
                            col = d * T + (stg // 4) * 512 + (stg % 4) * 128
                            te.mm(bank[:], XT[:, col:col + 128], wvsb[:, d, :],
                                  d == 0, d == NDC - 1,
                                  sem=TS_ if d == NDC - 1 else None)
                        mm_done[("v", stg)] = TS_.n
                        ve.wait(TS_, mm_done[("v", stg)])
                        ve.tcopy(Vtm[:, stg * 512:(stg + 1) * 512], bank[:], sem=VS)
                        ev_done[("v", stg)] = VS.n
                scos = stXT.enter_context(nc.sbuf_tensor("scos", [128, T], F32))
                ssin = stXT.enter_context(nc.sbuf_tensor("ssin", [128, T], F32))
                shb = stXT.enter_context(nc.sbuf_tensor("shb", [128, T], BF16))
                t1b = stXT.enter_context(nc.sbuf_tensor("t1b", [128, 1024], F32))
                t2b = stXT.enter_context(nc.sbuf_tensor("t2b", [128, 1024], F32))
                sy.wait(TS_, mm_done[("v", 15)])
                sy.dma(scos[:], cosf, LS)
                sy.dma(ssin[:], sinf, LS)
                tbl_done = LS.n
                for pi, m in enumerate(order):
                    te.wait(LQ, wqk_done[pi])
                    dst = QY if m < 16 else K4
                    dcol = (m if m < 16 else m - 16) * T
                    for tb in range(4):
                        bank = pqk[(pi * 4 + tb) % 2]
                        prev = (pi * 4 + tb) - 2
                        if prev >= 0:
                            te.wait(VS, ev_done[("qk", prev)])
                        for d in range(NDC):
                            te.mm(bank[:], wqkb[pi % 3][:, d, :],
                                  XT[:, d * T + tb * 512: d * T + tb * 512 + 512],
                                  d == 0, d == NDC - 1,
                                  sem=TS_ if d == NDC - 1 else None)
                        mm_done[("qk", pi, tb)] = TS_.n
                        ve.wait(TS_, mm_done[("qk", pi, tb)])
                        if pi == 0 and tb == 0:
                            ve.wait(LS, tbl_done)
                        ve.tcopy(dst[:, dcol + tb * 512: dcol + tb * 512 + 512],
                                 bank[:], sem=VS)
                        ev_done[("qk", pi * 4 + tb)] = VS.n
                    sy.wait(VS, ev_done[("qk", pi * 4 + 3)])
                    if pi >= 1:
                        sy.wait(VS, rope_done[order[pi - 1]])
                    sy.dma(shb[0:64, :], dst[64:128, dcol:dcol + T], LS)
                    sy.dma(shb[64:128, :], dst[0:64, dcol:dcol + T], LS)
                    ve.wait(LS, LS.n)
                    for hh in range(2):
                        lo, hi = hh * 1024, (hh + 1) * 1024
                        ve.tt(t1b[:], dst[:, dcol + lo: dcol + hi],
                              scos[:, lo:hi], ALU.mult)
                        ve.tt(t2b[:], shb[:, lo:hi], ssin[:, lo:hi], ALU.mult)
                        ve.tt(dst[:, dcol + lo: dcol + hi], t1b[:], t2b[:],
                              ALU.add, sem=VS if hh == 1 else None)
                    rope_done[m] = VS.n
                for si in range(16):
                    gp.wait(LS, ld_done[si])
                    gp.dma(xbf[si * 128:(si + 1) * 128, :], xsl[si % 2][:], LQ)
                    assert LQ.n == cast_done[si]
                gp.dma(wvsb[:], wv, LQ)
                assert LQ.n == wv_done
                for pi, m in enumerate(order):
                    if pi >= 3:
                        gp.wait(TS_, mm_done[("qk", pi - 3, 3)])
                    gp.dma(wqkb[pi % 3][:], wqk[m], LQ)
                    assert LQ.n == wqk_done[pi]
                qk_ev_last = ev_done[("qk", 79)]

            with ExitStack() as s4:
                e4 = s4.enter_context
                PT = e4(nc.sbuf_tensor("PT", [128, 16 * 512], BF16))
                invlb = [e4(nc.sbuf_tensor(f"invl{i}", [128, 512], F32))
                         for i in range(2)]
                lntmp = e4(nc.sbuf_tensor("lntmp", [128, 512], F32))
                ps = [e4(nc.psum_tensor(f"ps{i}", [128, 512], F32))
                      for i in range(4)]
                plb = [e4(nc.psum_tensor(f"pl{i}", [128, 512], F32))
                       for i in range(2)]
                pob = [e4(nc.psum_tensor(f"po{i}", [128, 512], F32))
                       for i in range(2)]
                bank_free = {0: 0, 1: 0, 2: 0, 3: 0}
                oevac_hist = [0, 0]
                first_st = [True]
                for h in range(NH):
                    g = h // 4
                    for jb in range(4):
                        nsc = 4 * (jb + 1)
                        q0 = h * T + jb * 512
                        par = (h * 4 + jb) % 2
                        pl, po = plb[par], pob[par]
                        exp_done, mask_done = [], {}

                        def emit_lav(c):
                            te.wait(SS, exp_done[c])
                            if c in mask_done:
                                te.wait(VS, mask_done[c])
                            if c == 0:
                                te.wait(VS, oevac_hist[par])
                            te.mm(pl[:], c_o128[:], PT[:, c * 512:(c + 1) * 512],
                                  c == 0, c == nsc - 1,
                                  sem=TS_ if c == nsc - 1 else None)
                            te.mm(po[:],
                                  Vtm[:, c * 512 + g * 128: c * 512 + g * 128 + 128],
                                  PT[:, c * 512:(c + 1) * 512],
                                  c == 0, c == nsc - 1,
                                  sem=TS_ if c == nsc - 1 else None)

                        for c in range(nsc):
                            bk = c % 4
                            te.wait(SS, bank_free[bk])
                            if first_st[0]:
                                te.wait(VS, qk_ev_last)
                                first_st[0] = False
                            if c == 0:
                                te.wait(VS, rope_done[h])
                                te.wait(VS, rope_done[16 + g])
                            te.mm(ps[bk][:],
                                  K4[:, g * T + c * 128: g * T + c * 128 + 128],
                                  QY[:, q0:q0 + 512], True, True, sem=TS_)
                            sc.wait(TS_, TS_.n)
                            sc.act(PT[:, c * 512:(c + 1) * 512], ps[bk][:],
                                   AF.Exp, bias=0.0, scale=SCALE, sem=SS)
                            exp_done.append(SS.n)
                            bank_free[bk] = SS.n
                            if c >= 4 * jb:
                                ve.wait(SS, exp_done[c])
                                ve.tt(PT[:, c * 512:(c + 1) * 512],
                                      PT[:, c * 512:(c + 1) * 512],
                                      c_tri[:, c - 4 * jb, :], ALU.mult, sem=VS)
                                mask_done[c] = VS.n
                            if c >= 4:
                                emit_lav(c - 4)
                        for c in range(max(0, nsc - 4), nsc):
                            emit_lav(c)
                        to = TS_.n
                        tl = TS_.n - 1
                        sc.wait(TS_, tl)
                        sc.wait(VS, oevac_hist[par])
                        sc.act(lntmp[:], pl[:], AF.Ln)
                        sc.act(invlb[par][:], lntmp[:], AF.Exp, bias=0.0,
                               scale=-1.0, sem=SS)
                        inv_d = SS.n
                        ve.wait(SS, inv_d)
                        ve.wait(TS_, to)
                        ve.tt(QY[:, q0:q0 + 512], po[:], invlb[par][:], ALU.mult, sem=VS)
                        oevac_hist[par] = VS.n
                for nm, tsr in [("y", QY), ("qy", QY), ("k4", K4), ("vtm", Vtm)]:
                    if nm in taps:
                        sy.wait(VS, VS.n)
                        sy.dma(taps[nm], tsr[:], LS)
                if taps:
                    sy.wait(LS, LS.n)
                run_block(nc, {"sync": sy, "gpsimd": gp, "tensor": te,
                               "vector": ve, "scalar": sc})

        x1st_done = [0] * 16
        xqst_done = [0] * 16
        zadd_v = [0] * 16
        with ExitStack() as s4:
            e4 = s4.enter_context
            afm = e4(nc.sbuf_tensor("afm", [128, NDC * 512], F32))
            Ztm = e4(nc.sbuf_tensor("Ztm", [128, 4 * D], F32))
            wob = [e4(nc.sbuf_tensor(f"wob{i}", [128, NDC, 128], BF16))
                   for i in range(2)]
            xslP = [e4(nc.sbuf_tensor(f"xslP{i}", [128, D], F32)) for i in range(2)]
            x1sl = [e4(nc.sbuf_tensor(f"x1sl{i}", [128, D], F32)) for i in range(2)]
            k8sl = [e4(nc.sbuf_tensor(f"k8sl{i}", [128, D], I8)) for i in range(2)]
            xqsl = [e4(nc.sbuf_tensor(f"xqsl{i}", [128, D], BF16)) for i in range(2)]
            sqscr = e4(nc.sbuf_tensor("sqscr", [128, D], F32))
            stt = e4(nc.sbuf_tensor("stt", [128, 48], F32))
            pw = [e4(nc.psum_tensor(f"pw{i}", [128, 512], F32)) for i in range(2)]
            ptr = [e4(nc.psum_tensor(f"ptr{i}", [128, 128], F32)) for i in range(2)]
            sy, gp, te, ve, sc = EP(), EP(), EP(), EP(), EP()
            lq0 = LQ.n
            wo_done = [lq0 + 16 * (g + 1) for g in range(4 * NDC)]
            wo_mm, afm_ev, tr_done, ztr_ev = [], [], {}, {}
            sq_ss = [0] * 16
            xq_v = [0] * 16
            for tblk in range(4):
                for mc in range(NDC):
                    mcg = tblk * NDC + mc
                    te.wait(LQ, wo_done[mcg])
                    if mcg >= 2:
                        te.wait(SS, afm_ev[mcg - 2])
                    for d in range(NDC):
                        te.mm(pw[mcg % 2][:], wob[mcg % 2][:, d, :],
                              QY[:, d * T + tblk * 512: d * T + tblk * 512 + 512],
                              d == 0, d == NDC - 1,
                              sem=TS_ if d == NDC - 1 else None)
                    wo_mm.append(TS_.n)
                    sc.wait(TS_, wo_mm[mcg])
                    if tblk > 0 and mc == 0:
                        sc.wait(TS_, tr_done[(tblk - 1) * 64 + 63])
                    sc.scopy(afm[:, mc * 512:(mc + 1) * 512], pw[mcg % 2][:], sem=SS)
                    afm_ev.append(SS.n)
                for mc in range(NDC):
                    mcg = tblk * NDC + mc
                    for ts_i in range(4):
                        idx = tblk * 64 + mc * 4 + ts_i
                        te.wait(SS, afm_ev[mcg])
                        if idx >= 2:
                            te.wait(SS, ztr_ev[idx - 2])
                        te.tr(ptr[idx % 2][:],
                              afm[:, mc * 512 + ts_i * 128: mc * 512 + ts_i * 128 + 128],
                              c_id32[:], sem=TS_)
                        tr_done[idx] = TS_.n
                        sc.wait(TS_, tr_done[idx])
                        if idx == tblk * 64 and tblk > 0:
                            sc.wait(VS, xq_v[tblk * 4 - 1])
                        sc.scopy(Ztm[:, ts_i * D + mc * 128: ts_i * D + mc * 128 + 128],
                                 ptr[idx % 2][:], sem=SS)
                        ztr_ev[idx] = SS.n
                mset = [0] * 4
                for ts_i in range(4):
                    gidx = tblk * 4 + ts_i
                    row0 = gidx * 128
                    Z = Ztm[:, ts_i * D:(ts_i + 1) * D]
                    if gidx >= 2:
                        sy.wait(VS, zadd_v[gidx - 2])
                    sy.dma(xslP[ts_i % 2][:], x[row0:row0 + 128, :], LS)
                    xld = LS.n
                    ve.wait(LS, xld)
                    ve.wait(SS, ztr_ev[tblk * 64 + 60 + ts_i])
                    ve.tt(Z, Z, xslP[ts_i % 2][:], ALU.add, sem=VS)
                    zadd_v[gidx] = VS.n
                    ve.rsum(stt[:, ts_i:ts_i + 1], Z)
                    ve.memset(stt[:, 4 + ts_i:5 + ts_i], 0.0, sem=VS)
                    mset[ts_i] = VS.n
                    sc.wait(VS, mset[ts_i])
                    sc.act(sqscr[:], Z, AF.Square, accum=stt[:, 4 + ts_i:5 + ts_i],
                           sem=SS)
                    sq_ss[tblk * 4 + ts_i] = SS.n
                ve.wait(SS, sq_ss[tblk * 4 + 3])
                ve.ts(stt[:, 8:12], stt[:, 0:4], -1.0 / D, None, ALU.mult)
                ve.ts(stt[:, 12:16], stt[:, 4:8], 1.0 / D, EPS, ALU.mult, ALU.add)
                ve.drain()
                ve.tt(stt[:, 16:20], stt[:, 8:12], stt[:, 8:12], ALU.mult)
                ve.drain()
                ve.tt(stt[:, 20:24], stt[:, 12:16], stt[:, 16:20], ALU.subtract)
                ve.drain()
                ve.recip(stt[:, 24:28], stt[:, 20:24], sem=VS)
                rvd = VS.n
                sc.wait(VS, rvd)
                sc.act(stt[:, 28:32], stt[:, 24:28], AF.Sqrt, sem=SS)
                rstd_d = SS.n
                ve.wait(SS, rstd_d)
                ve.tt(stt[:, 32:36], stt[:, 8:12], stt[:, 28:32], ALU.mult)
                ve.drain()
                for ts_i in range(4):
                    Z = Ztm[:, ts_i * D:(ts_i + 1) * D]
                    ve.ts(sqscr[:], Z, 1.0, stt[:, 8 + ts_i:9 + ts_i],
                          ALU.mult, ALU.add)
                    ve.rmax(stt[:, 36 + ts_i:37 + ts_i], sqscr[:], absval=True)
                ve.drain()
                ve.tt(stt[:, 40:44], stt[:, 36:40], stt[:, 28:32], ALU.mult)
                ve.drain()
                ve.ts(stt[:, 40:44], stt[:, 40:44], EPS, None, ALU.max)
                ve.drain()
                ve.recip(stt[:, 36:40], stt[:, 40:44])
                ve.drain()
                ve.ts(stt[:, 44:48], stt[:, 36:40], 127.0, None, ALU.mult)
                ve.ts(stt[:, 36:40], stt[:, 40:44], 1.0 / 127.0, None, ALU.mult)
                ve.drain()
                for ts_i in range(4):
                    gidx = tblk * 4 + ts_i
                    row0 = gidx * 128
                    Z = Ztm[:, ts_i * D:(ts_i + 1) * D]
                    if gidx >= 2:
                        ve.wait(LS, x1st_done[gidx - 2])
                        ve.wait(LS, xqst_done[gidx - 2])
                    ve.ts(x1sl[ts_i % 2][:], Z, stt[:, 28 + ts_i:29 + ts_i],
                          stt[:, 32 + ts_i:33 + ts_i], ALU.mult, ALU.add, sem=VS)
                    x1d = VS.n
                    sy.wait(VS, x1d)
                    sy.dma(x1sp[row0:row0 + 128, :], x1sl[ts_i % 2][:], LS)
                    x1st_done[gidx] = LS.n
                    ve.ts(k8sl[ts_i % 2][:], x1sl[ts_i % 2][:],
                          stt[:, 44 + ts_i:45 + ts_i], None, ALU.mult)
                    ve.ts(xqsl[ts_i % 2][:], k8sl[ts_i % 2][:],
                          stt[:, 36 + ts_i:37 + ts_i], None, ALU.mult, sem=VS)
                    xqd = VS.n
                    xq_v[gidx] = xqd
                    sy.wait(VS, xqd)
                    sy.dma(xqsp[row0:row0 + 128, :], xqsl[ts_i % 2][:], LS)
                    xqst_done[gidx] = LS.n
            gp2 = EP()
            for tblk in range(4):
                for mc in range(NDC):
                    mcg = tblk * NDC + mc
                    if mcg >= 2:
                        gp2.wait(TS_, wo_mm[mcg - 2])
                    gp2.dma(wob[mcg % 2][:], wo[mc], LQ)
                    assert LQ.n == wo_done[mcg]
            gp = gp2
            if "x1" in taps:
                sy.wait(LS, LS.n)
                sy.dma(taps["x1"], x1sp, LS)
            if "xq" in taps:
                sy.wait(LS, LS.n)
                sy.dma(taps["xq"], xqsp, LS)
                sy.wait(LS, LS.n)
            run_block(nc, {"sync": sy, "gpsimd": gp, "tensor": te,
                           "vector": ve, "scalar": sc})
        stQY.close()

        with ExitStack() as s4:
            e4 = s4.enter_context
            XqTb = e4(nc.sbuf_tensor("XqTb", [128, NDC * 512], BF16))
            GN = e4(nc.sbuf_tensor("GN", [128, NHC * 512 + D], BF16))
            w13b = [e4(nc.sbuf_tensor(f"w13b{i}", [128, 2, NDC, 128], BF16))
                    for i in range(2)]
            w2b = [e4(nc.sbuf_tensor(f"w2b{i}", [128, NHC, 128], BF16))
                   for i in range(2)]
            rbuf = e4(nc.sbuf_tensor("rb4", [128, 512], F32))
            r2buf = e4(nc.sbuf_tensor("r2b4", [128, 512], F32))
            gbuf = e4(nc.sbuf_tensor("gb4", [128, 512], F32))
            gsqb = [e4(nc.sbuf_tensor(f"gsq4_{i}", [128, 512], BF16))
                    for i in range(2)]
            gnsqb = e4(nc.sbuf_tensor("gnsq4", [128, 512], F32))
            runmax = e4(nc.sbuf_tensor("rmx4", [128, 512], F32))
            m2col = e4(nc.sbuf_tensor("m2c4", [128, 4], F32))
            rows4 = e4(nc.sbuf_tensor("r4_4", [4, 128], F32))
            rowscr = e4(nc.sbuf_tensor("rsc4", [1, 2048], F32))
            RSb = e4(nc.sbuf_tensor("RSb4", [128, 1024], BF16))
            iS2b = e4(nc.sbuf_tensor("iS2b4", [128, 512], F32))
            k8b = e4(nc.sbuf_tensor("k8b4", [128, 1024], I8))
            z2fm = [e4(nc.sbuf_tensor(f"z2f4_{i}", [128, 512], F32))
                    for i in range(2)]
            Ztm2 = e4(nc.sbuf_tensor("Zt2_4", [128, 4 * D], F32))
            x1r = [e4(nc.sbuf_tensor(f"x1r4_{i}", [128, D], F32)) for i in range(2)]
            outsl = [e4(nc.sbuf_tensor(f"osl4_{i}", [128, D], F32))
                     for i in range(2)]
            st2t = e4(nc.sbuf_tensor("st2_4", [128, 48], F32))
            pg = [e4(nc.psum_tensor(f"pg4_{i}", [128, 512], F32)) for i in range(4)]
            pstat = e4(nc.psum_tensor("pst4", [1, 512], F32))
            ptrx = [e4(nc.psum_tensor(f"ptx4_{i}", [128, 128], F32))
                    for i in range(2)]
            pz = e4(nc.psum_tensor("pz4", [128, 512], F32))
            sy, gp, te, ve, sc = EP(), EP(), EP(), EP(), EP()

            lq0 = LQ.n
            w13_done = [[lq0 + 16 * (tb * (NHC + NDC) + j + 1) for j in range(NHC)]
                        for tb in range(4)]
            w2_done = [[lq0 + 16 * (tb * (NHC + NDC) + NHC + mc + 1)
                        for mc in range(NDC)] for tb in range(4)]
            mm1_done, gate_v, gate_s, gsq_ready = [], [], [], []
            mm2_done, z2fm_ev, trz_ev = [], [], {}
            outst = [0] * 16
            z2add_v = [0] * 16
            prev = {"bc_d": 0, "ssr_d": 0, "trm": 0, "mm2_last": 0, "trz_last": 0}
            def emit_ln2(tblk):
                    ms2 = [0] * 4
                    sq2l = [0] * 4
                    for ts_i in range(4):
                        gidx = tblk * 4 + ts_i
                        row0 = gidx * 128
                        Z = Ztm2[:, ts_i * D:(ts_i + 1) * D]
                        sy.wait(LS, x1st_done[gidx])
                        if gidx >= 2:
                            sy.wait(VS, z2add_v[gidx - 2])
                        sy.dma(x1r[ts_i % 2][:], x1sp[row0:row0 + 128, :], LS)
                        x1rd = LS.n
                        ve.wait(LS, x1rd)
                        if ts_i == 0:
                            ve.wait(VS, trz_ev[tblk * 64 + 63])
                        ve.tt(Z, Z, x1r[ts_i % 2][:], ALU.add, sem=VS)
                        z2add_v[gidx] = VS.n
                        ve.rsum(st2t[:, ts_i:ts_i + 1], Z)
                        ve.memset(st2t[:, 4 + ts_i:5 + ts_i], 0.0, sem=VS)
                        ms2[ts_i] = VS.n
                        sc.wait(VS, ms2[ts_i])
                        sc.act(GN[:, NHC * 512:NHC * 512 + D], Z, AF.Square,
                               accum=st2t[:, 4 + ts_i:5 + ts_i], sem=SS)
                        sq2l[ts_i] = SS.n
                        prev["sq2_last"] = SS.n
                    ve.wait(SS, sq2l[3])
                    ve.ts(st2t[:, 8:12], st2t[:, 0:4], -1.0 / D, None, ALU.mult)
                    ve.ts(st2t[:, 12:16], st2t[:, 4:8], 1.0 / D, EPS, ALU.mult,
                          ALU.add)
                    ve.drain()
                    ve.tt(st2t[:, 16:20], st2t[:, 8:12], st2t[:, 8:12], ALU.mult)
                    ve.drain()
                    ve.tt(st2t[:, 20:24], st2t[:, 12:16], st2t[:, 16:20],
                          ALU.subtract)
                    ve.drain()
                    ve.recip(st2t[:, 24:28], st2t[:, 20:24], sem=VS)
                    rv3 = VS.n
                    sc.wait(VS, rv3)
                    sc.act(st2t[:, 28:32], st2t[:, 24:28], AF.Sqrt, sem=SS)
                    r2d2 = SS.n
                    ve.wait(SS, r2d2)
                    ve.tt(st2t[:, 32:36], st2t[:, 8:12], st2t[:, 28:32], ALU.mult)
                    ve.drain()
                    for ts_i in range(4):
                        gidx = tblk * 4 + ts_i
                        row0 = gidx * 128
                        Z = Ztm2[:, ts_i * D:(ts_i + 1) * D]
                        if gidx >= 2:
                            ve.wait(LS, outst[gidx - 2])
                        ve.ts(outsl[ts_i % 2][:], Z, st2t[:, 28 + ts_i:29 + ts_i],
                              st2t[:, 32 + ts_i:33 + ts_i], ALU.mult, ALU.add,
                              sem=VS)
                        od = VS.n
                        sy.wait(VS, od)
                        sy.dma(out[row0:row0 + 128, :], outsl[ts_i % 2][:], LS)
                        outst[gidx] = LS.n

            xqt_done_arr = [0] * 4
            for tblk in range(4):
                if tblk == 0:
                    sy.wait(LS, xqst_done[3])
                    for c in range(NDC):
                        sy.dmat(XqTb[:, c * 512:(c + 1) * 512],
                                xqsp[0:512, c * 128:(c + 1) * 128], LS)
                    xqt_done_arr[0] = LS.n
                xqt_done = xqt_done_arr[tblk]

                te.wait(VS, prev["bc_d"])
                te.wait(SS, prev["ssr_d"])
                for j in range(NHC):
                    jg = tblk * NHC + j
                    if j == 4 and tblk > 0:
                        emit_ln2(tblk - 1)
                    te.wait(LQ, w13_done[tblk][j])
                    if jg >= 2:
                        te.wait(VS, gate_v[jg - 2])
                        te.wait(SS, gate_s[jg - 2])
                    if j == 0:
                        te.wait(LS, xqt_done)
                    for half in range(2):
                        bank = pg[(jg % 2) * 2 + half]
                        for d in range(NDC):
                            te.mm(bank[:], w13b[jg % 2][:, half, d, :],
                                  XqTb[:, d * 512:(d + 1) * 512],
                                  d == 0, d == NDC - 1,
                                  sem=TS_ if d == NDC - 1 else None)
                        mm1_done.append(TS_.n)
                    if j >= 2:
                        te.wait(SS, gsq_ready[jg - 2])
                        te.mm(pstat[:], c_ocb[:], gsqb[(jg - 2) % 2][:],
                              j - 2 == 0, False, sem=None)
                    tx1, tx3 = mm1_done[2 * jg], mm1_done[2 * jg + 1]
                    sc.wait(TS_, tx1)
                    sc.act(rbuf[:], pg[(jg % 2) * 2][:], AF.Relu)
                    sc.act(r2buf[:], rbuf[:], AF.Square, sem=SS)
                    r2d = SS.n
                    ve.wait(SS, r2d)
                    ve.wait(TS_, tx3)
                    if j == 0 and tblk > 0:
                        ve.wait(TS_, prev["mm2_last"])
                        ve.wait(TS_, prev["trm"])
                    ve.tt(gbuf[:], r2buf[:], pg[(jg % 2) * 2 + 1][:], ALU.mult,
                          sem=VS)
                    gbd = VS.n
                    sc.wait(VS, gbd)
                    sc.act(gsqb[jg % 2][:], gbuf[:], AF.Square, sem=SS)
                    gsq_ready.append(SS.n)
                    ve.ts(GN[:, j * 512:(j + 1) * 512], gbuf[:],
                          c_nw[:, j:j + 1], None, ALU.mult, sem=VS)
                    sc.wait(VS, VS.n)
                    sc.act(gnsqb[:], GN[:, j * 512:(j + 1) * 512], AF.Square,
                           sem=SS)
                    ve.wait(SS, SS.n)
                    if j == 0:
                        ve.tcopy(runmax[:], gnsqb[:], sem=VS)
                    else:
                        ve.tt(runmax[:], runmax[:], gnsqb[:], ALU.max, sem=VS)
                    gate_v.append(VS.n)
                    gate_s.append(SS.n)
                te.wait(SS, gsq_ready[tblk * NHC + NHC - 2])
                te.mm(pstat[:], c_ocb[:], gsqb[(tblk * NHC + NHC - 2) % 2][:],
                      False, False, sem=None)
                te.wait(SS, gsq_ready[tblk * NHC + NHC - 1])
                te.mm(pstat[:], c_ocb[:], gsqb[(tblk * NHC + NHC - 1) % 2][:],
                      False, True, sem=TS_)
                tstat = TS_.n
                sc.wait(TS_, tstat)
                sc.scopy(rowscr[:, 0:512], pstat[:], sem=SS)
                ssr_d = SS.n
                prev["ssr_d"] = ssr_d
                ve.wait(SS, ssr_d)
                ve.ts(rowscr[:, 512:1024], rowscr[:, 0:512], 1.0 / HID, EPS,
                      ALU.mult, ALU.add)
                ve.drain()
                ve.recip(rowscr[:, 1024:1536], rowscr[:, 512:1024], sem=VS)
                rv2 = VS.n
                sc.wait(VS, rv2)
                sc.act(rowscr[:, 1536:2048], rowscr[:, 1024:1536], AF.Sqrt, sem=SS)
                rstd2_d = SS.n
                trv = []
                for i in range(4):
                    if i == 0:
                        te.wait(VS, gate_v[tblk * NHC + NHC - 1])
                        if tblk > 0:
                            te.wait(VS, trz_ev[tblk * 64 - 1])
                    if i >= 2:
                        te.wait(VS, trv[i - 2])
                    te.tr(ptrx[i % 2][:], runmax[:, i * 128:(i + 1) * 128],
                          c_id32[:], sem=TS_)
                    ti = TS_.n
                    ve.wait(TS_, ti)
                    ve.rmax(m2col[:, i:i + 1], ptrx[i % 2][:], absval=False,
                            sem=VS)
                    trv.append(VS.n)
                te.wait(VS, trv[3])
                te.tr(ptrx[0][0:4, :], m2col[:], c_id32[:], sem=TS_)
                trm = TS_.n
                prev["trm"] = trm
                ve.wait(TS_, trm)
                ve.tcopy(rows4[:], ptrx[0][0:4, 0:128], sem=VS)
                r4d = VS.n
                sy.wait(VS, r4d)
                sy.dma(rowscr[:, 0:512], rows4[:], LS)
                m2r_d = LS.n
                if tblk < 3:
                    sy.wait(LS, xqst_done[(tblk + 1) * 4 + 3])
                    sy.wait(TS_, mm1_done[2 * (tblk + 1) * NHC - 1])
                    for c in range(NDC):
                        sy.dmat(XqTb[:, c * 512:(c + 1) * 512],
                                xqsp[(tblk + 1) * 512:(tblk + 2) * 512,
                                     c * 128:(c + 1) * 128], LS)
                    xqt_done_arr[tblk + 1] = LS.n
                sc.wait(LS, m2r_d)
                sc.act(rowscr[:, 512:1024], rowscr[:, 0:512], AF.Sqrt, sem=SS)
                sm2_d = SS.n
                ve.wait(SS, sm2_d)
                ve.wait(SS, rstd2_d)
                ve.tt(rowscr[:, 1024:1536], rowscr[:, 512:1024],
                      rowscr[:, 1536:2048], ALU.mult)
                ve.drain()
                ve.ts(rowscr[:, 1024:1536], rowscr[:, 1024:1536], EPS, None,
                      ALU.max)
                ve.drain()
                ve.recip(rowscr[:, 0:512], rowscr[:, 1024:1536])
                ve.drain()
                ve.ts(rowscr[:, 0:512], rowscr[:, 0:512], 127.0, None,
                      ALU.mult)
                ve.drain()
                ve.ts(rowscr[:, 512:1024], rowscr[:, 1024:1536], 1.0 / 127.0,
                      None, ALU.mult)
                ve.drain()
                ve.tt(rowscr[:, 1024:1536], rowscr[:, 1536:2048],
                      rowscr[:, 0:512], ALU.mult, sem=VS)
                rows_d = VS.n
                te.wait(VS, rows_d)
                te.mm(pg[0][:], c_o1f[:], rowscr[:, 1024:1536], True, True,
                      sem=TS_)
                tb1 = TS_.n
                te.mm(pg[1][:], c_o1f[:], rowscr[:, 512:1024], True, True,
                      sem=TS_)
                tb2 = TS_.n
                ve.wait(TS_, tb1)
                ve.tcopy(RSb[:, 0:512], pg[0][:])
                ve.tcopy(RSb[:, 512:1024], pg[0][:])
                ve.wait(TS_, tb2)
                ve.tcopy(iS2b[:], pg[1][:], sem=VS)
                prev["bc_d"] = VS.n
                for jj in range(NHC // 2):
                    sl = slice(jj * 1024, (jj + 1) * 1024)
                    ve.tt(k8b[:], GN[:, sl], RSb[:], ALU.mult)
                    ve.tcopy(GN[:, sl], k8b[:], sem=VS)
                iq_d = VS.n
                if "gn" in taps and tblk == 0:
                    sy.wait(VS, iq_d)
                    sy.dma(taps["gn"], GN[:, 0:NHC * 512], LS)
                for mc in range(NDC):
                    mcg = tblk * NDC + mc
                    te.wait(LQ, w2_done[tblk][mc])
                    if mc == 0:
                        te.wait(VS, iq_d)
                    if mcg >= 1:
                        te.wait(VS, z2fm_ev[mcg - 1])
                    for j in range(NHC):
                        te.mm(pz[:], w2b[mcg % 2][:, j, :],
                              GN[:, j * 512:(j + 1) * 512],
                              j == 0, j == NHC - 1,
                              sem=TS_ if j == NHC - 1 else None)
                    mm2_done.append(TS_.n)
                    ve.wait(TS_, mm2_done[mcg])
                    ve.tt(z2fm[mcg % 2][:], pz[:], iS2b[:], ALU.mult, sem=VS)
                    z2fm_ev.append(VS.n)
                    for ts_i in range(4):
                        idx = tblk * 64 + mc * 4 + ts_i
                        if ts_i == 0:
                            te.wait(VS, z2fm_ev[mcg])
                        if idx >= 2:
                            te.wait(VS, trz_ev[idx - 2])
                        te.tr(ptrx[idx % 2][:],
                              z2fm[mcg % 2][:, ts_i * 128:(ts_i + 1) * 128],
                              c_id32[:], sem=TS_)
                        tdd = TS_.n
                        ve.wait(TS_, tdd)
                        if idx == tblk * 64 and tblk > 0:
                            ve.wait(SS, prev["sq2_last"])
                        ve.tcopy(Ztm2[:, ts_i * D + mc * 128:
                                 ts_i * D + mc * 128 + 128],
                                 ptrx[idx % 2][:], sem=VS)
                        trz_ev[idx] = VS.n
                prev["mm2_last"] = mm2_done[tblk * NDC + NDC - 1]
            emit_ln2(3)
            sy.wait(LS, LS.n)
            gp2 = EP()
            for tblk in range(4):
                for j in range(NHC):
                    jg = tblk * NHC + j
                    if jg >= 2:
                        gp2.wait(TS_, mm1_done[2 * (jg - 2) + 1])
                    gp2.dma(w13b[jg % 2][:], w13[j], LQ)
                    assert LQ.n == w13_done[tblk][j]
                for mc in range(NDC):
                    mcg = tblk * NDC + mc
                    if mcg >= 2:
                        gp2.wait(TS_, mm2_done[mcg - 2])
                    gp2.dma(w2b[mcg % 2][:], w2[mc], LQ)
                    assert LQ.n == w2_done[tblk][mc]
            gp = gp2
            run_block(nc, {"sync": sy, "gpsimd": gp, "tensor": te,
                           "vector": ve, "scalar": sc})

    return nc


def _host_layouts(Wq, Wk, Wv, Wo, w13, w2, norm_w):
    f32 = np.float32
    wqk_src = np.concatenate([np.asarray(Wq, f32), np.asarray(Wk, f32)], axis=1)
    wqk = np.ascontiguousarray(
        wqk_src.reshape(NDC, 128, 20, 128).transpose(2, 1, 0, 3))
    wvv = np.ascontiguousarray(
        np.asarray(Wv, f32).reshape(NDC, 128, 512).transpose(1, 0, 2))
    woo = np.ascontiguousarray(
        np.asarray(Wo, f32).reshape(NDC, 128, NDC, 128).transpose(2, 1, 0, 3))
    w13a = np.asarray(w13, f32).T.reshape(NDC, 128, 88, 128).transpose(2, 1, 0, 3)
    w13t = np.ascontiguousarray(
        w13a.reshape(2, NHC, 128, NDC, 128).transpose(1, 2, 0, 3, 4))
    w2t = np.ascontiguousarray(
        np.asarray(w2, f32).T.reshape(NHC, 128, NDC, 128).transpose(2, 1, 0, 3))
    nwt = np.ascontiguousarray(np.asarray(norm_w, f32).reshape(NHC, 128).T)

    i = np.arange(64, dtype=f32)
    e = (f32(-2.0) * i) / f32(128.0)
    inv = np.power(f32(10000.0), e, dtype=f32)
    tt = np.arange(T, dtype=f32)
    ang = (tt[:, None] * inv[None, :]).astype(f32)
    cosa = np.cos(ang.astype(np.float64)).astype(f32)
    sina = np.sin(ang.astype(np.float64)).astype(f32)
    cost = np.tile(cosa, (1, 2))
    sins = np.tile(sina, (1, 2))
    sins[:, :64] *= -1.0
    cosf = np.ascontiguousarray(cost.T)
    sinf = np.ascontiguousarray(sins.T)

    id32 = np.eye(128, dtype=f32)
    ones128 = np.ones((128, 128), ml_dtypes.bfloat16)
    ones1f = np.ones((1, 128), f32)
    onescb = np.ones((128, 1), ml_dtypes.bfloat16)
    q_idx = np.arange(512)
    tri = np.zeros((128, 4, 512), np.float32)
    for k in range(4):
        s_idx = np.arange(128)[:, None] + k * 128
        tri[:, k, :] = (q_idx[None, :] >= s_idx).astype(np.float32)
    tri = tri.astype(ml_dtypes.bfloat16)
    return dict(wqk=wqk, wv=wvv, wo=woo, w13=w13t, w2=w2t, nwt=nwt,
                cosf=cosf, sinf=sinf, id32=id32, ones128=ones128,
                ones1f=ones1f, onescb=onescb, tri=tri)


def kernel(L0_attn_x, L0_attn_Wq, L0_attn_Wk, L0_attn_Wv, L0_attn_Wo,
           L0_ffn_w13, L0_ffn_w2, L0_ffn_norm_w, _debug_taps=(), _nc=None):
    xin = np.ascontiguousarray(np.asarray(L0_attn_x, np.float32))
    consts = _host_layouts(L0_attn_Wq, L0_attn_Wk, L0_attn_Wv, L0_attn_Wo,
                           L0_ffn_w13, L0_ffn_w2, L0_ffn_norm_w)
    nc = _nc if _nc is not None else build(_debug_taps)
    in_maps = []
    for b in range(8):
        m = dict(consts)
        m["x"] = np.ascontiguousarray(xin[b])
        in_maps.append(m)
    res = run_bass_kernel_spmd(nc, in_maps, core_ids=list(range(8)))
    outs = [np.asarray(r["out"], np.float32) for r in res.results]
    if _debug_taps:
        return np.stack(outs, 0), res
    return np.stack(outs, 0)


# version 09
# speedup vs baseline: 1.0720x; 1.0046x over previous
import numpy as np
import ml_dtypes
import concourse.bass as bass
import concourse.mybir as mybir
from concourse.bass_utils import run_bass_kernel_spmd
from contextlib import ExitStack

F32 = mybir.dt.float32
BF16 = mybir.dt.bfloat16
I8 = mybir.dt.int8
AF = mybir.ActivationFunctionType
ALU = mybir.AluOpType
AX = mybir.AxisListType

T = 2048
D = 2048
NH = 16
NKV = 4
HID = 5632
NDC = 16
NHC = 44
SCALE = 1.0 / float(np.float32(np.sqrt(np.float32(128.0))))
EPS = 1e-5


class Sem:
    def __init__(self, h):
        self.h = h
        self.n = 0

    def inc(self, k):
        self.n += k
        return self.n


class EP:

    def __init__(self):
        self.ops = []

    def wait(self, sem, val):
        if val > 0:
            self.ops.append(("wait", sem.h, val))

    def _p(self, *a, sem=None, inc=0):
        if sem is not None:
            sem.inc(inc)
            self.ops.append(a + (sem.h, inc))
        else:
            self.ops.append(a + (None, 0))

    def dma(self, out, in_, sem):
        self._p("dma", out, in_, sem=sem, inc=16)

    def dmat(self, out, in_, sem):
        self._p("dmat", out, in_, sem=sem, inc=16)

    def mm(self, out, lhsT, rhs, start, stop, sem=None):
        self._p("mm", out, lhsT, rhs, start, stop, sem=sem, inc=1 if sem else 0)

    def tr(self, out, in_, ident, sem=None):
        self._p("tr", out, in_, ident, sem=sem, inc=1 if sem else 0)

    def act(self, out, in_, func, bias=0.0, scale=1.0, accum=None, sem=None):
        self._p("act", out, in_, func, bias, scale, accum, sem=sem,
                inc=1 if sem else 0)

    def scopy(self, out, in_, sem=None):
        self._p("scopy", out, in_, sem=sem, inc=1 if sem else 0)

    def tt(self, out, a, b, op, sem=None):
        self._p("tt", out, a, b, op, sem=sem, inc=1 if sem else 0)

    def ts(self, out, in0, s1, s2, op0, op1=None, sem=None):
        self._p("ts", out, in0, s1, s2, op0, op1, sem=sem, inc=1 if sem else 0)

    def tcopy(self, out, in_, sem=None):
        self._p("tcopy", out, in_, sem=sem, inc=1 if sem else 0)

    def recip(self, out, in_, sem=None):
        self._p("recip", out, in_, sem=sem, inc=1 if sem else 0)

    def rsum(self, out, in_, sem=None):
        self._p("rsum", out, in_, sem=sem, inc=1 if sem else 0)

    def rmax(self, out, in_, absval=False, sem=None):
        self._p("rmax", out, in_, absval, sem=sem, inc=1 if sem else 0)

    def memset(self, ap, v, sem=None):
        self._p("memset", ap, v, sem=sem, inc=1 if sem else 0)

    def drain(self):
        self.ops.append(("drain", None, 0))


def _replay(eng, ops):
    for op in ops:
        k = op[0]
        if k == "wait":
            eng.wait_ge(op[1], op[2])
            continue
        if k == "drain":
            eng.drain()
            continue
        sem, inc = op[-2], op[-1]
        if k == "dma":
            ins = eng.dma_start(out=op[1], in_=op[2])
        elif k == "dmat":
            ins = eng.dma_start_transpose(op[1], op[2])
        elif k == "mm":
            ins = eng.matmul(op[1], op[2], op[3], start=op[4], stop=op[5])
        elif k == "tr":
            ins = eng.transpose(op[1], op[2], op[3])
        elif k == "act":
            ins = eng.activation(op[1], op[2], op[3], bias=op[4], scale=op[5],
                                 accum_out=op[6])
        elif k == "scopy":
            ins = eng.copy(op[1], op[2])
        elif k == "tt":
            ins = eng.tensor_tensor(op[1], op[2], op[3], op[4])
        elif k == "ts":
            if op[6] is None:
                ins = eng.tensor_scalar(op[1], op[2], op[3], None, op[5])
            else:
                ins = eng.tensor_scalar(op[1], op[2], op[3], op[4], op[5], op[6])
        elif k == "tcopy":
            ins = eng.tensor_copy(op[1], op[2])
        elif k == "recip":
            ins = eng.reciprocal(op[1], op[2])
        elif k == "rsum":
            ins = eng.reduce_sum(op[1], op[2], axis=AX.X)
        elif k == "rmax":
            ins = eng.reduce_max(op[1], op[2], axis=AX.X,
                                 apply_absolute_value=op[3])
        elif k == "memset":
            ins = eng.memset(op[1], op[2])
        else:
            raise ValueError(k)
        if sem is not None:
            ins.then_inc(sem, inc)


def run_block(nc, plans):
    with nc.Block() as blk:
        for name, ep in plans.items():
            if not ep.ops:
                continue
            getattr(blk, name)(lambda eng, _ops=tuple(ep.ops): _replay(eng, _ops))


def build(debug_taps=()):
    nc = bass.Bass("TRN2", target_bir_lowering=False)
    dt = nc.dram_tensor
    x = dt("x", [T, D], F32, kind="ExternalInput").ap()
    wqk = dt("wqk", [20, 128, NDC, 128], F32, kind="ExternalInput").ap()
    wv = dt("wv", [128, NDC, 512], F32, kind="ExternalInput").ap()
    wo = dt("wo", [NDC, 128, NDC, 128], F32, kind="ExternalInput").ap()
    w13 = dt("w13", [NHC, 128, 2, NDC, 128], F32, kind="ExternalInput").ap()
    w2 = dt("w2", [NDC, 128, NHC, 128], F32, kind="ExternalInput").ap()
    nwt = dt("nwt", [128, NHC], F32, kind="ExternalInput").ap()
    cosf = dt("cosf", [128, T], F32, kind="ExternalInput").ap()
    sinf = dt("sinf", [128, T], F32, kind="ExternalInput").ap()
    id32 = dt("id32", [128, 128], F32, kind="ExternalInput").ap()
    ones128 = dt("ones128", [128, 128], BF16, kind="ExternalInput").ap()
    ones1f = dt("ones1f", [1, 128], F32, kind="ExternalInput").ap()
    onescb = dt("onescb", [128, 1], BF16, kind="ExternalInput").ap()
    tri = dt("tri", [128, 4, 512], BF16, kind="ExternalInput").ap()
    out = dt("out", [T, D], F32, kind="ExternalOutput").ap()

    xbf = dt("xbf", [T, D], BF16, kind="Internal").ap()
    x1sp = dt("x1sp", [T, D], F32, kind="Internal").ap()
    xqsp = dt("xqsp", [T, D], BF16, kind="Internal").ap()

    taps = {}
    for tap in debug_taps:
        shp = {"qy": [128, NH * T], "k4": [128, NKV * T], "vtm": [128, 16 * 512],
               "y": [128, NH * T], "x1": [T, D], "xq": [T, D],
               "gn": [128, NHC * 512], "afm0": [128, NDC * 512],
               "z0": [128, 4 * D]}[tap]
        dtp = {"x1": F32, "afm0": F32, "z0": F32}.get(tap, BF16)
        taps[tap] = dt("tap_" + tap, shp, dtp, kind="ExternalOutput").ap()

    with ExitStack() as st:
        ent = st.enter_context
        c_id32 = ent(nc.sbuf_tensor("c_id32", [128, 128], F32))
        c_o128 = ent(nc.sbuf_tensor("c_o128", [128, 128], BF16))
        c_o1f = ent(nc.sbuf_tensor("c_o1f", [1, 128], F32))
        c_ocb = ent(nc.sbuf_tensor("c_ocb", [128, 1], BF16))
        c_tri = ent(nc.sbuf_tensor("c_tri", [128, 4, 512], BF16))
        c_nw = ent(nc.sbuf_tensor("c_nw", [128, NHC], F32))
        LS = Sem(ent(nc.semaphore("LS")))
        LQ = Sem(ent(nc.semaphore("LQ")))
        TS_ = Sem(ent(nc.semaphore("TSm")))
        VS = Sem(ent(nc.semaphore("VSm")))
        SS = Sem(ent(nc.semaphore("SSm")))

        stQY = ExitStack()
        QY = stQY.enter_context(nc.sbuf_tensor("QY", [128, NH * T], BF16))

        with ExitStack() as stKV:
            eKV = stKV.enter_context
            K4 = eKV(nc.sbuf_tensor("K4", [128, NKV * T], BF16))
            Vtm = eKV(nc.sbuf_tensor("Vtm", [128, 16 * 512], BF16))

            sy, gp, te, ve, sc = EP(), EP(), EP(), EP(), EP()
            base = LQ.n
            cast_done = [base + 16 * (s + 1) for s in range(16)]
            wv_done = base + 16 * 17
            wqk_done = [wv_done + 16 * (pi + 1) for pi in range(20)]
            order = list(range(16, 20)) + list(range(16))

            with ExitStack() as stXT:
                XT = stXT.enter_context(nc.sbuf_tensor("XT", [128, NDC * T], BF16))
                with ExitStack() as s4:
                    e4 = s4.enter_context
                    xsl = [e4(nc.sbuf_tensor(f"xsl{i}", [128, D], F32))
                           for i in range(2)]
                    for dst, src in [(c_id32[:], id32), (c_o128[:], ones128),
                                     (c_o1f[:], ones1f), (c_ocb[:], onescb),
                                     (c_tri[:], tri), (c_nw[:], nwt)]:
                        sy.dma(dst, src, LS)
                    ld_done = []
                    for s in range(16):
                        if s >= 2:
                            sy.wait(LQ, cast_done[s - 2])
                        sy.dma(xsl[s % 2][:], x[s * 128:(s + 1) * 128, :], LS)
                        ld_done.append(LS.n)
                    sy.wait(LQ, cast_done[15])
                    for c in range(NDC):
                        sy.dmat(XT[:, c * T:(c + 1) * T],
                                xbf[:, c * 128:(c + 1) * 128], LS)
                    xt_done = LS.n
                wqkb = [stXT.enter_context(
                    nc.sbuf_tensor(f"wqkb{i}", [128, NDC, 128], BF16))
                    for i in range(3)]
                pqk = [stXT.enter_context(nc.psum_tensor(f"pqk{i}", [128, 512], F32))
                       for i in range(2)]
                pv = [stXT.enter_context(nc.psum_tensor(f"pv{i}", [128, 512], F32))
                      for i in range(2)]
                ev_done, mm_done, rope_done = {}, {}, {}
                with ExitStack() as sV:
                    wvsb = sV.enter_context(
                        nc.sbuf_tensor("wvsb", [128, NDC, 512], BF16))
                    te.wait(LQ, wv_done)
                    te.wait(LS, xt_done)
                    for stg in range(16):
                        bank = pv[stg % 2]
                        if stg >= 2:
                            te.wait(VS, ev_done[("v", stg - 2)])
                        for d in range(NDC):
                            col = d * T + (stg // 4) * 512 + (stg % 4) * 128
                            te.mm(bank[:], XT[:, col:col + 128], wvsb[:, d, :],
                                  d == 0, d == NDC - 1,
                                  sem=TS_ if d == NDC - 1 else None)
                        mm_done[("v", stg)] = TS_.n
                        ve.wait(TS_, mm_done[("v", stg)])
                        ve.tcopy(Vtm[:, stg * 512:(stg + 1) * 512], bank[:], sem=VS)
                        ev_done[("v", stg)] = VS.n
                scos = stXT.enter_context(nc.sbuf_tensor("scos", [128, T], F32))
                ssin = stXT.enter_context(nc.sbuf_tensor("ssin", [128, T], F32))
                shb = stXT.enter_context(nc.sbuf_tensor("shb", [128, T], BF16))
                t1b = stXT.enter_context(nc.sbuf_tensor("t1b", [128, 1024], F32))
                t2b = stXT.enter_context(nc.sbuf_tensor("t2b", [128, 1024], F32))
                sy.wait(TS_, mm_done[("v", 15)])
                sy.dma(scos[:], cosf, LS)
                sy.dma(ssin[:], sinf, LS)
                tbl_done = LS.n
                for pi, m in enumerate(order):
                    te.wait(LQ, wqk_done[pi])
                    dst = QY if m < 16 else K4
                    dcol = (m if m < 16 else m - 16) * T
                    for tb in range(4):
                        bank = pqk[(pi * 4 + tb) % 2]
                        prev = (pi * 4 + tb) - 2
                        if prev >= 0:
                            te.wait(VS, ev_done[("qk", prev)])
                        for d in range(NDC):
                            te.mm(bank[:], wqkb[pi % 3][:, d, :],
                                  XT[:, d * T + tb * 512: d * T + tb * 512 + 512],
                                  d == 0, d == NDC - 1,
                                  sem=TS_ if d == NDC - 1 else None)
                        mm_done[("qk", pi, tb)] = TS_.n
                        ve.wait(TS_, mm_done[("qk", pi, tb)])
                        if pi == 0 and tb == 0:
                            ve.wait(LS, tbl_done)
                        ve.tcopy(dst[:, dcol + tb * 512: dcol + tb * 512 + 512],
                                 bank[:], sem=VS)
                        ev_done[("qk", pi * 4 + tb)] = VS.n
                    sy.wait(VS, ev_done[("qk", pi * 4 + 3)])
                    if pi >= 1:
                        sy.wait(VS, rope_done[order[pi - 1]])
                    sy.dma(shb[0:64, :], dst[64:128, dcol:dcol + T], LS)
                    sy.dma(shb[64:128, :], dst[0:64, dcol:dcol + T], LS)
                    ve.wait(LS, LS.n)
                    for hh in range(2):
                        lo, hi = hh * 1024, (hh + 1) * 1024
                        ve.tt(t1b[:], dst[:, dcol + lo: dcol + hi],
                              scos[:, lo:hi], ALU.mult)
                        ve.tt(t2b[:], shb[:, lo:hi], ssin[:, lo:hi], ALU.mult)
                        ve.tt(dst[:, dcol + lo: dcol + hi], t1b[:], t2b[:],
                              ALU.add, sem=VS if hh == 1 else None)
                    rope_done[m] = VS.n
                for si in range(16):
                    gp.wait(LS, ld_done[si])
                    gp.dma(xbf[si * 128:(si + 1) * 128, :], xsl[si % 2][:], LQ)
                    assert LQ.n == cast_done[si]
                gp.dma(wvsb[:], wv, LQ)
                assert LQ.n == wv_done
                for pi, m in enumerate(order):
                    if pi >= 3:
                        gp.wait(TS_, mm_done[("qk", pi - 3, 3)])
                    gp.dma(wqkb[pi % 3][:], wqk[m], LQ)
                    assert LQ.n == wqk_done[pi]
                qk_ev_last = ev_done[("qk", 79)]

            with ExitStack() as s4:
                e4 = s4.enter_context
                PT = e4(nc.sbuf_tensor("PT", [128, 16 * 512], BF16))
                invlb = [e4(nc.sbuf_tensor(f"invl{i}", [128, 512], F32))
                         for i in range(2)]
                lntmp = e4(nc.sbuf_tensor("lntmp", [128, 512], F32))
                ps = [e4(nc.psum_tensor(f"ps{i}", [128, 512], F32))
                      for i in range(4)]
                plb = [e4(nc.psum_tensor(f"pl{i}", [128, 512], F32))
                       for i in range(2)]
                pob = [e4(nc.psum_tensor(f"po{i}", [128, 512], F32))
                       for i in range(2)]
                bank_free = {0: 0, 1: 0, 2: 0, 3: 0}
                oevac_hist = [0, 0]
                first_st = [True]
                for h in range(NH):
                    g = h // 4
                    for jb in range(4):
                        nsc = 4 * (jb + 1)
                        q0 = h * T + jb * 512
                        par = (h * 4 + jb) % 2
                        pl, po = plb[par], pob[par]
                        exp_done, mask_done = [], {}

                        def emit_lav(c):
                            te.wait(SS, exp_done[c])
                            if c in mask_done:
                                te.wait(VS, mask_done[c])
                            if c == 0:
                                te.wait(VS, oevac_hist[par])
                            te.mm(pl[:], c_o128[:], PT[:, c * 512:(c + 1) * 512],
                                  c == 0, c == nsc - 1,
                                  sem=TS_ if c == nsc - 1 else None)
                            te.mm(po[:],
                                  Vtm[:, c * 512 + g * 128: c * 512 + g * 128 + 128],
                                  PT[:, c * 512:(c + 1) * 512],
                                  c == 0, c == nsc - 1,
                                  sem=TS_ if c == nsc - 1 else None)

                        for c in range(nsc):
                            bk = c % 4
                            te.wait(SS, bank_free[bk])
                            if first_st[0]:
                                te.wait(VS, qk_ev_last)
                                first_st[0] = False
                            if c == 0:
                                te.wait(VS, rope_done[h])
                                te.wait(VS, rope_done[16 + g])
                            te.mm(ps[bk][:],
                                  K4[:, g * T + c * 128: g * T + c * 128 + 128],
                                  QY[:, q0:q0 + 512], True, True, sem=TS_)
                            sc.wait(TS_, TS_.n)
                            sc.act(PT[:, c * 512:(c + 1) * 512], ps[bk][:],
                                   AF.Exp, bias=0.0, scale=SCALE, sem=SS)
                            exp_done.append(SS.n)
                            bank_free[bk] = SS.n
                            if c >= 4 * jb:
                                ve.wait(SS, exp_done[c])
                                ve.tt(PT[:, c * 512:(c + 1) * 512],
                                      PT[:, c * 512:(c + 1) * 512],
                                      c_tri[:, c - 4 * jb, :], ALU.mult, sem=VS)
                                mask_done[c] = VS.n
                            if c >= 4:
                                emit_lav(c - 4)
                        for c in range(max(0, nsc - 4), nsc):
                            emit_lav(c)
                        to = TS_.n
                        tl = TS_.n - 1
                        sc.wait(TS_, tl)
                        sc.wait(VS, oevac_hist[par])
                        sc.act(lntmp[:], pl[:], AF.Ln)
                        sc.act(invlb[par][:], lntmp[:], AF.Exp, bias=0.0,
                               scale=-1.0, sem=SS)
                        inv_d = SS.n
                        ve.wait(SS, inv_d)
                        ve.wait(TS_, to)
                        ve.tt(QY[:, q0:q0 + 512], po[:], invlb[par][:], ALU.mult, sem=VS)
                        oevac_hist[par] = VS.n
                for nm, tsr in [("y", QY), ("qy", QY), ("k4", K4), ("vtm", Vtm)]:
                    if nm in taps:
                        sy.wait(VS, VS.n)
                        sy.dma(taps[nm], tsr[:], LS)
                if taps:
                    sy.wait(LS, LS.n)
                run_block(nc, {"sync": sy, "gpsimd": gp, "tensor": te,
                               "vector": ve, "scalar": sc})

        x1st_done = [0] * 16
        xqst_done = [0] * 16
        zadd_v = [0] * 16
        with ExitStack() as s4:
            e4 = s4.enter_context
            afm = e4(nc.sbuf_tensor("afm", [128, NDC * 512], F32))
            Ztm = e4(nc.sbuf_tensor("Ztm", [128, 4 * D], F32))
            wob = [e4(nc.sbuf_tensor(f"wob{i}", [128, NDC, 128], BF16))
                   for i in range(2)]
            xslP = [e4(nc.sbuf_tensor(f"xslP{i}", [128, D], F32)) for i in range(2)]
            x1sl = [e4(nc.sbuf_tensor(f"x1sl{i}", [128, D], F32)) for i in range(2)]
            k8sl = [e4(nc.sbuf_tensor(f"k8sl{i}", [128, D], I8)) for i in range(2)]
            xqsl = [e4(nc.sbuf_tensor(f"xqsl{i}", [128, D], BF16)) for i in range(2)]
            sqscr = e4(nc.sbuf_tensor("sqscr", [128, D], F32))
            stt = e4(nc.sbuf_tensor("stt", [128, 48], F32))
            pw = [e4(nc.psum_tensor(f"pw{i}", [128, 512], F32)) for i in range(2)]
            ptr = [e4(nc.psum_tensor(f"ptr{i}", [128, 128], F32)) for i in range(2)]
            sy, gp, te, ve, sc = EP(), EP(), EP(), EP(), EP()
            lq0 = LQ.n
            wo_done = [lq0 + 16 * (g + 1) for g in range(4 * NDC)]
            wo_mm, afm_ev, tr_done, ztr_ev = [], [], {}, {}
            sq_ss = [0] * 16
            xq_v = [0] * 16
            for tblk in range(4):
                for mc in range(NDC):
                    mcg = tblk * NDC + mc
                    te.wait(LQ, wo_done[mcg])
                    if mcg >= 2:
                        te.wait(SS, afm_ev[mcg - 2])
                    for d in range(NDC):
                        te.mm(pw[mcg % 2][:], wob[mcg % 2][:, d, :],
                              QY[:, d * T + tblk * 512: d * T + tblk * 512 + 512],
                              d == 0, d == NDC - 1,
                              sem=TS_ if d == NDC - 1 else None)
                    wo_mm.append(TS_.n)
                    sc.wait(TS_, wo_mm[mcg])
                    if tblk > 0 and mc == 0:
                        sc.wait(TS_, tr_done[(tblk - 1) * 64 + 63])
                    sc.scopy(afm[:, mc * 512:(mc + 1) * 512], pw[mcg % 2][:], sem=SS)
                    afm_ev.append(SS.n)
                for mc in range(NDC):
                    mcg = tblk * NDC + mc
                    for ts_i in range(4):
                        idx = tblk * 64 + mc * 4 + ts_i
                        te.wait(SS, afm_ev[mcg])
                        if idx >= 2:
                            te.wait(SS, ztr_ev[idx - 2])
                        te.tr(ptr[idx % 2][:],
                              afm[:, mc * 512 + ts_i * 128: mc * 512 + ts_i * 128 + 128],
                              c_id32[:], sem=TS_)
                        tr_done[idx] = TS_.n
                        sc.wait(TS_, tr_done[idx])
                        if idx == tblk * 64 and tblk > 0:
                            sc.wait(VS, xq_v[tblk * 4 - 1])
                        sc.scopy(Ztm[:, ts_i * D + mc * 128: ts_i * D + mc * 128 + 128],
                                 ptr[idx % 2][:], sem=SS)
                        ztr_ev[idx] = SS.n
                mset = [0] * 4
                for ts_i in range(4):
                    gidx = tblk * 4 + ts_i
                    row0 = gidx * 128
                    Z = Ztm[:, ts_i * D:(ts_i + 1) * D]
                    if gidx >= 2:
                        sy.wait(VS, zadd_v[gidx - 2])
                    sy.dma(xslP[ts_i % 2][:], x[row0:row0 + 128, :], LS)
                    xld = LS.n
                    ve.wait(LS, xld)
                    ve.wait(SS, ztr_ev[tblk * 64 + 60 + ts_i])
                    ve.tt(Z, Z, xslP[ts_i % 2][:], ALU.add, sem=VS)
                    zadd_v[gidx] = VS.n
                    ve.rsum(stt[:, ts_i:ts_i + 1], Z)
                    ve.memset(stt[:, 4 + ts_i:5 + ts_i], 0.0, sem=VS)
                    mset[ts_i] = VS.n
                    sc.wait(VS, mset[ts_i])
                    sc.act(sqscr[:], Z, AF.Square, accum=stt[:, 4 + ts_i:5 + ts_i],
                           sem=SS)
                    sq_ss[tblk * 4 + ts_i] = SS.n
                ve.wait(SS, sq_ss[tblk * 4 + 3])
                ve.ts(stt[:, 8:12], stt[:, 0:4], -1.0 / D, None, ALU.mult)
                ve.ts(stt[:, 12:16], stt[:, 4:8], 1.0 / D, EPS, ALU.mult, ALU.add)
                ve.drain()
                ve.tt(stt[:, 16:20], stt[:, 8:12], stt[:, 8:12], ALU.mult)
                ve.drain()
                ve.tt(stt[:, 20:24], stt[:, 12:16], stt[:, 16:20], ALU.subtract)
                ve.drain()
                ve.recip(stt[:, 24:28], stt[:, 20:24], sem=VS)
                rvd = VS.n
                sc.wait(VS, rvd)
                sc.act(stt[:, 28:32], stt[:, 24:28], AF.Sqrt, sem=SS)
                rstd_d = SS.n
                ve.wait(SS, rstd_d)
                ve.tt(stt[:, 32:36], stt[:, 8:12], stt[:, 28:32], ALU.mult)
                ve.drain()
                for ts_i in range(4):
                    Z = Ztm[:, ts_i * D:(ts_i + 1) * D]
                    ve.ts(sqscr[:], Z, 1.0, stt[:, 8 + ts_i:9 + ts_i],
                          ALU.mult, ALU.add)
                    ve.rmax(stt[:, 36 + ts_i:37 + ts_i], sqscr[:], absval=True)
                ve.drain()
                ve.tt(stt[:, 40:44], stt[:, 36:40], stt[:, 28:32], ALU.mult)
                ve.drain()
                ve.ts(stt[:, 40:44], stt[:, 40:44], EPS, None, ALU.max)
                ve.drain()
                ve.recip(stt[:, 36:40], stt[:, 40:44])
                ve.drain()
                ve.ts(stt[:, 44:48], stt[:, 36:40], 127.0, None, ALU.mult)
                ve.ts(stt[:, 36:40], stt[:, 40:44], 1.0 / 127.0, None, ALU.mult)
                ve.drain()
                for ts_i in range(4):
                    gidx = tblk * 4 + ts_i
                    row0 = gidx * 128
                    Z = Ztm[:, ts_i * D:(ts_i + 1) * D]
                    if gidx >= 2:
                        ve.wait(LS, x1st_done[gidx - 2])
                        ve.wait(LS, xqst_done[gidx - 2])
                    ve.ts(x1sl[ts_i % 2][:], Z, stt[:, 28 + ts_i:29 + ts_i],
                          stt[:, 32 + ts_i:33 + ts_i], ALU.mult, ALU.add, sem=VS)
                    x1d = VS.n
                    sy.wait(VS, x1d)
                    sy.dma(x1sp[row0:row0 + 128, :], x1sl[ts_i % 2][:], LS)
                    x1st_done[gidx] = LS.n
                    ve.ts(k8sl[ts_i % 2][:], x1sl[ts_i % 2][:],
                          stt[:, 44 + ts_i:45 + ts_i], None, ALU.mult)
                    ve.ts(xqsl[ts_i % 2][:], k8sl[ts_i % 2][:],
                          stt[:, 36 + ts_i:37 + ts_i], None, ALU.mult, sem=VS)
                    xqd = VS.n
                    xq_v[gidx] = xqd
                    sy.wait(VS, xqd)
                    sy.dma(xqsp[row0:row0 + 128, :], xqsl[ts_i % 2][:], LS)
                    xqst_done[gidx] = LS.n
            gp2 = EP()
            for tblk in range(4):
                for mc in range(NDC):
                    mcg = tblk * NDC + mc
                    if mcg >= 2:
                        gp2.wait(TS_, wo_mm[mcg - 2])
                    gp2.dma(wob[mcg % 2][:], wo[mc], LQ)
                    assert LQ.n == wo_done[mcg]
            gp = gp2
            if "x1" in taps:
                sy.wait(LS, LS.n)
                sy.dma(taps["x1"], x1sp, LS)
            if "xq" in taps:
                sy.wait(LS, LS.n)
                sy.dma(taps["xq"], xqsp, LS)
                sy.wait(LS, LS.n)
            run_block(nc, {"sync": sy, "gpsimd": gp, "tensor": te,
                           "vector": ve, "scalar": sc})
        stQY.close()

        with ExitStack() as s4:
            e4 = s4.enter_context
            XqTb = e4(nc.sbuf_tensor("XqTb", [128, NDC * 512], BF16))
            GN = e4(nc.sbuf_tensor("GN", [128, NHC * 512 + D], BF16))
            w13b = [e4(nc.sbuf_tensor(f"w13b{i}", [128, 2, NDC, 128], BF16))
                    for i in range(2)]
            w2b = [e4(nc.sbuf_tensor(f"w2b{i}", [128, NHC, 128], BF16))
                   for i in range(2)]
            rbuf = e4(nc.sbuf_tensor("rb4", [128, 512], F32))
            r2buf = e4(nc.sbuf_tensor("r2b4", [128, 512], F32))
            gbuf = e4(nc.sbuf_tensor("gb4", [128, 512], F32))
            gsqb = [e4(nc.sbuf_tensor(f"gsq4_{i}", [128, 512], BF16))
                    for i in range(2)]
            gnsqb = e4(nc.sbuf_tensor("gnsq4", [128, 512], F32))
            runmax = e4(nc.sbuf_tensor("rmx4", [128, 512], F32))
            m2col = e4(nc.sbuf_tensor("m2c4", [128, 4], F32))
            rows4 = e4(nc.sbuf_tensor("r4_4", [4, 128], F32))
            rowscr = e4(nc.sbuf_tensor("rsc4", [1, 2048], F32))
            RSb = e4(nc.sbuf_tensor("RSb4", [128, 1024], BF16))
            iS2b = e4(nc.sbuf_tensor("iS2b4", [128, 512], F32))
            k8b = e4(nc.sbuf_tensor("k8b4", [128, 1024], I8))
            z2fm = [e4(nc.sbuf_tensor(f"z2f4_{i}", [128, 512], F32))
                    for i in range(2)]
            Ztm2 = e4(nc.sbuf_tensor("Zt2_4", [128, 4 * D], F32))
            x1r = [e4(nc.sbuf_tensor(f"x1r4_{i}", [128, D], F32)) for i in range(2)]
            outsl = [e4(nc.sbuf_tensor(f"osl4_{i}", [128, D], F32))
                     for i in range(2)]
            st2t = e4(nc.sbuf_tensor("st2_4", [128, 48], F32))
            pg = [e4(nc.psum_tensor(f"pg4_{i}", [128, 512], F32)) for i in range(4)]
            pstat = e4(nc.psum_tensor("pst4", [1, 512], F32))
            ptrx = [e4(nc.psum_tensor(f"ptx4_{i}", [128, 128], F32))
                    for i in range(2)]
            pz = e4(nc.psum_tensor("pz4", [128, 512], F32))
            sy, gp, te, ve, sc = EP(), EP(), EP(), EP(), EP()

            lq0 = LQ.n
            w13_done = [[lq0 + 16 * (tb * (NHC + NDC) + j + 1) for j in range(NHC)]
                        for tb in range(4)]
            w2_done = [[lq0 + 16 * (tb * (NHC + NDC) + NHC + mc + 1)
                        for mc in range(NDC)] for tb in range(4)]
            mm1_done, gate_v, gate_s, gsq_ready = [], [], [], []
            mm2_done, z2fm_ev, trz_ev = [], [], {}
            outst = [0] * 16
            z2add_v = [0] * 16
            prev = {"bc_d": 0, "ssr_d": 0, "trm": 0, "mm2_last": 0, "trz_last": 0}
            def emit_ln2(tblk):
                    ms2 = [0] * 4
                    sq2l = [0] * 4
                    for ts_i in range(4):
                        gidx = tblk * 4 + ts_i
                        row0 = gidx * 128
                        Z = Ztm2[:, ts_i * D:(ts_i + 1) * D]
                        sy.wait(LS, x1st_done[gidx])
                        if gidx >= 2:
                            sy.wait(VS, z2add_v[gidx - 2])
                        sy.dma(x1r[ts_i % 2][:], x1sp[row0:row0 + 128, :], LS)
                        x1rd = LS.n
                        ve.wait(LS, x1rd)
                        if ts_i == 0:
                            ve.wait(VS, trz_ev[tblk * 64 + 63])
                        ve.tt(Z, Z, x1r[ts_i % 2][:], ALU.add, sem=VS)
                        z2add_v[gidx] = VS.n
                        ve.rsum(st2t[:, ts_i:ts_i + 1], Z)
                        ve.memset(st2t[:, 4 + ts_i:5 + ts_i], 0.0, sem=VS)
                        ms2[ts_i] = VS.n
                        sc.wait(VS, ms2[ts_i])
                        sc.act(GN[:, NHC * 512:NHC * 512 + D], Z, AF.Square,
                               accum=st2t[:, 4 + ts_i:5 + ts_i], sem=SS)
                        sq2l[ts_i] = SS.n
                        prev["sq2_last"] = SS.n
                    ve.wait(SS, sq2l[3])
                    ve.ts(st2t[:, 8:12], st2t[:, 0:4], -1.0 / D, None, ALU.mult)
                    ve.ts(st2t[:, 12:16], st2t[:, 4:8], 1.0 / D, EPS, ALU.mult,
                          ALU.add)
                    ve.drain()
                    ve.tt(st2t[:, 16:20], st2t[:, 8:12], st2t[:, 8:12], ALU.mult)
                    ve.drain()
                    ve.tt(st2t[:, 20:24], st2t[:, 12:16], st2t[:, 16:20],
                          ALU.subtract)
                    ve.drain()
                    ve.recip(st2t[:, 24:28], st2t[:, 20:24], sem=VS)
                    rv3 = VS.n
                    sc.wait(VS, rv3)
                    sc.act(st2t[:, 28:32], st2t[:, 24:28], AF.Sqrt, sem=SS)
                    r2d2 = SS.n
                    ve.wait(SS, r2d2)
                    ve.tt(st2t[:, 32:36], st2t[:, 8:12], st2t[:, 28:32], ALU.mult)
                    ve.drain()
                    for ts_i in range(4):
                        gidx = tblk * 4 + ts_i
                        row0 = gidx * 128
                        Z = Ztm2[:, ts_i * D:(ts_i + 1) * D]
                        if gidx >= 2:
                            ve.wait(LS, outst[gidx - 2])
                        ve.ts(outsl[ts_i % 2][:], Z, st2t[:, 28 + ts_i:29 + ts_i],
                              st2t[:, 32 + ts_i:33 + ts_i], ALU.mult, ALU.add,
                              sem=VS)
                        od = VS.n
                        sy.wait(VS, od)
                        sy.dma(out[row0:row0 + 128, :], outsl[ts_i % 2][:], LS)
                        outst[gidx] = LS.n

            xqt_done_arr = [0] * 4
            for tblk in range(4):
                if tblk == 0:
                    sy.wait(LS, xqst_done[3])
                    for c in range(NDC):
                        sy.dmat(XqTb[:, c * 512:(c + 1) * 512],
                                xqsp[0:512, c * 128:(c + 1) * 128], LS)
                    xqt_done_arr[0] = LS.n
                xqt_done = xqt_done_arr[tblk]

                te.wait(VS, prev["bc_d"])
                te.wait(SS, prev["ssr_d"])
                for j in range(NHC):
                    jg = tblk * NHC + j
                    te.wait(LQ, w13_done[tblk][j])
                    if jg >= 2:
                        te.wait(VS, gate_v[jg - 2])
                        te.wait(SS, gate_s[jg - 2])
                    if j == 0:
                        te.wait(LS, xqt_done)
                    for half in range(2):
                        bank = pg[(jg % 2) * 2 + half]
                        for d in range(NDC):
                            te.mm(bank[:], w13b[jg % 2][:, half, d, :],
                                  XqTb[:, d * 512:(d + 1) * 512],
                                  d == 0, d == NDC - 1,
                                  sem=TS_ if d == NDC - 1 else None)
                        mm1_done.append(TS_.n)
                    if j >= 2:
                        te.wait(SS, gsq_ready[jg - 2])
                        te.mm(pstat[:], c_ocb[:], gsqb[(jg - 2) % 2][:],
                              j - 2 == 0, False, sem=None)
                    tx1, tx3 = mm1_done[2 * jg], mm1_done[2 * jg + 1]
                    sc.wait(TS_, tx1)
                    sc.act(rbuf[:], pg[(jg % 2) * 2][:], AF.Relu)
                    sc.act(r2buf[:], rbuf[:], AF.Square, sem=SS)
                    r2d = SS.n
                    ve.wait(SS, r2d)
                    ve.wait(TS_, tx3)
                    if j == 0 and tblk > 0:
                        ve.wait(TS_, prev["mm2_last"])
                        ve.wait(TS_, prev["trm"])
                    ve.tt(gbuf[:], r2buf[:], pg[(jg % 2) * 2 + 1][:], ALU.mult,
                          sem=VS)
                    gbd = VS.n
                    sc.wait(VS, gbd)
                    sc.act(gsqb[jg % 2][:], gbuf[:], AF.Square, sem=SS)
                    gsq_ready.append(SS.n)
                    ve.ts(GN[:, j * 512:(j + 1) * 512], gbuf[:],
                          c_nw[:, j:j + 1], None, ALU.mult, sem=VS)
                    sc.wait(VS, VS.n)
                    sc.act(gnsqb[:], GN[:, j * 512:(j + 1) * 512], AF.Square,
                           sem=SS)
                    ve.wait(SS, SS.n)
                    if j == 0:
                        ve.tcopy(runmax[:], gnsqb[:], sem=VS)
                    else:
                        ve.tt(runmax[:], runmax[:], gnsqb[:], ALU.max, sem=VS)
                    gate_v.append(VS.n)
                    gate_s.append(SS.n)
                te.wait(SS, gsq_ready[tblk * NHC + NHC - 2])
                te.mm(pstat[:], c_ocb[:], gsqb[(tblk * NHC + NHC - 2) % 2][:],
                      False, False, sem=None)
                te.wait(SS, gsq_ready[tblk * NHC + NHC - 1])
                te.mm(pstat[:], c_ocb[:], gsqb[(tblk * NHC + NHC - 1) % 2][:],
                      False, True, sem=TS_)
                tstat = TS_.n
                sc.wait(TS_, tstat)
                sc.scopy(rowscr[:, 0:512], pstat[:], sem=SS)
                ssr_d = SS.n
                prev["ssr_d"] = ssr_d
                ve.wait(SS, ssr_d)
                ve.ts(rowscr[:, 512:1024], rowscr[:, 0:512], 1.0 / HID, EPS,
                      ALU.mult, ALU.add)
                ve.drain()
                ve.recip(rowscr[:, 1024:1536], rowscr[:, 512:1024], sem=VS)
                rv2 = VS.n
                sc.wait(VS, rv2)
                sc.act(rowscr[:, 1536:2048], rowscr[:, 1024:1536], AF.Sqrt, sem=SS)
                rstd2_d = SS.n
                trv = []
                for i in range(4):
                    if i == 0:
                        te.wait(VS, gate_v[tblk * NHC + NHC - 1])
                        if tblk > 0:
                            te.wait(VS, trz_ev[tblk * 64 - 1])
                    if i >= 2:
                        te.wait(VS, trv[i - 2])
                    te.tr(ptrx[i % 2][:], runmax[:, i * 128:(i + 1) * 128],
                          c_id32[:], sem=TS_)
                    ti = TS_.n
                    ve.wait(TS_, ti)
                    ve.rmax(m2col[:, i:i + 1], ptrx[i % 2][:], absval=False,
                            sem=VS)
                    trv.append(VS.n)
                te.wait(VS, trv[3])
                te.tr(ptrx[0][0:4, :], m2col[:], c_id32[:], sem=TS_)
                trm = TS_.n
                prev["trm"] = trm
                ve.wait(TS_, trm)
                ve.tcopy(rows4[:], ptrx[0][0:4, 0:128], sem=VS)
                r4d = VS.n
                sy.wait(VS, r4d)
                sy.dma(rowscr[:, 0:512], rows4[:], LS)
                m2r_d = LS.n
                if tblk < 3:
                    sy.wait(LS, xqst_done[(tblk + 1) * 4 + 3])
                    sy.wait(TS_, mm1_done[2 * (tblk + 1) * NHC - 1])
                    for c in range(NDC):
                        sy.dmat(XqTb[:, c * 512:(c + 1) * 512],
                                xqsp[(tblk + 1) * 512:(tblk + 2) * 512,
                                     c * 128:(c + 1) * 128], LS)
                    xqt_done_arr[tblk + 1] = LS.n
                sc.wait(LS, m2r_d)
                sc.act(rowscr[:, 512:1024], rowscr[:, 0:512], AF.Sqrt, sem=SS)
                sm2_d = SS.n
                ve.wait(SS, sm2_d)
                ve.wait(SS, rstd2_d)
                ve.tt(rowscr[:, 1024:1536], rowscr[:, 512:1024],
                      rowscr[:, 1536:2048], ALU.mult)
                ve.drain()
                ve.ts(rowscr[:, 1024:1536], rowscr[:, 1024:1536], EPS, None,
                      ALU.max)
                ve.drain()
                ve.recip(rowscr[:, 0:512], rowscr[:, 1024:1536])
                ve.drain()
                ve.ts(rowscr[:, 0:512], rowscr[:, 0:512], 127.0, None,
                      ALU.mult)
                ve.drain()
                ve.ts(rowscr[:, 512:1024], rowscr[:, 1024:1536], 1.0 / 127.0,
                      None, ALU.mult)
                ve.drain()
                ve.tt(rowscr[:, 1024:1536], rowscr[:, 1536:2048],
                      rowscr[:, 0:512], ALU.mult, sem=VS)
                rows_d = VS.n
                te.wait(VS, rows_d)
                te.mm(pg[0][:], c_o1f[:], rowscr[:, 1024:1536], True, True,
                      sem=TS_)
                tb1 = TS_.n
                te.mm(pg[1][:], c_o1f[:], rowscr[:, 512:1024], True, True,
                      sem=TS_)
                tb2 = TS_.n
                ve.wait(TS_, tb1)
                ve.tcopy(RSb[:, 0:512], pg[0][:])
                ve.tcopy(RSb[:, 512:1024], pg[0][:])
                ve.wait(TS_, tb2)
                ve.tcopy(iS2b[:], pg[1][:], sem=VS)
                prev["bc_d"] = VS.n
                for jj in range(NHC // 2):
                    sl = slice(jj * 1024, (jj + 1) * 1024)
                    ve.tt(k8b[:], GN[:, sl], RSb[:], ALU.mult)
                    ve.tcopy(GN[:, sl], k8b[:], sem=VS)
                iq_d = VS.n
                if "gn" in taps and tblk == 0:
                    sy.wait(VS, iq_d)
                    sy.dma(taps["gn"], GN[:, 0:NHC * 512], LS)
                for mc in range(NDC):
                    mcg = tblk * NDC + mc
                    te.wait(LQ, w2_done[tblk][mc])
                    if mc == 0:
                        te.wait(VS, iq_d)
                    if mcg >= 1:
                        te.wait(VS, z2fm_ev[mcg - 1])
                    for j in range(NHC):
                        te.mm(pz[:], w2b[mcg % 2][:, j, :],
                              GN[:, j * 512:(j + 1) * 512],
                              j == 0, j == NHC - 1,
                              sem=TS_ if j == NHC - 1 else None)
                    mm2_done.append(TS_.n)
                    ve.wait(TS_, mm2_done[mcg])
                    ve.tt(z2fm[mcg % 2][:], pz[:], iS2b[:], ALU.mult, sem=VS)
                    z2fm_ev.append(VS.n)
                    for ts_i in range(4):
                        idx = tblk * 64 + mc * 4 + ts_i
                        if ts_i == 0:
                            te.wait(VS, z2fm_ev[mcg])
                        if idx >= 2:
                            te.wait(VS, trz_ev[idx - 2])
                        te.tr(ptrx[idx % 2][:],
                              z2fm[mcg % 2][:, ts_i * 128:(ts_i + 1) * 128],
                              c_id32[:], sem=TS_)
                        tdd = TS_.n
                        ve.wait(TS_, tdd)
                        if idx == tblk * 64 and tblk > 0:
                            ve.wait(SS, prev["sq2_last"])
                        ve.tcopy(Ztm2[:, ts_i * D + mc * 128:
                                 ts_i * D + mc * 128 + 128],
                                 ptrx[idx % 2][:], sem=VS)
                        trz_ev[idx] = VS.n
                prev["mm2_last"] = mm2_done[tblk * NDC + NDC - 1]
                emit_ln2(tblk)
            sy.wait(LS, LS.n)
            gp2 = EP()
            for tblk in range(4):
                for j in range(NHC):
                    jg = tblk * NHC + j
                    if jg >= 2:
                        gp2.wait(TS_, mm1_done[2 * (jg - 2) + 1])
                    gp2.dma(w13b[jg % 2][:], w13[j], LQ)
                    assert LQ.n == w13_done[tblk][j]
                for mc in range(NDC):
                    mcg = tblk * NDC + mc
                    if mcg >= 2:
                        gp2.wait(TS_, mm2_done[mcg - 2])
                    gp2.dma(w2b[mcg % 2][:], w2[mc], LQ)
                    assert LQ.n == w2_done[tblk][mc]
            gp = gp2
            run_block(nc, {"sync": sy, "gpsimd": gp, "tensor": te,
                           "vector": ve, "scalar": sc})

    return nc


def _host_layouts(Wq, Wk, Wv, Wo, w13, w2, norm_w):
    f32 = np.float32
    wqk_src = np.concatenate([np.asarray(Wq, f32), np.asarray(Wk, f32)], axis=1)
    wqk = np.ascontiguousarray(
        wqk_src.reshape(NDC, 128, 20, 128).transpose(2, 1, 0, 3))
    wvv = np.ascontiguousarray(
        np.asarray(Wv, f32).reshape(NDC, 128, 512).transpose(1, 0, 2))
    woo = np.ascontiguousarray(
        np.asarray(Wo, f32).reshape(NDC, 128, NDC, 128).transpose(2, 1, 0, 3))
    w13a = np.asarray(w13, f32).T.reshape(NDC, 128, 88, 128).transpose(2, 1, 0, 3)
    w13t = np.ascontiguousarray(
        w13a.reshape(2, NHC, 128, NDC, 128).transpose(1, 2, 0, 3, 4))
    w2t = np.ascontiguousarray(
        np.asarray(w2, f32).T.reshape(NHC, 128, NDC, 128).transpose(2, 1, 0, 3))
    nwt = np.ascontiguousarray(np.asarray(norm_w, f32).reshape(NHC, 128).T)

    i = np.arange(64, dtype=f32)
    e = (f32(-2.0) * i) / f32(128.0)
    inv = np.power(f32(10000.0), e, dtype=f32)
    tt = np.arange(T, dtype=f32)
    ang = (tt[:, None] * inv[None, :]).astype(f32)
    cosa = np.cos(ang.astype(np.float64)).astype(f32)
    sina = np.sin(ang.astype(np.float64)).astype(f32)
    cost = np.tile(cosa, (1, 2))
    sins = np.tile(sina, (1, 2))
    sins[:, :64] *= -1.0
    cosf = np.ascontiguousarray(cost.T)
    sinf = np.ascontiguousarray(sins.T)

    id32 = np.eye(128, dtype=f32)
    ones128 = np.ones((128, 128), ml_dtypes.bfloat16)
    ones1f = np.ones((1, 128), f32)
    onescb = np.ones((128, 1), ml_dtypes.bfloat16)
    q_idx = np.arange(512)
    tri = np.zeros((128, 4, 512), np.float32)
    for k in range(4):
        s_idx = np.arange(128)[:, None] + k * 128
        tri[:, k, :] = (q_idx[None, :] >= s_idx).astype(np.float32)
    tri = tri.astype(ml_dtypes.bfloat16)
    return dict(wqk=wqk, wv=wvv, wo=woo, w13=w13t, w2=w2t, nwt=nwt,
                cosf=cosf, sinf=sinf, id32=id32, ones128=ones128,
                ones1f=ones1f, onescb=onescb, tri=tri)


def kernel(L0_attn_x, L0_attn_Wq, L0_attn_Wk, L0_attn_Wv, L0_attn_Wo,
           L0_ffn_w13, L0_ffn_w2, L0_ffn_norm_w, _debug_taps=(), _nc=None):
    xin = np.ascontiguousarray(np.asarray(L0_attn_x, np.float32))
    consts = _host_layouts(L0_attn_Wq, L0_attn_Wk, L0_attn_Wv, L0_attn_Wo,
                           L0_ffn_w13, L0_ffn_w2, L0_ffn_norm_w)
    nc = _nc if _nc is not None else build(_debug_taps)
    in_maps = []
    for b in range(8):
        m = dict(consts)
        m["x"] = np.ascontiguousarray(xin[b])
        in_maps.append(m)
    res = run_bass_kernel_spmd(nc, in_maps, core_ids=list(range(8)))
    outs = [np.asarray(r["out"], np.float32) for r in res.results]
    if _debug_taps:
        return np.stack(outs, 0), res
    return np.stack(outs, 0)
